# version 04
# baseline (speedup 1.0000x reference)
import numpy as np
import ml_dtypes
from contextlib import ExitStack

import concourse.bass as bass
import concourse.mybir as mybir
from concourse.bass_utils import run_bass_kernel_spmd

F32 = mybir.dt.float32
BF16 = mybir.dt.bfloat16
I32 = mybir.dt.int32
AF = mybir.ActivationFunctionType
ALU = mybir.AluOpType
AXX = mybir.AxisListType.X

MAGIC = 12582912.0
H = 2048
WIN = 4096
MAIN = 2048
INTER = 4096
KC = 4
CHUNK = 256
NCH = 16
FFI = 8192
EPS_LN = 1e-5
EPS_MIX = 1e-5
EPS_RMS = 1e-6
EPS_Q = 1e-5
NSUP = 4


class Bld:
    def __init__(self, nc, ctx):
        self.nc, self.ctx = nc, ctx
        self.fsem = None
        self.fn = 0

    def drain(self, eng, key, instr):
        eng.drain()

    def sem(self, name):
        return self.ctx.enter_context(self.nc.semaphore(name))

    def sb(self, name, shape, dt, st=None):
        return (st or self.ctx).enter_context(self.nc.sbuf_tensor(name, shape, dt))

    def ps(self, name, shape, st=None):
        return (st or self.ctx).enter_context(self.nc.psum_tensor(name, shape, F32))

    def fence(self, tag):
        nc = self.nc
        if self.fsem is None:
            self.fsem = self.sem("fence")
        self.fn += 5
        s, n = self.fsem, self.fn
        with nc.Block("f_" + tag) as blk:
            for setter in (blk.sync, blk.tensor):
                def f(e, s=s, n=n):
                    e.nop().then_inc(s, 1)
                    e.wait_ge(s, n)
                setter(f)

            for setter in (blk.vector, blk.scalar, blk.gpsimd):
                def f(e, s=s, n=n):
                    e.drain().then_inc(s, 1)
                    e.wait_ge(s, n)
                setter(f)


def build_nc(limit=99, sub=9, dump=()):
    nc = bass.Bass("TRN2", target_bir_lowering=False, detect_race_conditions=False)
    _dk = lambda n: dict(kind="ExternalOutput") if n in dump else {}

    xwin = nc.dram_tensor("xwin", [WIN, H], F32, kind="ExternalInput")
    w_conv = nc.dram_tensor("w_conv", [34, 16, 128, 128], F32, kind="ExternalInput")
    w_dt = nc.dram_tensor("w_dt", [16, 128, 64], F32, kind="ExternalInput")
    w_gate = nc.dram_tensor("w_gate", [8, 16, 128, 512], F32, kind="ExternalInput")
    w_out = nc.dram_tensor("w_out", [4, 32, 128, 512], F32, kind="ExternalInput")
    w_gT = nc.dram_tensor("w_gT", [16, 16, 128, 512], F32, kind="ExternalInput")
    w_uT = nc.dram_tensor("w_uT", [16, 16, 128, 512], F32, kind="ExternalInput")
    w_dT = nc.dram_tensor("w_dT", [4, 64, 128, 512], F32, kind="ExternalInput")
    convw = nc.dram_tensor("convw", [34, 128, KC], F32, kind="ExternalInput")
    convb = nc.dram_tensor("convb", [128, 34], F32, kind="ExternalInput")
    dtbias = nc.dram_tensor("dtbias", [64, 1], F32, kind="ExternalInput")
    alog = nc.dram_tensor("alog", [64, 1], F32, kind="ExternalInput")
    d_rep = nc.dram_tensor("d_rep", [1, INTER], F32, kind="ExternalInput")
    normw = nc.dram_tensor("normw", [128, 32], F32, kind="ExternalInput")
    lnww = nc.dram_tensor("lnww", [1, FFI], F32, kind="ExternalInput")
    selbank = nc.dram_tensor("selbank", [64, 64 * 128], F32, kind="ExternalInput")
    identf = nc.dram_tensor("identf", [128, 128], F32, kind="ExternalInput")
    identb = nc.dram_tensor("identb", [128, 128], BF16, kind="ExternalInput")
    onesr = nc.dram_tensor("onesr", [1, 128], F32, kind="ExternalInput")
    mask01 = nc.dram_tensor("mask01", [2, 128, CHUNK], BF16, kind="ExternalInput")
    maskst = nc.dram_tensor("maskst", [1, 1], F32, kind="ExternalInput")

    y_out = nc.dram_tensor("y_out", [MAIN, H], F32, kind="ExternalOutput")

    u1_sc = nc.dram_tensor("u1_sc", [WIN, H], BF16, **_dk("u1_sc"))
    conv_sc = nc.dram_tensor("conv_sc", [34, 128, WIN + 5], BF16, **_dk("conv_sc"))
    gate_sc = nc.dram_tensor("gate_sc", [MAIN, INTER], BF16, **_dk("gate_sc"))
    xs_sc = nc.dram_tensor("xs_sc", [32, 128, WIN], BF16, **_dk("xs_sc"))
    y_sc = nc.dram_tensor("y_sc", [MAIN, INTER], BF16, **_dk("y_sc"))
    xq_sc = nc.dram_tensor("xq_sc", [MAIN, H], BF16, **_dk("xq_sc"))
    hq_sc = nc.dram_tensor("hq_sc", [MAIN, FFI], BF16, **_dk("hq_sc"))
    wq_g = nc.dram_tensor("wq_g", [16, 16, 128, 512], BF16, **_dk("wq_g"))
    wq_u = nc.dram_tensor("wq_u", [16, 16, 128, 512], BF16, **_dk("wq_u"))
    wq_d = nc.dram_tensor("wq_d", [4, 64, 128, 512], BF16, **_dk("wq_d"))
    wout_sc = nc.dram_tensor("wout_sc", [4, 32, 128, 512], BF16, **_dk("wout_sc"))

    with ExitStack() as ctx:
        b = Bld(nc, ctx)

        onesr_sb = b.sb("onesr_sb", [1, 128], F32)
        onescol = b.sb("onescol", [128, 1], F32)
        epsln = b.sb("epsln", [128, 1], F32)
        epsrms = b.sb("epsrms", [128, 1], F32)
        epsq = b.sb("epsq", [128, 1], F32)
        invsw = b.sb("invsw", [128, 3], F32)

        p0 = b.sem("p0")
        with nc.Block("p0") as blk:
            @blk.sync
            def _(sync):
                sync.dma_start(onesr_sb[:, :], onesr[:, :]).then_inc(p0, 16)
                sync.wait_ge(p0, 17)

            @blk.gpsimd
            def _(g):
                g.memset(onescol[:, :], 1.0)
                g.memset(epsln[:, :], EPS_LN)
                g.memset(epsrms[:, :], EPS_RMS)
                g.memset(epsq[:, :], EPS_Q).then_inc(p0, 1)
        b.fence("p0")

        if limit < 2:
            return nc
        with ExitStack() as st:
            xt = [b.sb(f"l1x{i}", [128, H], F32, st) for i in range(2)]
            sq = b.sb("l1sq", [128, H], F32, st)
            u1t = [b.sb(f"l1u{i}", [128, H], BF16, st) for i in range(2)]
            cols = b.sb("l1c", [128, 8], F32, st)
            ld2 = [b.sem(f"l1_ld{i}") for i in range(2)]
            wr2 = [b.sem(f"l1_wr{i}") for i in range(2)]
            vd, sd, ud = (b.sem(f"l1_{n}") for n in "bcd")
            NT = WIN // 128
            with nc.Block("ln1") as blk:
                @blk.sync
                def _(sync):
                    for i in range(NT):
                        if i >= 2:
                            sync.wait_ge(ud, i - 1)
                        sync.dma_start(xt[i % 2][:, :],
                                       xwin[i * 128:(i + 1) * 128, :]).then_inc(ld2[i % 2], 16)
                        if i >= 1:
                            sync.wait_ge(ud, i)
                            sync.dma_start(u1_sc[(i - 1) * 128:i * 128, :],
                                           u1t[(i - 1) % 2][:, :]).then_inc(wr2[(i - 1) % 2], 16)
                    sync.wait_ge(ud, NT)
                    sync.dma_start(u1_sc[(NT - 1) * 128:NT * 128, :],
                                   u1t[(NT - 1) % 2][:, :]).then_inc(wr2[(NT - 1) % 2], 16)
                    sync.wait_ge(wr2[0], 16 * (NT // 2))
                    sync.wait_ge(wr2[1], 16 * (NT // 2))

                @blk.vector
                def _(v):
                    for i in range(NT):
                        v.wait_ge(ld2[i % 2], 16 * (i // 2 + 1))
                        b.drain(v, "v", v.reduce_sum(cols[:, 0:1], xt[i % 2][:, :],
                                                     axis=AXX))
                        v.wait_ge(sd, i + 1)
                        v.tensor_scalar_mul(cols[:, 2:3], cols[:, 0:1], 1.0 / H)
                        b.drain(v, "v", v.tensor_scalar_mul(cols[:, 3:4],
                                                            cols[:, 1:2], 1.0 / H))
                        b.drain(v, "v", v.tensor_mul(cols[:, 4:5], cols[:, 2:3],
                                                     cols[:, 2:3]))
                        v.tensor_sub(cols[:, 4:5], cols[:, 3:4], cols[:, 4:5]).then_inc(vd, 1)

                @blk.scalar
                def _(s):
                    for i in range(NT):
                        s.wait_ge(ld2[i % 2], 16 * (i // 2 + 1))
                        s.activation(sq[:, :], xt[i % 2][:, :], AF.Square,
                                     accum_out=cols[:, 1:2]).then_inc(sd, 1)
                        s.wait_ge(vd, i + 1)
                        if i >= 2:
                            s.wait_ge(wr2[i % 2], 16 * (i // 2))
                        b.drain(s, "s", s.activation(cols[:, 5:6], cols[:, 4:5],
                                                     AF.Ln, bias=epsln[:, :1]))
                        b.drain(s, "s", s.activation(cols[:, 5:6], cols[:, 5:6],
                                                     AF.Exp, scale=-0.5))
                        b.drain(s, "s", s.activation(cols[:, 6:7], cols[:, 2:3],
                                                     AF.Copy, scale=cols[:, 5:6]))
                        b.drain(s, "s", s.mul(cols[:, 6:7], cols[:, 6:7], -1.0))
                        s.activation(u1t[i % 2][:, :], xt[i % 2][:, :], AF.Identity,
                                     scale=cols[:, 5:6], bias=cols[:, 6:7]).then_inc(ud, 1)
        b.fence("ln1")

        if limit < 3:
            return nc
        with ExitStack() as st:
            wt = [b.sb(f"pww{i}", [128, 2048], F32, st) for i in range(2)]
            wb = [b.sb(f"pwb{i}", [128, 2048], BF16, st) for i in range(2)]
            ta = b.sb("pwta", [128, 2048], BF16, st)
            tb2 = b.sb("pwtb", [128, 2048], BF16, st)
            acc = b.sb("pwacc", [128, 64], F32, st)
            scal = b.sb("pwscal", [128, 4], F32, st)
            normw_sb = b.sb("normw_sb", [128, 32], F32, st)
            rowp = b.ps("pwrow", [1, 8], st)
            colp = b.ps("pwcol", [128, 1], st)

            ldn = b.sem("pw_ldn")
            ld2 = [b.sem(f"pw_ld{i}") for i in range(2)]
            wr2 = [b.sem(f"pw_wr{i}") for i in range(2)]
            qd = b.sem("pw_qd")
            NWT = 128
            with nc.Block("pw_out") as blk:
                @blk.sync
                def _(sync):
                    sync.dma_start(normw_sb[:, :], normw[:, :]).then_inc(ldn, 16)
                    for i in range(NWT):
                        ht, yc = divmod(i, 32)
                        if i >= 2:
                            sync.wait_ge(qd, i - 1)
                        sync.dma_start(wt[i % 2][:, :512],
                                       w_out[ht, yc]).then_inc(ld2[i % 2], 16)

                @blk.vector
                def _(v):
                    v.wait_ge(ldn, 16)
                    for i in range(NWT):
                        ht, yc = divmod(i, 32)
                        v.wait_ge(ld2[i % 2], 16 * (i // 2 + 1))
                        if i >= 2:
                            v.wait_ge(wr2[i % 2], 16 * (i // 2))
                        v.tensor_scalar_mul(wb[i % 2][:, :512], wt[i % 2][:, :512],
                                            normw_sb[:, yc:yc + 1]).then_inc(qd, 1)

                @blk.gpsimd
                def _(g):
                    for i in range(NWT):
                        ht, yc = divmod(i, 32)
                        g.wait_ge(qd, i + 1)
                        g.dma_start(wout_sc[ht, yc],
                                    wb[i % 2][:, :512]).then_inc(wr2[i % 2], 16)
                    g.wait_ge(wr2[0], 16 * (NWT // 2))
                    g.wait_ge(wr2[1], 16 * (NWT // 2))
            b.fence("pw_out")

            if limit < 4:
                return nc
            lda2 = [b.sem(f"pwa_l{i}") for i in range(2)]
            aba = b.sem("pwa_a")
            ldb2 = [b.sem(f"pwb_l{i}") for i in range(2)]
            wrb2 = [b.sem(f"pwb_w{i}") for i in range(2)]
            qdb = b.sem("pwb_q")
            for wi, (wsrc, wdst, nblk) in enumerate(
                    ((w_gT, wq_g, 16), (w_uT, wq_u, 16), (w_dT, wq_d, 64))):
                NA = 64
                BL, BA = wi * 4096, wi * 69

                def idx2(i, nblk=nblk):
                    return divmod(i, nblk)
                with nc.Block(f"pw_a{wi}") as blk:
                    @blk.gpsimd
                    def _(g, wsrc=wsrc, idx2=idx2, BL=BL, BA=BA, wi=wi):
                        for i in range(NA):
                            if i >= 2:
                                g.wait_ge(aba, BA + i - 1)
                            gg = wi * 64 + i
                            for j in range(4):
                                a0, c0 = idx2(4 * i + j)
                                g.dma_start(wb[i % 2][:, 512 * j:512 * (j + 1)],
                                            wsrc[a0, c0]).then_inc(lda2[gg % 2], 16)

                    @blk.scalar
                    def _(s, BL=BL, BA=BA, wi=wi):
                        for i in range(NA):
                            gg = wi * 64 + i
                            s.wait_ge(lda2[gg % 2], 64 * (gg // 2 + 1))
                            s.activation(ta[:, :], wb[i % 2][:, :], AF.Abs,
                                         accum_out=acc[:, i:i + 1]).then_inc(aba, 1)

                    @blk.vector
                    def _(v, BA=BA):
                        v.wait_ge(aba, BA + NA)
                        v.reduce_sum(scal[:, 0:1], acc[:, :NA], axis=AXX).then_inc(aba, 1)

                    @blk.tensor
                    def _(t, BA=BA):
                        t.wait_ge(aba, BA + NA + 1)
                        t.matmul(rowp[:1, 0:1], scal[:, 0:1], onescol[:, :1],
                                 start=True, stop=True).then_inc(aba, 1)

                    @blk.scalar
                    def _(s, BA=BA):
                        s.wait_ge(aba, BA + NA + 2)
                        b.drain(s, "s", s.copy(scal[:1, 1:2], rowp[:1, 0:1]))
                        s.mul(scal[:1, 1:2], scal[:1, 1:2],
                              1.0 / (256 * 128 * 512)).then_inc(aba, 1)

                    @blk.tensor
                    def _(t, BA=BA):
                        t.wait_ge(aba, BA + NA + 3)
                        t.matmul(colp[:, 0:1], onesr_sb[:1, :], scal[:1, 1:2],
                                 start=True, stop=True).then_inc(aba, 1)

                    @blk.vector
                    def _(v, wi=wi, BA=BA):
                        v.wait_ge(aba, BA + NA + 4)
                        b.drain(v, "v", v.tensor_copy(invsw[:, wi:wi + 1],
                                                      colp[:, 0:1]))
                        v.tensor_scalar_mul(scal[:, 2:3], invsw[:, wi:wi + 1],
                                            0.5).then_inc(aba, 1)
                b.fence(f"pw_a{wi}")

                with nc.Block(f"pw_b{wi}") as blk:
                    @blk.sync
                    def _(sync, wsrc=wsrc, idx2=idx2, BL=BL, wi=wi):
                        for i in range(NA):
                            if i >= 2:
                                sync.wait_ge(qdb, BL // 64 + i - 1)
                            gg = wi * 64 + i
                            for j in range(4):
                                a0, c0 = idx2(4 * i + j)
                                sync.dma_start(wt[i % 2][:, 512 * j:512 * (j + 1)],
                                               wsrc[a0, c0]).then_inc(ldb2[gg % 2], 16)

                    @blk.vector
                    def _(v, BL=BL, wi=wi, BA=BA):
                        v.wait_ge(aba, BA + NA + 5)
                        for i in range(NA):
                            gg = wi * 64 + i
                            v.wait_ge(ldb2[gg % 2], 64 * (gg // 2 + 1))
                            if i >= 2:
                                v.wait_ge(wrb2[gg % 2], 64 * (gg // 2))
                            v.tensor_scalar(ta[:, :], wt[i % 2][:, :], scal[:, 2:3],
                                            None, ALU.is_gt)
                            v.tensor_scalar(tb2[:, :], wt[i % 2][:, :], -1.0,
                                            scal[:, 2:3], ALU.mult, ALU.is_gt)
                            v.tensor_sub(wb[i % 2][:, :], ta[:, :], tb2[:, :]).then_inc(qdb, 1)

                    @blk.gpsimd
                    def _(g, wdst=wdst, idx2=idx2, BL=BL, wi=wi):
                        for i in range(NA):
                            g.wait_ge(qdb, BL // 64 + i + 1)
                            gg = wi * 64 + i
                            for j in range(4):
                                a0, c0 = idx2(4 * i + j)
                                g.dma_start(wdst[a0, c0],
                                            wb[i % 2][:, 512 * j:512 * (j + 1)]).then_inc(wrb2[gg % 2], 16)
                        g.wait_ge(wrb2[0], 64 * 32 * (wi + 1))
                        g.wait_ge(wrb2[1], 64 * 32 * (wi + 1))
                b.fence(f"pw_b{wi}")

        if limit < 7:
            return nc
        with ExitStack() as mid:
            dt_fm = b.sb("dt_fm", [64, WIN], F32, mid)
            Bfm = b.sb("Bfm", [128, WIN], BF16, mid)
            Cfm = b.sb("Cfm", [128, WIN], BF16, mid)

            with ExitStack() as st:
                u1f = b.sb("u1f", [128, 16 * MAIN], BF16, st)
                wbuf = [b.sb(f"p1w{i}", [128, 2048], BF16, st) for i in range(2)]
                stg = [b.sb(f"p1s{i}", [128, 512], BF16, st) for i in range(2)]
                gwb = b.sb("p1gw", [128, 16 * 512], BF16, st)
                pss = [b.ps(f"p1p{i}", [128, 512], st) for i in range(2)]
                uld = b.sem("p1u")
                wl2 = [b.sem(f"p1wl{i}") for i in range(2)]
                wr2 = [b.sem(f"p1wr{i}") for i in range(2)]
                md, ev = b.sem("p1md"), b.sem("p1ev")
                for half in range(2):
                    BU, BW, BM = half * 256, half * 560, half * 140
                    with nc.Block(f"p1u_{half}") as blk:
                        @blk.sync
                        def _(sync, half=half, BU=BU):
                            for hc in range(16):
                                sync.dma_start_transpose(
                                    u1f[:, hc * MAIN:(hc + 1) * MAIN],
                                    u1_sc[half * MAIN:(half + 1) * MAIN,
                                          hc * 128:(hc + 1) * 128]).then_inc(uld, 16)
                            sync.wait_ge(uld, BU + 256)
                    b.fence(f"p1u_{half}")
                    if limit == 7 and sub < 1 + 2 * half:
                        return nc

                    NCC = 35
                    with nc.Block(f"p1a_{half}") as blk:
                        @blk.gpsimd
                        def _(g, BM=BM, half=half):
                            for cc in range(NCC):
                                gcc = half * NCC + cc
                                if cc >= 2:
                                    g.wait_ge(md, BM + 4 * (cc - 1))
                                if cc < 34:
                                    g.dma_start(
                                        wbuf[cc % 2][:, :].rearrange("p (k m) -> p k m", m=128),
                                        w_conv[cc].rearrange("k p m -> p k m"),
                                    ).then_inc(wl2[gcc % 2], 16)
                                else:
                                    g.dma_start(
                                        wbuf[cc % 2][:, :16 * 64].rearrange("p (k m) -> p k m", m=64),
                                        w_dt.rearrange("k p m -> p k m"),
                                    ).then_inc(wl2[gcc % 2], 16)

                        @blk.tensor
                        def _(t, BW=BW, BM=BM, half=half):
                            for cc in range(NCC):
                                gcc = half * NCC + cc
                                t.wait_ge(wl2[gcc % 2], 16 * (gcc // 2 + 1))
                                w = 128 if cc < 34 else 64
                                for tb in range(4):
                                    i = cc * 4 + tb
                                    if i >= 2:
                                        t.wait_ge(ev, BM + i - 1)
                                    for hc in range(16):
                                        mm = t.matmul(
                                            pss[i % 2][:w, :],
                                            wbuf[cc % 2][:, hc * w:(hc + 1) * w],
                                            u1f[:, hc * MAIN + tb * 512:hc * MAIN + (tb + 1) * 512],
                                            start=(hc == 0), stop=(hc == 15))
                                    mm.then_inc(md, 1)

                        @blk.vector
                        def _(v, half=half, BM=BM):
                            for cc in range(NCC):
                                for tb in range(4):
                                    i = cc * 4 + tb
                                    v.wait_ge(md, BM + i + 1)
                                    if cc < 34:
                                        j = half * 136 + i
                                        if j >= 2:
                                            v.wait_ge(wr2[j % 2], 16 * (j // 2))
                                        v.tensor_copy(stg[j % 2][:, :],
                                                      pss[i % 2][:, :]).then_inc(ev, 1)
                                    else:
                                        v.tensor_copy(
                                            dt_fm[:, half * MAIN + tb * 512:half * MAIN + (tb + 1) * 512],
                                            pss[i % 2][:64, :]).then_inc(ev, 1)

                        @blk.sync
                        def _(sync, half=half, BM=BM):
                            for cc in range(NCC):
                                for tb in range(4):
                                    i = cc * 4 + tb
                                    sync.wait_ge(ev, BM + i + 1)
                                    if cc < 34:
                                        j = half * 136 + i
                                        sync.dma_start(
                                            conv_sc[cc, :, 4 + half * MAIN + tb * 512:
                                                    4 + half * MAIN + (tb + 1) * 512],
                                            stg[j % 2][:, :]).then_inc(wr2[j % 2], 16)
                            sync.wait_ge(wr2[0], 16 * ((half * 136 + 136 + 1) // 2))
                            sync.wait_ge(wr2[1], 16 * ((half * 136 + 136) // 2))
                    b.fence(f"p1a_{half}")
                    if limit == 7 and sub < 2 + 2 * half:
                        return nc

                    if half == 1:
                        gl, gm, ge = (b.sem(f"p1g{n}") for n in "lme")
                        gws = [b.sem(f"p1gw{i}") for i in range(2)]
                        with nc.Block("p1b") as blk:
                            @blk.gpsimd
                            def _(g):
                                for cb in range(8):
                                    if cb >= 1:
                                        g.wait_ge(gm, 16 * cb)
                                    for hc in range(16):
                                        g.dma_start(gwb[:, hc * 512:(hc + 1) * 512],
                                                    w_gate[cb, hc]).then_inc(gl, 16)

                            @blk.tensor
                            def _(t):
                                for cb in range(8):
                                    t.wait_ge(gl, 256 * (cb + 1))
                                    for tb in range(16):
                                        i = cb * 16 + tb
                                        if i >= 2:
                                            t.wait_ge(ge, i - 1)
                                        for hc in range(16):
                                            mm = t.matmul(
                                                pss[i % 2][:, :],
                                                u1f[:, hc * MAIN + tb * 128:hc * MAIN + (tb + 1) * 128],
                                                gwb[:, hc * 512:(hc + 1) * 512],
                                                start=(hc == 0), stop=(hc == 15))
                                        mm.then_inc(gm, 1)

                            @blk.vector
                            def _(v):
                                for i in range(128):
                                    v.wait_ge(gm, i + 1)
                                    if i >= 2:
                                        v.wait_ge(gws[i % 2], 16 * (i // 2))
                                    v.tensor_copy(stg[i % 2][:, :],
                                                  pss[i % 2][:, :]).then_inc(ge, 1)

                            @blk.sync
                            def _(sync):
                                for cb in range(8):
                                    for tb in range(16):
                                        i = cb * 16 + tb
                                        sync.wait_ge(ge, i + 1)
                                        sync.dma_start(
                                            bass.AP(gate_sc, tb * 128 * INTER + cb * 512,
                                                    [[INTER, 128], [1, 512]]),
                                            stg[i % 2][:, :]).then_inc(gws[i % 2], 16)
                                sync.wait_ge(gws[0], 16 * 64)
                                sync.wait_ge(gws[1], 16 * 64)
                        b.fence("p1b")

            if limit < 8:
                return nc
            with ExitStack() as st:
                convw_sb = b.sb("convw_sb", [128, 34 * KC], F32, st)
                convb_sb = b.sb("convb_sb", [128, 34], F32, st)
                cin = [b.sb(f"p2i{i}", [128, 34 * 516], BF16, st) for i in range(2)]
                cacc = [b.sb(f"p2a{i}", [128, 512], BF16, st) for i in range(2)]
                cout = [b.sb(f"p2o{i}", [128, 32 * 512], BF16, st) for i in range(2)]
                zl = b.sb("p2z", [128, 4 * 34], BF16, st)
                ld2 = [b.sem(f"p2ld{i}") for i in range(2)]
                wr2 = [b.sem(f"p2wr{i}") for i in range(2)]
                cv, sl, zz, zzg = (b.sem(f"p2_{n}") for n in "bcef")
                NS = 8
                with nc.Block("p2") as blk:
                    @blk.gpsimd
                    def _(g):
                        b.drain(g, "g", g.memset(zl[:, :], 0.0))
                        g.dma_start(
                            conv_sc.rearrange("c p t -> p c t")[:, :, 0:4],
                            zl[:, :].rearrange("p (c k) -> p c k", k=4),
                        ).then_inc(zzg, 16)
                        g.dma_start(
                            conv_sc.rearrange("c p t -> p c t")[:, :, WIN + 4:WIN + 5],
                            zl[:, :34].rearrange("p (c k) -> p c k", k=1),
                            allow_slow_non_contiguous=True,
                        ).then_inc(zzg, 16)
                        for s_ in range(NS):
                            g.wait_ge(sl, s_ * 34 + 32)
                            g.dma_start(
                                xs_sc.rearrange("c p t -> p c t")[:, :, s_ * 512:(s_ + 1) * 512],
                                cout[s_ % 2][:, :].rearrange("p (c t) -> p c t", t=512),
                            ).then_inc(wr2[s_ % 2], 16)
                        g.wait_ge(wr2[0], 16 * (NS // 2))
                        g.wait_ge(wr2[1], 16 * (NS // 2))
                        g.wait_ge(sl, NS * 34)

                    @blk.sync
                    def _(sync):
                        sync.dma_start(convw_sb[:, :].rearrange("p (c k) -> p c k", k=KC),
                                       convw.rearrange("c p k -> p c k")).then_inc(zz, 16)
                        sync.dma_start(convb_sb[:, :], convb[:, :]).then_inc(zz, 16)
                        sync.wait_ge(zz, 32)
                        sync.wait_ge(zzg, 32)
                        for s_ in range(NS):
                            if s_ >= 2:
                                sync.wait_ge(cv, 34 * (s_ - 1))
                            sync.dma_start(
                                cin[s_ % 2][:, :].rearrange("p (c t) -> p c t", t=516),
                                conv_sc.rearrange("c p t -> p c t")[:, :, s_ * 512 + 1:s_ * 512 + 517],
                            ).then_inc(ld2[s_ % 2], 16)

                    @blk.vector
                    def _(v):
                        for s_ in range(NS):
                            v.wait_ge(ld2[s_ % 2], 16 * (s_ // 2 + 1))
                            for cc in range(34):
                                i = s_ * 34 + cc
                                if i >= 2:
                                    v.wait_ge(sl, i - 1)
                                base = cc * 516
                                src = cin[s_ % 2]
                                v.tensor_scalar_mul(cacc[i % 2][:, :], src[:, base:base + 512],
                                                    convw_sb[:, cc * KC:cc * KC + 1])
                                for k in range(1, KC):
                                    st3 = v.scalar_tensor_tensor(
                                        cacc[i % 2][:, :], src[:, base + k:base + k + 512],
                                        convw_sb[:, cc * KC + k:cc * KC + k + 1],
                                        cacc[i % 2][:, :], ALU.mult, ALU.add)
                                st3.then_inc(cv, 1)

                    @blk.scalar
                    def _(s):
                        for s_ in range(NS):
                            for cc in range(34):
                                i = s_ * 34 + cc
                                s.wait_ge(cv, i + 1)
                                if cc < 32:
                                    if s_ >= 2:
                                        s.wait_ge(wr2[s_ % 2], 16 * (s_ // 2))
                                    dst = cout[s_ % 2][:, cc * 512:(cc + 1) * 512]
                                elif cc == 32:
                                    dst = Bfm[:, s_ * 512:(s_ + 1) * 512]
                                else:
                                    dst = Cfm[:, s_ * 512:(s_ + 1) * 512]
                                s.activation(dst, cacc[i % 2][:, :], AF.Silu,
                                             bias=convb_sb[:, cc:cc + 1]).then_inc(sl, 1)
            b.fence("p2")

            if limit < 9:
                return nc
            _emit_p3(nc, b, dict(
                dt_fm=dt_fm, Bfm=Bfm, Cfm=Cfm, selbank=selbank, identf=identf,
                identb=identb, mask01=mask01, d_rep=d_rep, dtbias=dtbias, alog=alog,
                maskst=maskst, xs_sc=xs_sc, gate_sc=gate_sc, y_sc=y_sc,
                onesr_sb=onesr_sb, onescol=onescol, epsln=epsln))
            b.fence("p3")

        if limit < 10:
            return nc
        _emit_p456(nc, b, dict(
            xwin=xwin, y_sc=y_sc, xq_sc=xq_sc, hq_sc=hq_sc, y_out=y_out,
            wout_sc=wout_sc, wq_g=wq_g, wq_u=wq_u, wq_d=wq_d, lnww=lnww,
            invsw=invsw, epsln=epsln, epsrms=epsrms, epsq=epsq))
        b.fence("end")

    return nc


def _emit_p3(nc, b, d):
    st = ExitStack()
    with st:
        sel_sb = b.sb("sel_sb", [64, 32 * 128], F32, st)
        identf_sb = b.sb("identf_sb", [128, 128], F32, st)
        identb_sb = b.sb("identb_sb", [128, 128], BF16, st)
        mask01_sb = b.sb("mask01_sb", [128, 2 * CHUNK], BF16, st)
        drep_sb = b.sb("drep_sb", [128, INTER], BF16, st)
        dtb_sb = b.sb("dtb_sb", [64, 1], F32, st)
        negA_sb = b.sb("negA_sb", [64, 1], F32, st)
        maskst_sb = b.sb("maskst_sb", [128, 1], F32, st)
        Ssb = b.sb("Ssb", [128, INTER], F32, st)
        Sbf = b.sb("Sbf", [128, INTER], BF16, st)
        dtp_fm = b.sb("dtp_fm", [64, CHUNK], F32, st)
        tmp_fm = b.sb("tmp_fm", [64, CHUNK], F32, st)
        adt_fm = b.sb("adt_fm", [64, CHUNK], F32, st)
        cumA_fm = b.sb("cumA_fm", [64, CHUNK], F32, st)
        dcy_fm = b.sb("dcy_fm", [64, CHUNK], F32, st)
        dpd_fm = b.sb("dpd_fm", [64, CHUNK], F32, st)
        gcol = b.sb("gcol", [64, 1], F32, st)
        grow = b.sb("grow", [1, 64], F32, st)
        gbc = b.sb("gbc", [128, 64], F32, st)
        dtp_tm = [b.sb(f"dtpt{i}", [128, 64], F32, st) for i in range(2)]
        ncum_tm = [b.sb(f"ncumt{i}", [128, 64], F32, st) for i in range(2)]
        ecum_tm = [b.sb(f"ecumt{i}", [128, 64], F32, st) for i in range(2)]
        dpd_tm = [b.sb(f"dpdt{i}", [128, 64], F32, st) for i in range(2)]
        B_tm = [b.sb(f"Btm{i}", [128, 128], BF16, st) for i in range(2)]
        xs_tm = [b.sb(f"xstm{i}", [128, INTER], BF16, st) for i in range(2)]
        xdt_tm = [b.sb(f"xdtm{i}", [128, INTER], BF16, st) for i in range(2)]
        xdd_tm = [b.sb(f"xddm{i}", [128, INTER], BF16, st) for i in range(2)]
        xsD_tm = [b.sb(f"xsDm{i}", [128, INTER], BF16, st) for i in range(2)]
        scm = [b.sb(f"scm{i}", [128, CHUNK], BF16, st) for i in range(2)]
        Et = [[b.sb(f"Et{p}_{s2}", [128, CHUNK], F32, st) for s2 in range(2)]
              for p in range(2)]
        sex = [[b.sb(f"sex{p}_{s2}", [128, CHUNK], BF16, st) for s2 in range(2)]
               for p in range(2)]
        scf = [[b.sb(f"scf{p}_{s2}", [128, CHUNK], BF16, st) for s2 in range(2)]
               for p in range(2)]
        ytmp = [[b.sb(f"ytp{p}_{lt}", [128, 64], F32, st) for lt in range(2)]
                for p in range(2)]
        Y_sb = [b.sb(f"Ysb{lt}", [128, INTER], BF16, st) for lt in range(2)]
        gate_tm = [b.sb(f"gtm{i}", [128, INTER], BF16, st) for i in range(2)]
        ysq = b.sb("ysq", [128, INTER], BF16, st)
        rc3 = b.sb("rc3", [128, 8], F32, st)
        ps_hp = [b.ps(f"pshp{i}", [128, 320], st) for i in range(2)]
        ps_bc = [b.ps(f"psbc{i}", [128, 256], st) for i in range(2)]
        ps_t = b.ps("ps_t", [128, 512], st)
        ps_sc = b.ps("ps_sc", [128, 512], st)

        def hp_st(par):
            return ps_hp[par][:, 0:64]

        def hp_yd(par, lt):
            return ps_hp[par][:, 64 + lt * 64:128 + lt * 64]

        def hp_yo(par, lt):
            return ps_hp[par][:, 192 + lt * 64:256 + lt * 64]

        s_p0 = b.sem("p3_p0")
        s_p0g = b.sem("p3_p0g")
        s_ldx = b.sem("p3_ldx")
        s_ldb = b.sem("p3_ldb")
        s_ldg = b.sem("p3_ldg")
        s_pre = b.sem("p3_pre")
        s_prs = b.sem("p3_prs")
        s_prt = b.sem("p3_prt")
        s_selm = b.sem("p3_selm")
        s_E = b.sem("p3_E")
        s_ex = b.sem("p3_ex")
        s_sf = b.sem("p3_sf")
        s_ym = b.sem("p3_ym")
        s_yo = b.sem("p3_yo")
        s_ev = b.sem("p3_ev")
        s_su = b.sem("p3_su")
        s_sg = b.sem("p3_sg")
        s_r = b.sem("p3_r")
        s_gd = b.sem("p3_gd")
        s_yw = b.sem("p3_yw")
        s_sel2 = b.sem("p3_sel2")

        dt_fm, Bfm, Cfm = d["dt_fm"], d["Bfm"], d["Cfm"]
        xs_sc, gate_sc, y_sc = d["xs_sc"], d["gate_sc"], d["y_sc"]
        onesr_sb, onescol, epsln = d["onesr_sb"], d["onescol"], d["epsln"]

        PL = []
        tt = dict(ldx=0, ldg=0, pre=0, prs=0, prt=0, selm=0, E=0, ex=0, sf=0,
                  ym=0, yo=0, ev=0, su=0, sg=0, r=0, gd=0, yw=0, sel2=0)
        for c in range(NCH):
            main = c >= 8
            p = dict(tt)
            p.update(c=c, main=main)
            tt["ldx"] += 32
            if main:
                tt["ldg"] += 32
            tt["pre"] += 4
            tt["prs"] += 4
            tt["prt"] += 2 + (1 if main else 0)
            if main:
                tt["selm"] += 64
                tt["E"] += 128
                tt["ex"] += 128
                tt["sf"] += 128
                tt["yo"] += 128
                tt["ev"] += 128
                tt["sg"] += 1
                tt["r"] += 1
                tt["gd"] += 2
                tt["yw"] += 32
                tt["sel2"] += 32
            tt["ym"] += 64
            tt["su"] += 64
            for k, v_ in tt.items():
                p[k + "1"] = v_
            PL.append(p)

        W = lambda eng, sem, val: eng.wait_ge(sem, val) if val > 0 else None

        def _v_tail(v, p, hm, main):
            pm = hm % 2
            hsm = slice(hm * 64, (hm + 1) * 64)
            v.wait_ge(s_ym, p["ym"] + hm + 1)
            if main:
                v.wait_ge(s_yo, p["yo"] + 2 * hm + 2)
                for lt in range(2):
                    v.tensor_add(Y_sb[lt][:, hsm], hp_yd(pm, lt),
                                 ytmp[pm][lt][:, :]).then_inc(s_ev, 1)
            v.scalar_tensor_tensor(Ssb[:, hsm], Ssb[:, hsm], gbc[:, hm:hm + 1],
                                   hp_st(pm), ALU.mult, ALU.add).then_inc(s_su, 1)

        with nc.Block("p3") as blk:
            @blk.gpsimd
            def _(g):
                g.memset(Ssb[:, :], 0.0)
                g.memset(Sbf[:, :], 0.0).then_inc(s_p0, 1)
                g.dma_start(drep_sb[:, :],
                            bass.AP(d["d_rep"], 0, [[0, 128], [1, INTER]])).then_inc(s_p0g, 16)
                for p in PL:
                    if not p["main"]:
                        continue
                    mc = p["c"] - 8
                    g.wait_ge(s_gd, p["gd"] + 2)
                    for lt in range(2):
                        g.dma_start(
                            y_sc[mc * 256 + lt * 128:mc * 256 + (lt + 1) * 128, :],
                            Y_sb[lt][:, :]).then_inc(s_yw, 16)
                g.wait_ge(s_yw, PL[-1]["yw1"])

            @blk.sync
            def _(sync):
                for dst, src in (
                    (sel_sb[:, :], d["selbank"][:, 0:32 * 128]),
                    (identf_sb[:, :], d["identf"][:, :]),
                    (identb_sb[:, :], d["identb"][:, :]),
                    (mask01_sb[:, :], d["mask01"].rearrange("s p c -> p s c")),
                    (dtb_sb[:, :], d["dtbias"][:, :]),
                    (negA_sb[:, :], d["alog"][:, :]),
                    (maskst_sb[:, :], bass.AP(d["maskst"], 0, [[0, 128], [1, 1]])),
                ):
                    sync.dma_start(dst, src).then_inc(s_p0, 16)
                for p in PL:
                    c = p["c"]
                    if c >= 1:
                        sync.wait_ge(s_pre, PL[c - 1]["pre1"])
                    for s2 in range(2):
                        sync.dma_start_transpose(
                            xs_tm[s2][:, :],
                            xs_sc.rearrange("c p t -> (c p) t")[
                                :, c * 256 + s2 * 128:c * 256 + (s2 + 1) * 128],
                        ).then_inc(s_ldx, 16)
                    if c >= 1:
                        sync.wait_ge(s_ym, PL[c - 1]["ym1"])
                    for s2 in range(2):
                        sync.dma_start_transpose(
                            B_tm[s2][:, :],
                            Bfm[:, c * 256 + s2 * 128:c * 256 + (s2 + 1) * 128],
                        ).then_inc(s_ldb, 16)
                    if p["main"]:
                        mc = c - 8
                        if c >= 9 and PL[c - 1]["main"]:
                            sync.wait_ge(s_gd, PL[c - 1]["gd"] + 1)
                        for s2 in range(2):
                            sync.dma_start(
                                gate_tm[s2][:, :],
                                gate_sc[mc * 256 + s2 * 128:mc * 256 + (s2 + 1) * 128, :],
                            ).then_inc(s_ldg, 16)
                        sync.wait_ge(s_selm, p["selm"] + 32)
                        sync.dma_start(sel_sb[:, :],
                                       d["selbank"][:, 32 * 128:]).then_inc(s_sel2, 16)
                        sync.wait_ge(s_selm, p["selm"] + 64)
                        sync.dma_start(sel_sb[:, :],
                                       d["selbank"][:, 0:32 * 128]).then_inc(s_sel2, 16)
                sync.wait_ge(s_yw, PL[-1]["yw1"])
                sync.wait_ge(s_sel2, PL[-1]["sel21"])

            @blk.scalar
            def _(s):
                s.wait_ge(s_p0, 112 + 1)
                s.activation(negA_sb[:, :1], negA_sb[:, :1], AF.Exp).then_inc(s_p0, 1)
                for p in PL:
                    c, main = p["c"], p["main"]
                    prev = PL[c - 1] if c >= 1 else None
                    dsl = dt_fm[:, c * 256:(c + 1) * 256]
                    if prev is not None:
                        s.wait_ge(s_pre, prev["pre"] + 2)
                        s.wait_ge(s_prt, prev["prt"] + 1)
                    s.activation(dtp_fm[:, :], dsl, AF.Relu, bias=dtb_sb[:, :1])
                    s.activation(tmp_fm[:, :], dsl, AF.Abs, bias=dtb_sb[:, :1])
                    s.activation(tmp_fm[:, :], tmp_fm[:, :], AF.Exp, scale=-1.0)
                    s.activation(tmp_fm[:, :], tmp_fm[:, :], AF.Ln,
                                 bias=onescol[:64, :1]).then_inc(s_prs, 1)
                    s.wait_ge(s_pre, p["pre"] + 1)
                    s.activation(dcy_fm[:, :], cumA_fm[:, :], AF.Exp, scale=-1.0,
                                 bias=cumA_fm[:, 255:256])
                    s.activation(gcol[:, :], cumA_fm[:, 255:256],
                                 AF.Exp).then_inc(s_prs, 1)
                    s.wait_ge(s_prt, p["prt"] + 1)
                    s.copy(grow[:1, :], ps_t[:1, 448:512]).then_inc(s_prs, 1)
                    s.wait_ge(s_pre, p["pre"] + 3)
                    if main:
                        for s2 in range(2):
                            ec = s.activation(ecum_tm[s2][:, :], ncum_tm[s2][:, :],
                                              AF.Exp, scale=-1.0)
                        ec.then_inc(s_prs, 1)
                    else:
                        s.nop().then_inc(s_prs, 1)
                    if main:
                        for h in range(64):
                            par = h % 2
                            s.wait_ge(s_E, p["E"] + 2 * h + 2)
                            W(s, s_sf, p["sf"] + 2 * (h - 2) + 2)
                            for s2 in range(2):
                                s.activation(sex[par][s2][:, :], Et[par][s2][:, :],
                                             AF.Exp).then_inc(s_ex, 1)
                            if h >= 1:
                                hm = h - 1
                                s.wait_ge(s_ym, p["ym"] + hm + 1)
                                W(s, s_ev, p["ev"] + 2 * (hm - 2) + 2)
                                for lt in range(2):
                                    s.activation(ytmp[hm % 2][lt][:, :],
                                                 hp_yo(hm % 2, lt), AF.Copy,
                                                 scale=ecum_tm[lt][:, hm:hm + 1]).then_inc(s_yo, 1)
                        hm = 63
                        s.wait_ge(s_ym, p["ym"] + 64)
                        W(s, s_ev, p["ev"] + 2 * (hm - 2) + 2)
                        for lt in range(2):
                            s.activation(ytmp[hm % 2][lt][:, :], hp_yo(hm % 2, lt),
                                         AF.Copy,
                                         scale=ecum_tm[lt][:, hm:hm + 1]).then_inc(s_yo, 1)
                        s.wait_ge(s_ldg, p["ldg"] + 32)
                        for s2 in range(2):
                            sg = s.activation(gate_tm[s2][:, :], gate_tm[s2][:, :],
                                              AF.Silu)
                        sg.then_inc(s_sg, 1)
                        s.wait_ge(s_gd, p["gd"] + 1)
                        for lt in range(2):
                            b.drain(s, "s", s.activation(rc3[:, 4 + lt:5 + lt],
                                                         rc3[:, 2 + lt:3 + lt],
                                                         AF.Ln, bias=epsln[:, :1]))
                            ex = s.activation(rc3[:, 4 + lt:5 + lt],
                                              rc3[:, 4 + lt:5 + lt],
                                              AF.Exp, scale=-0.5)
                        ex.then_inc(s_r, 1)

            @blk.tensor
            def _(t):
                t.wait_ge(s_p0, 112 + 2)
                t.wait_ge(s_p0g, 16)
                for p in PL:
                    c, main = p["c"], p["main"]
                    prev = PL[c - 1] if c >= 1 else None
                    if prev is not None:
                        t.wait_ge(s_su, prev["su1"])
                    t.wait_ge(s_pre, p["pre"] + 2)
                    t.wait_ge(s_prs, p["prs"] + 2)
                    for s2 in range(2):
                        t.transpose(ps_t[:, s2 * 64:(s2 + 1) * 64],
                                    dtp_fm[:, s2 * 128:(s2 + 1) * 128], identf_sb[:64, :64])
                        t.transpose(ps_t[:, 128 + s2 * 64:128 + (s2 + 1) * 64],
                                    cumA_fm[:, s2 * 128:(s2 + 1) * 128], identf_sb[:64, :64])
                        t.transpose(ps_t[:, 256 + s2 * 64:256 + (s2 + 1) * 64],
                                    dpd_fm[:, s2 * 128:(s2 + 1) * 128], identf_sb[:64, :64])
                    t.transpose(ps_t[:1, 448:512], gcol[:, :1],
                                identf_sb[:64, :64]).then_inc(s_prt, 1)
                    t.wait_ge(s_prs, p["prs"] + 3)
                    t.matmul(ps_t[:, 384:448], onesr_sb[:1, :], grow[:1, :],
                             start=True, stop=True).then_inc(s_prt, 1)
                    if main:
                        t.wait_ge(s_pre, p["pre"] + 3)
                        for s2 in range(2):
                            mm = t.matmul(ps_sc[:, s2 * 256:(s2 + 1) * 256],
                                          Bfm[:, c * 256 + s2 * 128:c * 256 + (s2 + 1) * 128],
                                          Cfm[:, c * 256:(c + 1) * 256],
                                          start=True, stop=True)
                        mm.then_inc(s_prt, 1)
                    t.wait_ge(s_pre, p["pre1"])
                    t.wait_ge(s_ldb, 32 * (c + 1))

                    def sel_mm(h):
                        par = h % 2
                        W(t, s_E, p["E"] + 2 * (h - 2) + 2)
                        if h == 32:
                            t.wait_ge(s_sel2, p["sel2"] + 16)
                        if h == 0:
                            W(t, s_sel2, p["sel2"])
                        hsel = h % 32
                        t.matmul(ps_bc[par][:, :], sel_sb[:, hsel * 128:(hsel + 1) * 128],
                                 cumA_fm[:, :], start=True, stop=True).then_inc(s_selm, 1)

                    if main:
                        sel_mm(0)
                        sel_mm(1)
                    for h in range(64):
                        par = h % 2
                        hs = slice(h * 64, (h + 1) * 64)
                        if main and h + 2 < 64:
                            sel_mm(h + 2)
                        W(t, s_su, p["su"] + (h - 2) + 1)
                        t.matmul(hp_st(par), B_tm[0][:, :], xdd_tm[0][:, hs],
                                 start=True, stop=False)
                        mm = t.matmul(hp_st(par), B_tm[1][:, :], xdd_tm[1][:, hs],
                                      start=False, stop=True)
                        if main:
                            t.wait_ge(s_sf, p["sf"] + 2 * h + 2)
                            W(t, s_ev, p["ev"] + 2 * (h - 2) + 2)
                            t.matmul(hp_yd(par, 0), scf[par][0][:, 0:128],
                                     xdt_tm[0][:, hs], start=True, stop=False)
                            t.matmul(hp_yd(par, 0), identb_sb[:, :],
                                     xsD_tm[0][:, hs], start=False, stop=True)
                            t.matmul(hp_yd(par, 1), scf[par][0][:, 128:256],
                                     xdt_tm[0][:, hs], start=True, stop=False)
                            t.matmul(hp_yd(par, 1), scf[par][1][:, 128:256],
                                     xdt_tm[1][:, hs], start=False, stop=False)
                            t.matmul(hp_yd(par, 1), identb_sb[:, :],
                                     xsD_tm[1][:, hs], start=False, stop=True)
                            W(t, s_yo, p["yo"] + 2 * (h - 2) + 2)
                            t.matmul(hp_yo(par, 0), Cfm[:, c * 256:c * 256 + 128],
                                     Sbf[:, hs], start=True, stop=True)
                            mm = t.matmul(hp_yo(par, 1),
                                          Cfm[:, c * 256 + 128:c * 256 + 256],
                                          Sbf[:, hs], start=True, stop=True)
                        mm.then_inc(s_ym, 1)

            @blk.vector
            def _(v):
                v.wait_ge(s_p0, 112 + 2)
                v.wait_ge(s_p0g, 16)
                v.tensor_scalar_mul(negA_sb[:64, :1], negA_sb[:64, :1],
                                    -1.0).then_inc(s_p0, 1)
                v.wait_ge(s_p0, 112 + 3)
                for p in PL:
                    c, main = p["c"], p["main"]
                    prev = PL[c - 1] if c >= 1 else None
                    if prev is not None:
                        v.wait_ge(s_ym, prev["ym1"])
                        v.wait_ge(s_yo, prev["yo1"])
                    v.wait_ge(s_prs, p["prs"] + 1)
                    v.tensor_add(dtp_fm[:, :], dtp_fm[:, :], tmp_fm[:, :])
                    v.tensor_scalar_mul(adt_fm[:, :], dtp_fm[:, :], negA_sb[:64, :1])
                    v.tensor_tensor_scan(cumA_fm[:, :], adt_fm[:, :], adt_fm[:, :],
                                         0.0, ALU.add, ALU.bypass).then_inc(s_pre, 1)
                    v.wait_ge(s_prs, p["prs"] + 2)
                    v.tensor_mul(dpd_fm[:, :], dtp_fm[:, :],
                                 dcy_fm[:, :]).then_inc(s_pre, 1)
                    v.wait_ge(s_prt, p["prt"] + 2)
                    for s2 in range(2):
                        v.tensor_copy(dtp_tm[s2][:, :], ps_t[:, s2 * 64:(s2 + 1) * 64])
                        v.tensor_scalar_mul(ncum_tm[s2][:, :],
                                            ps_t[:, 128 + s2 * 64:128 + (s2 + 1) * 64], -1.0)
                        v.tensor_copy(dpd_tm[s2][:, :],
                                      ps_t[:, 256 + s2 * 64:256 + (s2 + 1) * 64])
                    v.tensor_copy(gbc[:, :], ps_t[:, 384:448]).then_inc(s_pre, 1)
                    v.wait_ge(s_pre, p["pre"] + 3)
                    v.wait_ge(s_ldx, 32 * (c + 1))
                    for s2 in range(2):
                        if main:
                            v.tensor_mul(xsD_tm[s2][:, :], xs_tm[s2][:, :], drep_sb[:, :])
                        for h in range(64):
                            hs = slice(h * 64, (h + 1) * 64)
                            if main:
                                v.tensor_scalar_mul(xdt_tm[s2][:, hs], xs_tm[s2][:, hs],
                                                    dtp_tm[s2][:, h:h + 1])
                            v.tensor_scalar_mul(xdd_tm[s2][:, hs], xs_tm[s2][:, hs],
                                                dpd_tm[s2][:, h:h + 1])
                    if main:
                        v.wait_ge(s_prt, p["prt1"])
                        for s2 in range(2):
                            v.tensor_mul(scm[s2][:, :], ps_sc[:, s2 * 256:(s2 + 1) * 256],
                                         mask01_sb[:, s2 * 256:(s2 + 1) * 256])
                    if c == 0:
                        v.wait_ge(s_p0, 112 + 3)
                    v.tensor_copy(Sbf[:, :], Ssb[:, :]).then_inc(s_pre, 1)
                    if main and p["yw"] > 0:
                        v.wait_ge(s_yw, p["yw"])
                    for h in range(64):
                        par = h % 2
                        if main:
                            v.wait_ge(s_selm, p["selm"] + h + 1)
                            W(v, s_ex, p["ex"] + 2 * (h - 2) + 2)
                            for s2 in range(2):
                                v.tensor_scalar(Et[par][s2][:, :], ps_bc[par][:, :],
                                                ncum_tm[s2][:, h:h + 1], 0.0,
                                                ALU.add, ALU.min).then_inc(s_E, 1)
                            v.wait_ge(s_ex, p["ex"] + 2 * h + 2)
                            W(v, s_ym, p["ym"] + (h - 2) + 1)
                            for s2 in range(2):
                                v.tensor_mul(scf[par][s2][:, :], scm[s2][:, :],
                                             sex[par][s2][:, :]).then_inc(s_sf, 1)
                        if h >= 1:
                            _v_tail(v, p, h - 1, main)
                    _v_tail(v, p, 63, main)
                    if c == 7:
                        v.tensor_scalar_mul(Ssb[:, :], Ssb[:, :], maskst_sb[:, :1])
                    if main:
                        v.wait_ge(s_sg, p["sg"] + 1)
                        for lt in range(2):
                            v.tensor_mul(Y_sb[lt][:, :], Y_sb[lt][:, :], gate_tm[lt][:, :])
                        v.tensor_mul(ysq[:, :], Y_sb[0][:, :], Y_sb[0][:, :])
                        v.reduce_sum(rc3[:, 0:1], ysq[:, :], axis=AXX)
                        v.tensor_mul(ysq[:, :], Y_sb[1][:, :], Y_sb[1][:, :])
                        b.drain(v, "v", v.reduce_sum(rc3[:, 1:2], ysq[:, :], axis=AXX))
                        v.tensor_scalar_mul(rc3[:, 2:3], rc3[:, 0:1], 1.0 / INTER)
                        v.tensor_scalar_mul(rc3[:, 3:4], rc3[:, 1:2],
                                            1.0 / INTER).then_inc(s_gd, 1)
                        v.wait_ge(s_r, p["r"] + 1)
                        for lt in range(2):
                            ym = v.tensor_scalar_mul(Y_sb[lt][:, :], Y_sb[lt][:, :],
                                                     rc3[:, 4 + lt:5 + lt])
                        ym.then_inc(s_gd, 1)



def _emit_p456(nc, b, d):
    st = ExitStack()
    with st:
        arena = b.sb("arena", [128, 4 * FFI], BF16, st)
        stream = [b.sb(f"stm{i}", [128, H], F32, st) for i in range(4)]
        xf32 = [b.sb(f"xr{i}", [128, H], F32, st) for i in range(1)]
        wo_b = [b.sb(f"wo{i}", [128, 512], BF16, st) for i in range(2)]
        wq_b = [b.sb(f"wq{i}", [128, 512], BF16, st) for i in range(2)]
        xq_fm = b.sb("xq_fm", [128, 16 * 512], BF16, st)
        qt = b.sb("qt", [128, 4096], F32, st)
        qi = b.sb("qi", [128, 4096], I32, st)
        qb = b.sb("qb", [128, 4096], BF16, st)
        hlnw = b.sb("hlnw", [128, FFI], BF16, st)
        lnw_sb = b.sb("lnw_sb", [128, FFI], BF16, st)
        sgt = [b.sb(f"sgt{i}", [128, 512], F32, st) for i in range(2)]
        ost = [b.sb(f"ost{i}", [128, 512], F32, st) for i in range(2)]
        qc = b.sb("qc", [128, 16], F32, st)
        invsx = b.sb("invsx", [128, 4], F32, st)
        invsh = b.sb("invsh", [128, 4], F32, st)
        comb = b.sb("comb", [128, 12], F32, st)
        pga = [b.ps(f"pg{i}", [128, 512], st) for i in range(4)]
        pua = [b.ps(f"pu{i}", [128, 512], st) for i in range(4)]

        xwin, y_sc, xq_sc, hq_sc, y_out = (d[k] for k in
                                           ("xwin", "y_sc", "xq_sc", "hq_sc", "y_out"))
        wout_sc, wq_g, wq_u, wq_d = (d[k] for k in ("wout_sc", "wq_g", "wq_u", "wq_d"))
        invsw, epsln, epsrms, epsq = (d[k] for k in ("invsw", "epsln", "epsrms", "epsq"))

        s_l0 = b.sem("p456_l0")
        s_yl = b.sem("p4_yl")
        s_wl2 = [b.sem(f"p4_wl{i}") for i in range(2)]
        s_tc = b.sem("p4_tc")
        s_pe = b.sem("p4_pe")
        s_xl = b.sem("p4_xl")
        s_sr = b.sem("p4_sr")
        s_q4 = b.sem("p4_q4")
        s_qs = b.sem("p4_qs")
        s_xw = b.sem("p4_xw")
        s_x5 = b.sem("p5_xl")
        s_w52 = [b.sem(f"p5_wl{i}") for i in range(2)]
        s_t5 = b.sem("p5_tc")
        s_sg5 = b.sem("p5_sg")
        s_h5 = b.sem("p5_h")
        s_hq = b.sem("p5_hq")
        s_hqs = b.sem("p5_hqs")
        s_hw = b.sem("p5_hw")
        s_h6 = b.sem("p6_hl")
        s_w62 = [b.sem(f"p6_wl{i}") for i in range(2)]
        s_t6 = b.sem("p6_tc")
        s_e6 = b.sem("p6_ev")
        s_ow2 = [b.sem(f"p6_ow{i}") for i in range(2)]

        with nc.Block("p456_init") as blk:
            @blk.gpsimd
            def _(g):
                g.dma_start(lnw_sb[:, :],
                            bass.AP(d["lnww"], 0, [[0, 128], [1, FFI]])).then_inc(s_l0, 16)

            @blk.scalar
            def _(s):
                s.wait_ge(s_l0, 16)
        b.fence("p456i")

        for sp in range(NSUP):
            t0 = sp * 512

            with nc.Block(f"p4a_{sp}") as blk:
                @blk.sync
                def _(sync, sp=sp, t0=t0):
                    for yc in range(32):
                        sync.dma_start_transpose(
                            arena[:, yc * 512:(yc + 1) * 512],
                            y_sc[t0:t0 + 512, yc * 128:(yc + 1) * 128]).then_inc(s_yl, 16)
                    for tb in range(4):
                        if tb >= 1:
                            sync.wait_ge(s_sr, sp * 4 + tb)
                        sync.dma_start(
                            xf32[0][:, :],
                            xwin[MAIN + t0 + tb * 128:MAIN + t0 + (tb + 1) * 128, :],
                        ).then_inc(s_xl, 16)

                @blk.gpsimd
                def _(g, sp=sp):
                    for i in range(128):
                        if i >= 2:
                            g.wait_ge(s_tc, sp * 128 + i - 1)
                        ht, yc = divmod(i, 32)
                        g.dma_start(wo_b[i % 2][:, :],
                                    wout_sc[ht, yc]).then_inc(s_wl2[i % 2], 16)

                @blk.tensor
                def _(t, sp=sp):
                    t.wait_ge(s_yl, 16 * 32 * (sp + 1))
                    for ht in range(4):
                        if ht >= 1:
                            t.wait_ge(s_pe, sp * 16 + 4 * ht)
                        for yc in range(32):
                            i = sp * 128 + ht * 32 + yc
                            t.wait_ge(s_wl2[i % 2], 16 * (i // 2 + 1))
                            for tb in range(4):
                                mm = t.matmul(
                                    pga[tb][:, :],
                                    arena[:, yc * 512 + tb * 128:yc * 512 + (tb + 1) * 128],
                                    wo_b[i % 2][:, :],
                                    start=(yc == 0), stop=(yc == 31))
                            mm.then_inc(s_tc, 1)

                @blk.vector
                def _(v, sp=sp):
                    for ht in range(4):
                        v.wait_ge(s_tc, sp * 128 + 32 * (ht + 1))
                        for tb in range(4):
                            v.tensor_copy(stream[tb][:, ht * 512:(ht + 1) * 512],
                                          pga[tb][:, :]).then_inc(s_pe, 1)
                    for tb in range(4):
                        v.wait_ge(s_xl, 16 * (sp * 4 + tb + 1))
                        v.tensor_add(stream[tb][:, :], stream[tb][:, :],
                                     xf32[0][:, :]).then_inc(s_sr, 1)
            b.fence(f"p4a_{sp}")

            with nc.Block(f"p4b_{sp}") as blk:
                @blk.vector
                def _(v, sp=sp):
                    for tb in range(4):
                        j = sp * 4 + tb
                        b.drain(v, "v", v.reduce_sum(qc[:, 0:1], stream[tb][:, :],
                                                     axis=AXX))
                        v.wait_ge(s_qs, 3 * j + 1)
                        v.tensor_scalar_mul(qc[:, 2:3], qc[:, 0:1], 1.0 / H)
                        b.drain(v, "v", v.tensor_scalar_mul(qc[:, 3:4], qc[:, 1:2],
                                                            1.0 / H))
                        b.drain(v, "v", v.tensor_mul(qc[:, 4:5], qc[:, 2:3],
                                                     qc[:, 2:3]))
                        v.tensor_sub(qc[:, 4:5], qc[:, 3:4], qc[:, 4:5]).then_inc(s_q4, 1)
                        v.wait_ge(s_qs, 3 * j + 2)
                        b.drain(v, "v", v.reduce_max(qc[:, 7:8], qt[:, :H], axis=AXX,
                                                     apply_absolute_value=True))
                        b.drain(v, "v", v.tensor_scalar_max(qc[:, 7:8], qc[:, 7:8],
                                                            EPS_Q))
                        b.drain(v, "v", v.reciprocal(qc[:, 8:9], qc[:, 7:8]))
                        b.drain(v, "v", v.tensor_scalar_mul(qc[:, 9:10], qc[:, 8:9],
                                                            127.0))
                        v.tensor_scalar_mul(invsx[:, tb:tb + 1], qc[:, 7:8], 1.0 / 127.0)
                        v.tensor_scalar(qt[:, :H], qt[:, :H], qc[:, 9:10], MAGIC,
                                        ALU.mult, ALU.add)
                        v.tensor_scalar(qt[:, :H], qt[:, :H], -MAGIC, None, ALU.add)
                        v.tensor_copy(qi[:, :H], qt[:, :H]).then_inc(s_q4, 1)
                    for tb in range(4):
                        v.tensor_mul(comb[:, tb:tb + 1], invsx[:, tb:tb + 1], invsw[:, 0:1])
                        v.tensor_mul(comb[:, 4 + tb:5 + tb], invsx[:, tb:tb + 1],
                                     invsw[:, 1:2])

                @blk.scalar
                def _(s, sp=sp):
                    for tb in range(4):
                        j = sp * 4 + tb
                        s.activation(hlnw[:, :H], stream[tb][:, :], AF.Square,
                                     accum_out=qc[:, 1:2]).then_inc(s_qs, 1)
                        s.wait_ge(s_q4, 2 * j + 1)
                        b.drain(s, "s", s.activation(qc[:, 5:6], qc[:, 4:5],
                                                     AF.Ln, bias=epsln[:, :1]))
                        b.drain(s, "s", s.activation(qc[:, 5:6], qc[:, 5:6],
                                                     AF.Exp, scale=-0.5))
                        b.drain(s, "s", s.activation(qc[:, 6:7], qc[:, 2:3],
                                                     AF.Copy, scale=qc[:, 5:6]))
                        b.drain(s, "s", s.mul(qc[:, 6:7], qc[:, 6:7], -1.0))
                        s.activation(qt[:, :H], stream[tb][:, :], AF.Identity,
                                     scale=qc[:, 5:6], bias=qc[:, 6:7]).then_inc(s_qs, 1)
                        s.wait_ge(s_q4, 2 * j + 2)
                        if tb >= 1:
                            s.wait_ge(s_xw, 16 * (j - 1) + 16)
                        s.copy(qb[:, :H], qi[:, :H]).then_inc(s_qs, 1)

                @blk.sync
                def _(sync, sp=sp, t0=t0):
                    for tb in range(4):
                        j = sp * 4 + tb
                        sync.wait_ge(s_qs, 3 * j + 3)
                        sync.dma_start(xq_sc[t0 + tb * 128:t0 + (tb + 1) * 128, :],
                                       qb[:, :H]).then_inc(s_xw, 16)
                    sync.wait_ge(s_xw, 16 * 4 * (sp + 1))
            b.fence(f"p4b_{sp}")

            with nc.Block(f"p5a_{sp}") as blk:
                @blk.sync
                def _(sync, sp=sp, t0=t0):
                    for hc in range(16):
                        sync.dma_start_transpose(
                            xq_fm[:, hc * 512:(hc + 1) * 512],
                            xq_sc[t0:t0 + 512, hc * 128:(hc + 1) * 128]).then_inc(s_x5, 16)

                @blk.gpsimd
                def _(g, sp=sp):
                    for i in range(512):
                        if i >= 2:
                            g.wait_ge(s_t5, sp * 512 + i - 1)
                        it, r = divmod(i, 32)
                        gu, hc = divmod(r, 16)
                        g.dma_start(wq_b[i % 2][:, :],
                                    (wq_g if gu == 0 else wq_u)[it, hc]).then_inc(s_w52[i % 2], 16)

                @blk.tensor
                def _(t, sp=sp):
                    t.wait_ge(s_x5, 16 * 16 * (sp + 1))
                    for it in range(16):
                        for gu in range(2):
                            pp = pga if gu == 0 else pua
                            if it >= 1:
                                if gu == 0:
                                    t.wait_ge(s_sg5, sp * 64 + 4 * it)
                                else:
                                    t.wait_ge(s_h5, sp * 64 + 4 * it)
                            for hc in range(16):
                                i = sp * 512 + it * 32 + gu * 16 + hc
                                t.wait_ge(s_w52[i % 2], 16 * (i // 2 + 1))
                                for tb in range(4):
                                    mm = t.matmul(
                                        pp[tb][:, :],
                                        xq_fm[:, hc * 512 + tb * 128:hc * 512 + (tb + 1) * 128],
                                        wq_b[i % 2][:, :],
                                        start=(hc == 0), stop=(hc == 15))
                                mm.then_inc(s_t5, 1)

                @blk.scalar
                def _(s, sp=sp):
                    for it in range(16):
                        s.wait_ge(s_t5, sp * 512 + it * 32 + 16)
                        if it >= 1:
                            s.wait_ge(s_h5, sp * 64 + 4 * it)
                        for tb in range(4):
                            if tb >= 2:
                                s.wait_ge(s_h5, sp * 64 + 4 * it + tb - 1)
                            s.activation(sgt[tb % 2][:, :], pga[tb][:, :], AF.Silu,
                                         scale=comb[:, tb:tb + 1]).then_inc(s_sg5, 1)

                @blk.vector
                def _(v, sp=sp):
                    for it in range(16):
                        v.wait_ge(s_t5, sp * 512 + it * 32 + 32)
                        for tb in range(4):
                            v.wait_ge(s_sg5, sp * 64 + 4 * it + tb + 1)
                            v.scalar_tensor_tensor(
                                arena[:, tb * FFI + it * 512:tb * FFI + (it + 1) * 512],
                                pua[tb][:, :], comb[:, 4 + tb:5 + tb], sgt[tb % 2][:, :],
                                ALU.mult, ALU.mult).then_inc(s_h5, 1)
            b.fence(f"p5a_{sp}")

            with nc.Block(f"p5b_{sp}") as blk:
                @blk.scalar
                def _(s, sp=sp):
                    for tb in range(4):
                        j = sp * 4 + tb
                        b.drain(s, "s", s.activation(hlnw[:, :],
                                                     arena[:, tb * FFI:(tb + 1) * FFI],
                                                     AF.Square, accum_out=qc[:, 1:2]))
                        b.drain(s, "s", s.activation(qc[:, 13:14], qc[:, 1:2], AF.Ln,
                                                     scale=1.0 / FFI,
                                                     bias=epsrms[:, :1]))
                        s.activation(qc[:, 13:14], qc[:, 13:14], AF.Exp,
                                     scale=-0.5).then_inc(s_hqs, 1)
                        for hf in range(2):
                            s.wait_ge(s_hq, 2 * j + hf + 1)
                            if not (tb == 0 and hf == 0):
                                s.wait_ge(s_hw, 16 * (2 * j + hf - 1) + 16)
                            s.copy(qb[:, :4096], qi[:, :]).then_inc(s_hqs, 1)

                @blk.vector
                def _(v, sp=sp):
                    for tb in range(4):
                        j = sp * 4 + tb
                        v.wait_ge(s_hqs, 3 * j + 1)
                        v.tensor_mul(hlnw[:, :], arena[:, tb * FFI:(tb + 1) * FFI],
                                     lnw_sb[:, :])
                        b.drain(v, "v", v.reduce_max(qc[:, 10:11], hlnw[:, :],
                                                     axis=AXX,
                                                     apply_absolute_value=True))
                        b.drain(v, "v", v.tensor_mul(qc[:, 11:12], qc[:, 10:11],
                                                     qc[:, 13:14]))
                        b.drain(v, "v", v.tensor_scalar_max(qc[:, 11:12],
                                                            qc[:, 11:12], EPS_Q))
                        b.drain(v, "v", v.reciprocal(qc[:, 12:13], qc[:, 11:12]))
                        b.drain(v, "v", v.tensor_scalar_mul(qc[:, 12:13],
                                                            qc[:, 12:13], 127.0))
                        b.drain(v, "v", v.tensor_mul(qc[:, 12:13], qc[:, 12:13],
                                                     qc[:, 13:14]))
                        v.tensor_scalar_mul(invsh[:, tb:tb + 1], qc[:, 11:12], 1.0 / 127.0)
                        for hf in range(2):
                            if not (tb == 0 and hf == 0):
                                v.wait_ge(s_hqs, 3 * j + 1 + hf)
                            v.tensor_scalar(qt[:, :], hlnw[:, hf * 4096:(hf + 1) * 4096],
                                            qc[:, 12:13], MAGIC, ALU.mult, ALU.add)
                            v.tensor_scalar(qt[:, :], qt[:, :], -MAGIC, None, ALU.add)
                            v.tensor_copy(qi[:, :], qt[:, :]).then_inc(s_hq, 1)
                    for tb in range(4):
                        v.tensor_mul(comb[:, 8 + tb:9 + tb], invsh[:, tb:tb + 1],
                                     invsw[:, 2:3])

                @blk.sync
                def _(sync, sp=sp, t0=t0):
                    for tb in range(4):
                        j = sp * 4 + tb
                        for hf in range(2):
                            sync.wait_ge(s_hqs, 3 * j + 2 + hf)
                            sync.dma_start(
                                bass.AP(hq_sc, (t0 + tb * 128) * FFI + hf * 4096,
                                        [[FFI, 128], [1, 4096]]),
                                qb[:, :4096]).then_inc(s_hw, 16)
                    sync.wait_ge(s_hw, 16 * 8 * (sp + 1))
            b.fence(f"p5b_{sp}")

            with nc.Block(f"p6_{sp}") as blk:
                @blk.sync
                def _(sync, sp=sp, t0=t0):
                    for ic in range(64):
                        sync.dma_start_transpose(
                            arena[:, ic * 512:(ic + 1) * 512],
                            hq_sc[t0:t0 + 512, ic * 128:(ic + 1) * 128]).then_inc(s_h6, 16)
                    for k in range(16):
                        ht, tb = divmod(k, 4)
                        o = sp * 16 + k
                        sync.wait_ge(s_e6, sp * 16 + k + 1)
                        sync.dma_start(
                            bass.AP(y_out, (t0 + tb * 128) * H + ht * 512,
                                    [[H, 128], [1, 512]]),
                            ost[o % 2][:, :]).then_inc(s_ow2[o % 2], 16)

                @blk.gpsimd
                def _(g, sp=sp, t0=t0):
                    for i in range(256):
                        if i >= 2:
                            g.wait_ge(s_t6, sp * 256 + i - 1)
                        ht, ic = divmod(i, 64)
                        g.dma_start(wq_b[i % 2][:, :],
                                    wq_d[ht, ic]).then_inc(s_w62[i % 2], 16)

                @blk.tensor
                def _(t, sp=sp):
                    t.wait_ge(s_h6, 16 * 64 * (sp + 1))
                    for ht in range(4):
                        if ht >= 1:
                            t.wait_ge(s_e6, sp * 16 + 4 * ht)
                        for ic in range(64):
                            i = sp * 256 + ht * 64 + ic
                            t.wait_ge(s_w62[i % 2], 16 * (i // 2 + 1))
                            for tb in range(4):
                                mm = t.matmul(
                                    pga[tb][:, :],
                                    arena[:, ic * 512 + tb * 128:ic * 512 + (tb + 1) * 128],
                                    wq_b[i % 2][:, :],
                                    start=(ic == 0), stop=(ic == 63))
                            mm.then_inc(s_t6, 1)

                @blk.vector
                def _(v, sp=sp):
                    for ht in range(4):
                        v.wait_ge(s_t6, sp * 256 + 64 * (ht + 1))
                        for tb in range(4):
                            k = ht * 4 + tb
                            o = sp * 16 + k
                            if o >= 2:
                                v.wait_ge(s_ow2[o % 2], 16 * (o // 2))
                            v.scalar_tensor_tensor(
                                ost[o % 2][:, :], pga[tb][:, :], comb[:, 8 + tb:9 + tb],
                                stream[tb][:, ht * 512:(ht + 1) * 512],
                                ALU.mult, ALU.add).then_inc(s_e6, 1)
            b.fence(f"p6_{sp}")

        with nc.Block("p456_end") as blk:
            @blk.sync
            def _(sync):
                sync.wait_ge(s_ow2[0], 16 * 8 * NSUP)
                sync.wait_ge(s_ow2[1], 16 * 8 * NSUP)


_NC_CACHE = {}


def _get_nc():
    if "nc" not in _NC_CACHE:
        _NC_CACHE["nc"] = build_nc()
    return _NC_CACHE["nc"]


def kernel(L0_attn_x, L0_attn_in_proj_w, L0_attn_conv_w, L0_attn_conv_b,
           L0_attn_dt_bias, L0_attn_A_log, L0_attn_D, L0_attn_norm_w,
           L0_attn_out_proj_w, L0_ffn_w_gate, L0_ffn_w_up, L0_ffn_w_down,
           L0_ffn_ln_weight):
    f32 = np.float32
    bf16 = ml_dtypes.bfloat16
    x = np.asarray(L0_attn_x, dtype=f32)
    W_in = np.asarray(L0_attn_in_proj_w, dtype=f32)

    Wc = W_in[:, INTER:INTER + 4352]
    w_conv = np.ascontiguousarray(
        Wc.reshape(16, 128, 34, 128).transpose(2, 0, 1, 3))
    w_dt = np.ascontiguousarray(W_in[:, 8448:8512].reshape(16, 128, 64))
    w_gate = np.ascontiguousarray(
        W_in[:, :INTER].reshape(16, 128, 8, 512).transpose(2, 0, 1, 3))
    w_out = np.ascontiguousarray(
        np.asarray(L0_attn_out_proj_w, dtype=f32).reshape(32, 128, 4, 512)
        .transpose(2, 0, 1, 3))
    w_gT = np.ascontiguousarray(
        np.asarray(L0_ffn_w_gate, dtype=f32).T.reshape(16, 128, 16, 512)
        .transpose(2, 0, 1, 3))
    w_uT = np.ascontiguousarray(
        np.asarray(L0_ffn_w_up, dtype=f32).T.reshape(16, 128, 16, 512)
        .transpose(2, 0, 1, 3))
    w_dT = np.ascontiguousarray(
        np.asarray(L0_ffn_w_down, dtype=f32).T.reshape(64, 128, 4, 512)
        .transpose(2, 0, 1, 3))
    convw = np.ascontiguousarray(np.asarray(L0_attn_conv_w, dtype=f32)
                                 .reshape(34, 128, KC))
    convb = np.ascontiguousarray(np.asarray(L0_attn_conv_b, dtype=f32)
                                 .reshape(34, 128).T)
    dtbias = np.ascontiguousarray(np.asarray(L0_attn_dt_bias, dtype=f32)
                                  .reshape(64, 1))
    alog = np.ascontiguousarray(np.asarray(L0_attn_A_log, dtype=f32).reshape(64, 1))
    d_rep = np.ascontiguousarray(
        np.repeat(np.asarray(L0_attn_D, dtype=f32), 64).reshape(1, INTER))
    normw = np.ascontiguousarray(np.asarray(L0_attn_norm_w, dtype=f32)
                                 .reshape(32, 128).T)
    lnww = np.ascontiguousarray(np.asarray(L0_ffn_ln_weight, dtype=f32)
                                .reshape(1, FFI))
    selb = np.zeros((64, 64 * 128), f32)
    for h in range(64):
        selb[h, h * 128:(h + 1) * 128] = 1.0
    identf_a = np.eye(128, dtype=f32)
    identb_a = np.eye(128, dtype=f32).astype(bf16)
    onesr_a = np.ones((1, 128), f32)
    m01 = np.zeros((2, 128, CHUNK), f32)
    for s2 in range(2):
        for p in range(128):
            m01[s2, p, s2 * 128 + p:] = 1.0
    m01 = m01.astype(bf16)

    shared = dict(w_conv=w_conv, w_dt=w_dt, w_gate=w_gate, w_out=w_out,
                  w_gT=w_gT, w_uT=w_uT, w_dT=w_dT, convw=convw, convb=convb,
                  dtbias=dtbias, alog=alog, d_rep=d_rep, normw=normw, lnww=lnww,
                  selbank=selb, identf=identf_a, identb=identb_a, onesr=onesr_a,
                  mask01=m01)

    in_maps = []
    for core in range(8):
        bb, hh = divmod(core, 2)
        xw = np.zeros((WIN, H), f32)
        xw[MAIN:] = x[bb, hh * MAIN:(hh + 1) * MAIN]
        if hh == 1:
            xw[:MAIN] = x[bb, :MAIN]
        m = dict(shared)
        m["xwin"] = xw
        m["maskst"] = np.array([[1.0 if hh == 1 else 0.0]], f32)
        in_maps.append(m)

    nc = _get_nc()
    res = run_bass_kernel_spmd(nc, in_maps, core_ids=list(range(8)))
    out = np.empty((4, 4096, H), np.float32)
    for core in range(8):
        bb, hh = divmod(core, 2)
        out[bb, hh * MAIN:(hh + 1) * MAIN] = res.results[core]["y_out"]
    return out


# version 05
# speedup vs baseline: 1.3470x; 1.3470x over previous
import numpy as np
import ml_dtypes
from contextlib import ExitStack

import concourse.bass as bass
import concourse.mybir as mybir
from concourse.bass_utils import run_bass_kernel_spmd

F32 = mybir.dt.float32
BF16 = mybir.dt.bfloat16
I32 = mybir.dt.int32
AF = mybir.ActivationFunctionType
ALU = mybir.AluOpType
AXX = mybir.AxisListType.X

MAGIC = 12582912.0
H = 2048
WIN = 4096
MAIN = 2048
INTER = 4096
KC = 4
CHUNK = 256
NCH = 16
FFI = 8192
EPS_LN = 1e-5
EPS_MIX = 1e-5
EPS_RMS = 1e-6
EPS_Q = 1e-5
NSUP = 4


class Bld:
    def __init__(self, nc, ctx):
        self.nc, self.ctx = nc, ctx
        self.fsem = None
        self.fn = 0

    def drain(self, eng, key, instr):
        eng.drain()

    def sem(self, name):
        return self.ctx.enter_context(self.nc.semaphore(name))

    def sb(self, name, shape, dt, st=None):
        return (st or self.ctx).enter_context(self.nc.sbuf_tensor(name, shape, dt))

    def ps(self, name, shape, st=None):
        return (st or self.ctx).enter_context(self.nc.psum_tensor(name, shape, F32))

    def fence(self, tag):
        nc = self.nc
        if self.fsem is None:
            self.fsem = self.sem("fence")
        self.fn += 5
        s, n = self.fsem, self.fn
        with nc.Block("f_" + tag) as blk:
            for setter in (blk.sync, blk.tensor):
                def f(e, s=s, n=n):
                    e.nop().then_inc(s, 1)
                    e.wait_ge(s, n)
                setter(f)

            for setter in (blk.vector, blk.scalar, blk.gpsimd):
                def f(e, s=s, n=n):
                    e.drain().then_inc(s, 1)
                    e.wait_ge(s, n)
                setter(f)


def build_nc(limit=99, sub=9, dump=()):
    nc = bass.Bass("TRN2", target_bir_lowering=False, detect_race_conditions=False)
    _dk = lambda n: dict(kind="ExternalOutput") if n in dump else {}

    xwin = nc.dram_tensor("xwin", [WIN, H], F32, kind="ExternalInput")
    w_conv = nc.dram_tensor("w_conv", [34, 16, 128, 128], F32, kind="ExternalInput")
    w_dt = nc.dram_tensor("w_dt", [16, 128, 64], F32, kind="ExternalInput")
    w_gate = nc.dram_tensor("w_gate", [8, 16, 128, 512], F32, kind="ExternalInput")
    w_out = nc.dram_tensor("w_out", [4, 32, 128, 512], F32, kind="ExternalInput")
    w_gT = nc.dram_tensor("w_gT", [16, 16, 128, 512], F32, kind="ExternalInput")
    w_uT = nc.dram_tensor("w_uT", [16, 16, 128, 512], F32, kind="ExternalInput")
    w_dT = nc.dram_tensor("w_dT", [4, 64, 128, 512], F32, kind="ExternalInput")
    convw = nc.dram_tensor("convw", [34, 128, KC], F32, kind="ExternalInput")
    convb = nc.dram_tensor("convb", [128, 34], F32, kind="ExternalInput")
    dtbias = nc.dram_tensor("dtbias", [64, 1], F32, kind="ExternalInput")
    alog = nc.dram_tensor("alog", [64, 1], F32, kind="ExternalInput")
    d_rep = nc.dram_tensor("d_rep", [1, INTER], F32, kind="ExternalInput")
    normw = nc.dram_tensor("normw", [128, 32], F32, kind="ExternalInput")
    lnww = nc.dram_tensor("lnww", [1, FFI], F32, kind="ExternalInput")
    selbank = nc.dram_tensor("selbank", [64, 64 * 128], F32, kind="ExternalInput")
    identf = nc.dram_tensor("identf", [128, 128], F32, kind="ExternalInput")
    identb = nc.dram_tensor("identb", [128, 128], BF16, kind="ExternalInput")
    onesr = nc.dram_tensor("onesr", [1, 128], F32, kind="ExternalInput")
    mask01 = nc.dram_tensor("mask01", [2, 128, CHUNK], BF16, kind="ExternalInput")
    maskst = nc.dram_tensor("maskst", [1, 1], F32, kind="ExternalInput")

    y_out = nc.dram_tensor("y_out", [MAIN, H], F32, kind="ExternalOutput")

    u1_sc = nc.dram_tensor("u1_sc", [WIN, H], BF16, **_dk("u1_sc"))
    conv_sc = nc.dram_tensor("conv_sc", [34, 128, WIN + 5], BF16, **_dk("conv_sc"))
    gate_sc = nc.dram_tensor("gate_sc", [MAIN, INTER], BF16, **_dk("gate_sc"))
    xs_sc = nc.dram_tensor("xs_sc", [32, 128, WIN], BF16, **_dk("xs_sc"))
    y_sc = nc.dram_tensor("y_sc", [MAIN, INTER], BF16, **_dk("y_sc"))
    xq_sc = nc.dram_tensor("xq_sc", [MAIN, H], BF16, **_dk("xq_sc"))
    hq_sc = nc.dram_tensor("hq_sc", [MAIN, FFI], BF16, **_dk("hq_sc"))
    wq_g = nc.dram_tensor("wq_g", [16, 16, 128, 512], BF16, **_dk("wq_g"))
    wq_u = nc.dram_tensor("wq_u", [16, 16, 128, 512], BF16, **_dk("wq_u"))
    wq_d = nc.dram_tensor("wq_d", [4, 64, 128, 512], BF16, **_dk("wq_d"))
    wout_sc = nc.dram_tensor("wout_sc", [4, 32, 128, 512], BF16, **_dk("wout_sc"))

    with ExitStack() as ctx:
        b = Bld(nc, ctx)

        onesr_sb = b.sb("onesr_sb", [1, 128], F32)
        onescol = b.sb("onescol", [128, 1], F32)
        epsln = b.sb("epsln", [128, 1], F32)
        epsrms = b.sb("epsrms", [128, 1], F32)
        epsq = b.sb("epsq", [128, 1], F32)
        invsw = b.sb("invsw", [128, 3], F32)

        p0 = b.sem("p0")
        with nc.Block("p0") as blk:
            @blk.sync
            def _(sync):
                sync.dma_start(onesr_sb[:, :], onesr[:, :]).then_inc(p0, 16)
                sync.wait_ge(p0, 17)

            @blk.gpsimd
            def _(g):
                g.memset(onescol[:, :], 1.0)
                g.memset(epsln[:, :], EPS_LN)
                g.memset(epsrms[:, :], EPS_RMS)
                g.memset(epsq[:, :], EPS_Q).then_inc(p0, 1)
        b.fence("p0")

        if limit < 2:
            return nc
        with ExitStack() as st:
            xt = [b.sb(f"l1x{i}", [128, H], F32, st) for i in range(2)]
            sq = b.sb("l1sq", [128, H], F32, st)
            u1t = [b.sb(f"l1u{i}", [128, H], BF16, st) for i in range(2)]
            cols = b.sb("l1c", [128, 8], F32, st)
            ld2 = [b.sem(f"l1_ld{i}") for i in range(2)]
            wr2 = [b.sem(f"l1_wr{i}") for i in range(2)]
            vd, sd, ud = (b.sem(f"l1_{n}") for n in "bcd")
            NT = WIN // 128
            with nc.Block("ln1") as blk:
                @blk.sync
                def _(sync):
                    for i in range(NT):
                        if i >= 2:
                            sync.wait_ge(ud, i - 1)
                        sync.dma_start(xt[i % 2][:, :],
                                       xwin[i * 128:(i + 1) * 128, :]).then_inc(ld2[i % 2], 16)
                        if i >= 1:
                            sync.wait_ge(ud, i)
                            sync.dma_start(u1_sc[(i - 1) * 128:i * 128, :],
                                           u1t[(i - 1) % 2][:, :]).then_inc(wr2[(i - 1) % 2], 16)
                    sync.wait_ge(ud, NT)
                    sync.dma_start(u1_sc[(NT - 1) * 128:NT * 128, :],
                                   u1t[(NT - 1) % 2][:, :]).then_inc(wr2[(NT - 1) % 2], 16)
                    sync.wait_ge(wr2[0], 16 * (NT // 2))
                    sync.wait_ge(wr2[1], 16 * (NT // 2))

                @blk.vector
                def _(v):
                    for i in range(NT):
                        v.wait_ge(ld2[i % 2], 16 * (i // 2 + 1))
                        b.drain(v, "v", v.reduce_sum(cols[:, 0:1], xt[i % 2][:, :],
                                                     axis=AXX))
                        v.wait_ge(sd, i + 1)
                        v.tensor_scalar_mul(cols[:, 2:3], cols[:, 0:1], 1.0 / H)
                        b.drain(v, "v", v.tensor_scalar_mul(cols[:, 3:4],
                                                            cols[:, 1:2], 1.0 / H))
                        b.drain(v, "v", v.tensor_mul(cols[:, 4:5], cols[:, 2:3],
                                                     cols[:, 2:3]))
                        v.tensor_sub(cols[:, 4:5], cols[:, 3:4], cols[:, 4:5]).then_inc(vd, 1)

                @blk.scalar
                def _(s):
                    for i in range(NT):
                        s.wait_ge(ld2[i % 2], 16 * (i // 2 + 1))
                        s.activation(sq[:, :], xt[i % 2][:, :], AF.Square,
                                     accum_out=cols[:, 1:2]).then_inc(sd, 1)
                        s.wait_ge(vd, i + 1)
                        if i >= 2:
                            s.wait_ge(wr2[i % 2], 16 * (i // 2))
                        b.drain(s, "s", s.activation(cols[:, 5:6], cols[:, 4:5],
                                                     AF.Ln, bias=epsln[:, :1]))
                        b.drain(s, "s", s.activation(cols[:, 5:6], cols[:, 5:6],
                                                     AF.Exp, scale=-0.5))
                        b.drain(s, "s", s.activation(cols[:, 6:7], cols[:, 2:3],
                                                     AF.Copy, scale=cols[:, 5:6]))
                        b.drain(s, "s", s.mul(cols[:, 6:7], cols[:, 6:7], -1.0))
                        s.activation(u1t[i % 2][:, :], xt[i % 2][:, :], AF.Identity,
                                     scale=cols[:, 5:6], bias=cols[:, 6:7]).then_inc(ud, 1)
        b.fence("ln1")

        if limit < 3:
            return nc
        with ExitStack() as st:
            wt = [b.sb(f"pww{i}", [128, 2048], F32, st) for i in range(2)]
            wb = [b.sb(f"pwb{i}", [128, 2048], BF16, st) for i in range(2)]
            ta = b.sb("pwta", [128, 2048], BF16, st)
            tb2 = b.sb("pwtb", [128, 2048], BF16, st)
            acc = b.sb("pwacc", [128, 64], F32, st)
            scal = b.sb("pwscal", [128, 4], F32, st)
            normw_sb = b.sb("normw_sb", [128, 32], F32, st)
            rowp = b.ps("pwrow", [1, 8], st)
            colp = b.ps("pwcol", [128, 1], st)

            ldn = b.sem("pw_ldn")
            ld2 = [b.sem(f"pw_ld{i}") for i in range(2)]
            wr2 = [b.sem(f"pw_wr{i}") for i in range(2)]
            qd = b.sem("pw_qd")
            NWT = 128
            with nc.Block("pw_out") as blk:
                @blk.sync
                def _(sync):
                    sync.dma_start(normw_sb[:, :], normw[:, :]).then_inc(ldn, 16)
                    for i in range(NWT):
                        ht, yc = divmod(i, 32)
                        if i >= 2:
                            sync.wait_ge(qd, i - 1)
                        sync.dma_start(wt[i % 2][:, :512],
                                       w_out[ht, yc]).then_inc(ld2[i % 2], 16)

                @blk.vector
                def _(v):
                    v.wait_ge(ldn, 16)
                    for i in range(NWT):
                        ht, yc = divmod(i, 32)
                        v.wait_ge(ld2[i % 2], 16 * (i // 2 + 1))
                        if i >= 2:
                            v.wait_ge(wr2[i % 2], 16 * (i // 2))
                        v.tensor_scalar_mul(wb[i % 2][:, :512], wt[i % 2][:, :512],
                                            normw_sb[:, yc:yc + 1]).then_inc(qd, 1)

                @blk.gpsimd
                def _(g):
                    for i in range(NWT):
                        ht, yc = divmod(i, 32)
                        g.wait_ge(qd, i + 1)
                        g.dma_start(wout_sc[ht, yc],
                                    wb[i % 2][:, :512]).then_inc(wr2[i % 2], 16)
                    g.wait_ge(wr2[0], 16 * (NWT // 2))
                    g.wait_ge(wr2[1], 16 * (NWT // 2))
            b.fence("pw_out")

            if limit < 4:
                return nc
            lda2 = [b.sem(f"pwa_l{i}") for i in range(2)]
            aba = b.sem("pwa_a")
            ldb2 = [b.sem(f"pwb_l{i}") for i in range(2)]
            wrb2 = [b.sem(f"pwb_w{i}") for i in range(2)]
            qdb = b.sem("pwb_q")
            for wi, (wsrc, wdst, nblk) in enumerate(
                    ((w_gT, wq_g, 16), (w_uT, wq_u, 16), (w_dT, wq_d, 64))):
                NA = 64
                BL, BA = wi * 4096, wi * 69

                def idx2(i, nblk=nblk):
                    return divmod(i, nblk)
                with nc.Block(f"pw_a{wi}") as blk:
                    @blk.gpsimd
                    def _(g, wsrc=wsrc, idx2=idx2, BL=BL, BA=BA, wi=wi):
                        for i in range(NA):
                            if i >= 2:
                                g.wait_ge(aba, BA + i - 1)
                            gg = wi * 64 + i
                            for j in range(4):
                                a0, c0 = idx2(4 * i + j)
                                g.dma_start(wb[i % 2][:, 512 * j:512 * (j + 1)],
                                            wsrc[a0, c0]).then_inc(lda2[gg % 2], 16)

                    @blk.scalar
                    def _(s, BL=BL, BA=BA, wi=wi):
                        for i in range(NA):
                            gg = wi * 64 + i
                            s.wait_ge(lda2[gg % 2], 64 * (gg // 2 + 1))
                            s.activation(ta[:, :], wb[i % 2][:, :], AF.Abs,
                                         accum_out=acc[:, i:i + 1]).then_inc(aba, 1)

                    @blk.vector
                    def _(v, BA=BA):
                        v.wait_ge(aba, BA + NA)
                        v.reduce_sum(scal[:, 0:1], acc[:, :NA], axis=AXX).then_inc(aba, 1)

                    @blk.tensor
                    def _(t, BA=BA):
                        t.wait_ge(aba, BA + NA + 1)
                        t.matmul(rowp[:1, 0:1], scal[:, 0:1], onescol[:, :1],
                                 start=True, stop=True).then_inc(aba, 1)

                    @blk.scalar
                    def _(s, BA=BA):
                        s.wait_ge(aba, BA + NA + 2)
                        b.drain(s, "s", s.copy(scal[:1, 1:2], rowp[:1, 0:1]))
                        s.mul(scal[:1, 1:2], scal[:1, 1:2],
                              1.0 / (256 * 128 * 512)).then_inc(aba, 1)

                    @blk.tensor
                    def _(t, BA=BA):
                        t.wait_ge(aba, BA + NA + 3)
                        t.matmul(colp[:, 0:1], onesr_sb[:1, :], scal[:1, 1:2],
                                 start=True, stop=True).then_inc(aba, 1)

                    @blk.vector
                    def _(v, wi=wi, BA=BA):
                        v.wait_ge(aba, BA + NA + 4)
                        b.drain(v, "v", v.tensor_copy(invsw[:, wi:wi + 1],
                                                      colp[:, 0:1]))
                        v.tensor_scalar_mul(scal[:, 2:3], invsw[:, wi:wi + 1],
                                            0.5).then_inc(aba, 1)
                b.fence(f"pw_a{wi}")

                with nc.Block(f"pw_b{wi}") as blk:
                    @blk.sync
                    def _(sync, wsrc=wsrc, idx2=idx2, BL=BL, wi=wi):
                        for i in range(NA):
                            if i >= 2:
                                sync.wait_ge(qdb, BL // 64 + i - 1)
                            gg = wi * 64 + i
                            for j in range(4):
                                a0, c0 = idx2(4 * i + j)
                                sync.dma_start(wt[i % 2][:, 512 * j:512 * (j + 1)],
                                               wsrc[a0, c0]).then_inc(ldb2[gg % 2], 16)

                    @blk.vector
                    def _(v, BL=BL, wi=wi, BA=BA):
                        v.wait_ge(aba, BA + NA + 5)
                        for i in range(NA):
                            gg = wi * 64 + i
                            v.wait_ge(ldb2[gg % 2], 64 * (gg // 2 + 1))
                            if i >= 2:
                                v.wait_ge(wrb2[gg % 2], 64 * (gg // 2))
                            v.tensor_scalar(ta[:, :], wt[i % 2][:, :], scal[:, 2:3],
                                            None, ALU.is_gt)
                            v.tensor_scalar(tb2[:, :], wt[i % 2][:, :], -1.0,
                                            scal[:, 2:3], ALU.mult, ALU.is_gt)
                            v.tensor_sub(wb[i % 2][:, :], ta[:, :], tb2[:, :]).then_inc(qdb, 1)

                    @blk.gpsimd
                    def _(g, wdst=wdst, idx2=idx2, BL=BL, wi=wi):
                        for i in range(NA):
                            g.wait_ge(qdb, BL // 64 + i + 1)
                            gg = wi * 64 + i
                            for j in range(4):
                                a0, c0 = idx2(4 * i + j)
                                g.dma_start(wdst[a0, c0],
                                            wb[i % 2][:, 512 * j:512 * (j + 1)]).then_inc(wrb2[gg % 2], 16)
                        g.wait_ge(wrb2[0], 64 * 32 * (wi + 1))
                        g.wait_ge(wrb2[1], 64 * 32 * (wi + 1))
                b.fence(f"pw_b{wi}")

        if limit < 7:
            return nc
        with ExitStack() as mid:
            dt_fm = b.sb("dt_fm", [64, WIN], F32, mid)
            Bfm = b.sb("Bfm", [128, WIN], BF16, mid)
            Cfm = b.sb("Cfm", [128, WIN], BF16, mid)

            with ExitStack() as st:
                u1f = b.sb("u1f", [128, 16 * MAIN], BF16, st)
                wbuf = [b.sb(f"p1w{i}", [128, 2048], BF16, st) for i in range(2)]
                stg = [b.sb(f"p1s{i}", [128, 512], BF16, st) for i in range(2)]
                gwb = b.sb("p1gw", [128, 16 * 512], BF16, st)
                pss = [b.ps(f"p1p{i}", [128, 512], st) for i in range(2)]
                uld = b.sem("p1u")
                wl2 = [b.sem(f"p1wl{i}") for i in range(2)]
                wr2 = [b.sem(f"p1wr{i}") for i in range(2)]
                md, ev = b.sem("p1md"), b.sem("p1ev")
                for half in range(2):
                    BU, BW, BM = half * 256, half * 560, half * 140
                    with nc.Block(f"p1u_{half}") as blk:
                        @blk.sync
                        def _(sync, half=half, BU=BU):
                            for hc in range(16):
                                sync.dma_start_transpose(
                                    u1f[:, hc * MAIN:(hc + 1) * MAIN],
                                    u1_sc[half * MAIN:(half + 1) * MAIN,
                                          hc * 128:(hc + 1) * 128]).then_inc(uld, 16)
                            sync.wait_ge(uld, BU + 256)
                    b.fence(f"p1u_{half}")
                    if limit == 7 and sub < 1 + 2 * half:
                        return nc

                    NCC = 35
                    with nc.Block(f"p1a_{half}") as blk:
                        @blk.gpsimd
                        def _(g, BM=BM, half=half):
                            for cc in range(NCC):
                                gcc = half * NCC + cc
                                if cc >= 2:
                                    g.wait_ge(md, BM + 4 * (cc - 1))
                                if cc < 34:
                                    g.dma_start(
                                        wbuf[cc % 2][:, :].rearrange("p (k m) -> p k m", m=128),
                                        w_conv[cc].rearrange("k p m -> p k m"),
                                    ).then_inc(wl2[gcc % 2], 16)
                                else:
                                    g.dma_start(
                                        wbuf[cc % 2][:, :16 * 64].rearrange("p (k m) -> p k m", m=64),
                                        w_dt.rearrange("k p m -> p k m"),
                                    ).then_inc(wl2[gcc % 2], 16)

                        @blk.tensor
                        def _(t, BW=BW, BM=BM, half=half):
                            for cc in range(NCC):
                                gcc = half * NCC + cc
                                t.wait_ge(wl2[gcc % 2], 16 * (gcc // 2 + 1))
                                w = 128 if cc < 34 else 64
                                for tb in range(4):
                                    i = cc * 4 + tb
                                    if i >= 2:
                                        t.wait_ge(ev, BM + i - 1)
                                    for hc in range(16):
                                        mm = t.matmul(
                                            pss[i % 2][:w, :],
                                            wbuf[cc % 2][:, hc * w:(hc + 1) * w],
                                            u1f[:, hc * MAIN + tb * 512:hc * MAIN + (tb + 1) * 512],
                                            start=(hc == 0), stop=(hc == 15))
                                    mm.then_inc(md, 1)

                        @blk.vector
                        def _(v, half=half, BM=BM):
                            for cc in range(NCC):
                                for tb in range(4):
                                    i = cc * 4 + tb
                                    v.wait_ge(md, BM + i + 1)
                                    if cc < 34:
                                        j = half * 136 + i
                                        if j >= 2:
                                            v.wait_ge(wr2[j % 2], 16 * (j // 2))
                                        v.tensor_copy(stg[j % 2][:, :],
                                                      pss[i % 2][:, :]).then_inc(ev, 1)
                                    else:
                                        v.tensor_copy(
                                            dt_fm[:, half * MAIN + tb * 512:half * MAIN + (tb + 1) * 512],
                                            pss[i % 2][:64, :]).then_inc(ev, 1)

                        @blk.sync
                        def _(sync, half=half, BM=BM):
                            for cc in range(NCC):
                                for tb in range(4):
                                    i = cc * 4 + tb
                                    sync.wait_ge(ev, BM + i + 1)
                                    if cc < 34:
                                        j = half * 136 + i
                                        sync.dma_start(
                                            conv_sc[cc, :, 4 + half * MAIN + tb * 512:
                                                    4 + half * MAIN + (tb + 1) * 512],
                                            stg[j % 2][:, :]).then_inc(wr2[j % 2], 16)
                            sync.wait_ge(wr2[0], 16 * ((half * 136 + 136 + 1) // 2))
                            sync.wait_ge(wr2[1], 16 * ((half * 136 + 136) // 2))
                    b.fence(f"p1a_{half}")
                    if limit == 7 and sub < 2 + 2 * half:
                        return nc

                    if half == 1:
                        gl, gm, ge = (b.sem(f"p1g{n}") for n in "lme")
                        gws = [b.sem(f"p1gw{i}") for i in range(2)]
                        with nc.Block("p1b") as blk:
                            @blk.gpsimd
                            def _(g):
                                for cb in range(8):
                                    if cb >= 1:
                                        g.wait_ge(gm, 16 * cb)
                                    for hc in range(16):
                                        g.dma_start(gwb[:, hc * 512:(hc + 1) * 512],
                                                    w_gate[cb, hc]).then_inc(gl, 16)

                            @blk.tensor
                            def _(t):
                                for cb in range(8):
                                    t.wait_ge(gl, 256 * (cb + 1))
                                    for tb in range(16):
                                        i = cb * 16 + tb
                                        if i >= 2:
                                            t.wait_ge(ge, i - 1)
                                        for hc in range(16):
                                            mm = t.matmul(
                                                pss[i % 2][:, :],
                                                u1f[:, hc * MAIN + tb * 128:hc * MAIN + (tb + 1) * 128],
                                                gwb[:, hc * 512:(hc + 1) * 512],
                                                start=(hc == 0), stop=(hc == 15))
                                        mm.then_inc(gm, 1)

                            @blk.vector
                            def _(v):
                                for i in range(128):
                                    v.wait_ge(gm, i + 1)
                                    if i >= 2:
                                        v.wait_ge(gws[i % 2], 16 * (i // 2))
                                    v.tensor_copy(stg[i % 2][:, :],
                                                  pss[i % 2][:, :]).then_inc(ge, 1)

                            @blk.sync
                            def _(sync):
                                for cb in range(8):
                                    for tb in range(16):
                                        i = cb * 16 + tb
                                        sync.wait_ge(ge, i + 1)
                                        sync.dma_start(
                                            bass.AP(gate_sc, tb * 128 * INTER + cb * 512,
                                                    [[INTER, 128], [1, 512]]),
                                            stg[i % 2][:, :]).then_inc(gws[i % 2], 16)
                                sync.wait_ge(gws[0], 16 * 64)
                                sync.wait_ge(gws[1], 16 * 64)
                        b.fence("p1b")

            if limit < 8:
                return nc
            with ExitStack() as st:
                convw_sb = b.sb("convw_sb", [128, 34 * KC], F32, st)
                convb_sb = b.sb("convb_sb", [128, 34], F32, st)
                cin = [b.sb(f"p2i{i}", [128, 34 * 516], BF16, st) for i in range(2)]
                cacc = [b.sb(f"p2a{i}", [128, 512], BF16, st) for i in range(2)]
                cout = [b.sb(f"p2o{i}", [128, 32 * 512], BF16, st) for i in range(2)]
                zl = b.sb("p2z", [128, 4 * 34], BF16, st)
                ld2 = [b.sem(f"p2ld{i}") for i in range(2)]
                wr2 = [b.sem(f"p2wr{i}") for i in range(2)]
                cv, sl, zz, zzg = (b.sem(f"p2_{n}") for n in "bcef")
                NS = 8
                with nc.Block("p2") as blk:
                    @blk.gpsimd
                    def _(g):
                        b.drain(g, "g", g.memset(zl[:, :], 0.0))
                        g.dma_start(
                            conv_sc.rearrange("c p t -> p c t")[:, :, 0:4],
                            zl[:, :].rearrange("p (c k) -> p c k", k=4),
                        ).then_inc(zzg, 16)
                        g.dma_start(
                            conv_sc.rearrange("c p t -> p c t")[:, :, WIN + 4:WIN + 5],
                            zl[:, :34].rearrange("p (c k) -> p c k", k=1),
                            allow_slow_non_contiguous=True,
                        ).then_inc(zzg, 16)
                        for s_ in range(NS):
                            g.wait_ge(sl, s_ * 34 + 32)
                            g.dma_start(
                                xs_sc.rearrange("c p t -> p c t")[:, :, s_ * 512:(s_ + 1) * 512],
                                cout[s_ % 2][:, :].rearrange("p (c t) -> p c t", t=512),
                            ).then_inc(wr2[s_ % 2], 16)
                        g.wait_ge(wr2[0], 16 * (NS // 2))
                        g.wait_ge(wr2[1], 16 * (NS // 2))
                        g.wait_ge(sl, NS * 34)

                    @blk.sync
                    def _(sync):
                        sync.dma_start(convw_sb[:, :].rearrange("p (c k) -> p c k", k=KC),
                                       convw.rearrange("c p k -> p c k")).then_inc(zz, 16)
                        sync.dma_start(convb_sb[:, :], convb[:, :]).then_inc(zz, 16)
                        sync.wait_ge(zz, 32)
                        sync.wait_ge(zzg, 32)
                        for s_ in range(NS):
                            if s_ >= 2:
                                sync.wait_ge(cv, 34 * (s_ - 1))
                            sync.dma_start(
                                cin[s_ % 2][:, :].rearrange("p (c t) -> p c t", t=516),
                                conv_sc.rearrange("c p t -> p c t")[:, :, s_ * 512 + 1:s_ * 512 + 517],
                            ).then_inc(ld2[s_ % 2], 16)

                    @blk.vector
                    def _(v):
                        for s_ in range(NS):
                            v.wait_ge(ld2[s_ % 2], 16 * (s_ // 2 + 1))
                            for cc in range(34):
                                i = s_ * 34 + cc
                                if i >= 2:
                                    v.wait_ge(sl, i - 1)
                                base = cc * 516
                                src = cin[s_ % 2]
                                v.tensor_scalar_mul(cacc[i % 2][:, :], src[:, base:base + 512],
                                                    convw_sb[:, cc * KC:cc * KC + 1])
                                for k in range(1, KC):
                                    st3 = v.scalar_tensor_tensor(
                                        cacc[i % 2][:, :], src[:, base + k:base + k + 512],
                                        convw_sb[:, cc * KC + k:cc * KC + k + 1],
                                        cacc[i % 2][:, :], ALU.mult, ALU.add)
                                st3.then_inc(cv, 1)

                    @blk.scalar
                    def _(s):
                        for s_ in range(NS):
                            for cc in range(34):
                                i = s_ * 34 + cc
                                s.wait_ge(cv, i + 1)
                                if cc < 32:
                                    if s_ >= 2:
                                        s.wait_ge(wr2[s_ % 2], 16 * (s_ // 2))
                                    dst = cout[s_ % 2][:, cc * 512:(cc + 1) * 512]
                                elif cc == 32:
                                    dst = Bfm[:, s_ * 512:(s_ + 1) * 512]
                                else:
                                    dst = Cfm[:, s_ * 512:(s_ + 1) * 512]
                                s.activation(dst, cacc[i % 2][:, :], AF.Silu,
                                             bias=convb_sb[:, cc:cc + 1]).then_inc(sl, 1)
            b.fence("p2")

            if limit < 9:
                return nc
            _emit_p3(nc, b, dict(
                dt_fm=dt_fm, Bfm=Bfm, Cfm=Cfm, selbank=selbank, identf=identf,
                identb=identb, mask01=mask01, d_rep=d_rep, dtbias=dtbias, alog=alog,
                maskst=maskst, xs_sc=xs_sc, gate_sc=gate_sc, y_sc=y_sc,
                onesr_sb=onesr_sb, onescol=onescol, epsln=epsln))
            b.fence("p3")

        if limit < 10:
            return nc
        _emit_p456(nc, b, dict(
            xwin=xwin, y_sc=y_sc, xq_sc=xq_sc, hq_sc=hq_sc, y_out=y_out,
            wout_sc=wout_sc, wq_g=wq_g, wq_u=wq_u, wq_d=wq_d, lnww=lnww,
            invsw=invsw, epsln=epsln, epsrms=epsrms, epsq=epsq))
        b.fence("end")

    return nc


def _emit_p3(nc, b, d):
    st = ExitStack()
    with st:
        sel_sb = b.sb("sel_sb", [64, 32 * 128], F32, st)
        identf_sb = b.sb("identf_sb", [128, 128], F32, st)
        identb_sb = b.sb("identb_sb", [128, 128], BF16, st)
        mask01_sb = b.sb("mask01_sb", [128, 2 * CHUNK], BF16, st)
        drep_sb = b.sb("drep_sb", [128, INTER], BF16, st)
        dtb_sb = b.sb("dtb_sb", [64, 1], F32, st)
        negA_sb = b.sb("negA_sb", [64, 1], F32, st)
        maskst_sb = b.sb("maskst_sb", [128, 1], F32, st)
        Ssb = b.sb("Ssb", [128, INTER], F32, st)
        Sbf = b.sb("Sbf", [128, INTER], BF16, st)
        dtp_fm = b.sb("dtp_fm", [64, CHUNK], F32, st)
        tmp_fm = b.sb("tmp_fm", [64, CHUNK], F32, st)
        adt_fm = b.sb("adt_fm", [64, CHUNK], F32, st)
        cumA_fm = b.sb("cumA_fm", [64, CHUNK], F32, st)
        dcy_fm = b.sb("dcy_fm", [64, CHUNK], F32, st)
        dpd_fm = b.sb("dpd_fm", [64, CHUNK], F32, st)
        gcol = b.sb("gcol", [64, 1], F32, st)
        grow = b.sb("grow", [1, 64], F32, st)
        gbc = b.sb("gbc", [128, 64], F32, st)
        dtp_tm = [b.sb(f"dtpt{i}", [128, 64], F32, st) for i in range(2)]
        ncum_tm = [b.sb(f"ncumt{i}", [128, 64], F32, st) for i in range(2)]
        ecum_tm = [b.sb(f"ecumt{i}", [128, 64], F32, st) for i in range(2)]
        dpd_tm = [b.sb(f"dpdt{i}", [128, 64], F32, st) for i in range(2)]
        B_tm = [b.sb(f"Btm{i}", [128, 128], BF16, st) for i in range(2)]
        xs_tm = [b.sb(f"xstm{i}", [128, INTER], BF16, st) for i in range(2)]
        xdt_tm = [b.sb(f"xdtm{i}", [128, INTER], BF16, st) for i in range(2)]
        xdd_tm = [b.sb(f"xddm{i}", [128, INTER], BF16, st) for i in range(2)]
        xsD_tm = [b.sb(f"xsDm{i}", [128, INTER], BF16, st) for i in range(2)]
        scm = [b.sb(f"scm{i}", [128, CHUNK], BF16, st) for i in range(2)]
        Et = [[b.sb(f"Et{p}_{s2}", [128, CHUNK], F32, st) for s2 in range(2)]
              for p in range(2)]
        sex = [[b.sb(f"sex{p}_{s2}", [128, CHUNK], BF16, st) for s2 in range(2)]
               for p in range(2)]
        scf = [[b.sb(f"scf{p}_{s2}", [128, CHUNK], BF16, st) for s2 in range(2)]
               for p in range(2)]
        ytmp = [[b.sb(f"ytp{p}_{lt}", [128, 64], F32, st) for lt in range(2)]
                for p in range(2)]
        Y_sb = [b.sb(f"Ysb{lt}", [128, INTER], BF16, st) for lt in range(2)]
        gate_tm = [b.sb(f"gtm{i}", [128, INTER], BF16, st) for i in range(2)]
        ysq = b.sb("ysq", [128, INTER], BF16, st)
        rc3 = b.sb("rc3", [128, 8], F32, st)
        ps_hp = [b.ps(f"pshp{i}", [128, 320], st) for i in range(2)]
        ps_bc = [b.ps(f"psbc{i}", [128, 256], st) for i in range(2)]
        ps_t = b.ps("ps_t", [128, 512], st)
        ps_sc = b.ps("ps_sc", [128, 512], st)

        def hp_st(par):
            return ps_hp[par][:, 0:64]

        def hp_yd(par, lt):
            return ps_hp[par][:, 64 + lt * 64:128 + lt * 64]

        def hp_yo(par, lt):
            return ps_hp[par][:, 192 + lt * 64:256 + lt * 64]

        s_p0 = b.sem("p3_p0")
        s_p0g = b.sem("p3_p0g")
        s_ldx = b.sem("p3_ldx")
        s_ldb = b.sem("p3_ldb")
        s_ldg = b.sem("p3_ldg")
        s_pre = b.sem("p3_pre")
        s_prs = b.sem("p3_prs")
        s_prt = b.sem("p3_prt")
        s_selm = b.sem("p3_selm")
        s_E = b.sem("p3_E")
        s_ex = b.sem("p3_ex")
        s_sf = b.sem("p3_sf")
        s_ym = b.sem("p3_ym")
        s_yo = b.sem("p3_yo")
        s_ev = b.sem("p3_ev")
        s_su = b.sem("p3_su")
        s_sg = b.sem("p3_sg")
        s_r = b.sem("p3_r")
        s_gd = b.sem("p3_gd")
        s_yw = b.sem("p3_yw")
        s_sel2 = b.sem("p3_sel2")

        dt_fm, Bfm, Cfm = d["dt_fm"], d["Bfm"], d["Cfm"]
        xs_sc, gate_sc, y_sc = d["xs_sc"], d["gate_sc"], d["y_sc"]
        onesr_sb, onescol, epsln = d["onesr_sb"], d["onescol"], d["epsln"]

        PL = []
        tt = dict(ldx=0, ldg=0, pre=0, prs=0, prt=0, selm=0, E=0, ex=0, sf=0,
                  ym=0, yo=0, ev=0, su=0, sg=0, r=0, gd=0, yw=0, sel2=0)
        for c in range(NCH):
            main = c >= 8
            p = dict(tt)
            p.update(c=c, main=main)
            tt["ldx"] += 32
            if main:
                tt["ldg"] += 32
            tt["pre"] += 4
            tt["prs"] += 4
            tt["prt"] += 2 + (1 if main else 0)
            if main:
                tt["selm"] += 64
                tt["E"] += 128
                tt["ex"] += 128
                tt["sf"] += 128
                tt["yo"] += 128
                tt["ev"] += 128
                tt["sg"] += 1
                tt["r"] += 1
                tt["gd"] += 2
                tt["yw"] += 32
                tt["sel2"] += 32
            tt["ym"] += 64
            tt["su"] += 64
            for k, v_ in tt.items():
                p[k + "1"] = v_
            PL.append(p)

        W = lambda eng, sem, val: eng.wait_ge(sem, val) if val > 0 else None

        def _v_tail(v, p, hm, main):
            pm = hm % 2
            hsm = slice(hm * 64, (hm + 1) * 64)
            v.wait_ge(s_ym, p["ym"] + hm + 1)
            if main:
                v.wait_ge(s_yo, p["yo"] + 2 * hm + 2)
                for lt in range(2):
                    v.tensor_add(Y_sb[lt][:, hsm], hp_yd(pm, lt),
                                 ytmp[pm][lt][:, :]).then_inc(s_ev, 1)
            v.scalar_tensor_tensor(Ssb[:, hsm], Ssb[:, hsm], gbc[:, hm:hm + 1],
                                   hp_st(pm), ALU.mult, ALU.add).then_inc(s_su, 1)

        with nc.Block("p3") as blk:
            @blk.gpsimd
            def _(g):
                g.memset(Ssb[:, :], 0.0)
                g.memset(Sbf[:, :], 0.0).then_inc(s_p0, 1)
                g.dma_start(drep_sb[:, :],
                            bass.AP(d["d_rep"], 0, [[0, 128], [1, INTER]])).then_inc(s_p0g, 16)
                for p in PL:
                    if not p["main"]:
                        continue
                    mc = p["c"] - 8
                    g.wait_ge(s_gd, p["gd"] + 2)
                    for lt in range(2):
                        g.dma_start(
                            y_sc[mc * 256 + lt * 128:mc * 256 + (lt + 1) * 128, :],
                            Y_sb[lt][:, :]).then_inc(s_yw, 16)
                g.wait_ge(s_yw, PL[-1]["yw1"])

            @blk.sync
            def _(sync):
                for dst, src in (
                    (sel_sb[:, :], d["selbank"][:, 0:32 * 128]),
                    (identf_sb[:, :], d["identf"][:, :]),
                    (identb_sb[:, :], d["identb"][:, :]),
                    (mask01_sb[:, :], d["mask01"].rearrange("s p c -> p s c")),
                    (dtb_sb[:, :], d["dtbias"][:, :]),
                    (negA_sb[:, :], d["alog"][:, :]),
                    (maskst_sb[:, :], bass.AP(d["maskst"], 0, [[0, 128], [1, 1]])),
                ):
                    sync.dma_start(dst, src).then_inc(s_p0, 16)
                for p in PL:
                    c = p["c"]
                    if c >= 1:
                        sync.wait_ge(s_pre, PL[c - 1]["pre1"])
                    for s2 in range(2):
                        sync.dma_start_transpose(
                            xs_tm[s2][:, :],
                            xs_sc.rearrange("c p t -> (c p) t")[
                                :, c * 256 + s2 * 128:c * 256 + (s2 + 1) * 128],
                        ).then_inc(s_ldx, 16)
                    if c >= 1:
                        sync.wait_ge(s_ym, PL[c - 1]["ym1"])
                    for s2 in range(2):
                        sync.dma_start_transpose(
                            B_tm[s2][:, :],
                            Bfm[:, c * 256 + s2 * 128:c * 256 + (s2 + 1) * 128],
                        ).then_inc(s_ldb, 16)
                    if p["main"]:
                        mc = c - 8
                        if c >= 9 and PL[c - 1]["main"]:
                            sync.wait_ge(s_gd, PL[c - 1]["gd"] + 1)
                        for s2 in range(2):
                            sync.dma_start(
                                gate_tm[s2][:, :],
                                gate_sc[mc * 256 + s2 * 128:mc * 256 + (s2 + 1) * 128, :],
                            ).then_inc(s_ldg, 16)
                        sync.wait_ge(s_selm, p["selm"] + 32)
                        sync.dma_start(sel_sb[:, :],
                                       d["selbank"][:, 32 * 128:]).then_inc(s_sel2, 16)
                        sync.wait_ge(s_selm, p["selm"] + 64)
                        sync.dma_start(sel_sb[:, :],
                                       d["selbank"][:, 0:32 * 128]).then_inc(s_sel2, 16)
                sync.wait_ge(s_yw, PL[-1]["yw1"])
                sync.wait_ge(s_sel2, PL[-1]["sel21"])

            @blk.scalar
            def _(s):
                s.wait_ge(s_p0, 112 + 1)
                s.activation(negA_sb[:, :1], negA_sb[:, :1], AF.Exp).then_inc(s_p0, 1)
                for p in PL:
                    c, main = p["c"], p["main"]
                    prev = PL[c - 1] if c >= 1 else None
                    dsl = dt_fm[:, c * 256:(c + 1) * 256]
                    if prev is not None:
                        s.wait_ge(s_pre, prev["pre"] + 2)
                        s.wait_ge(s_prt, prev["prt"] + 1)
                    s.activation(dtp_fm[:, :], dsl, AF.Relu, bias=dtb_sb[:, :1])
                    s.activation(tmp_fm[:, :], dsl, AF.Abs, bias=dtb_sb[:, :1])
                    s.activation(tmp_fm[:, :], tmp_fm[:, :], AF.Exp, scale=-1.0)
                    s.activation(tmp_fm[:, :], tmp_fm[:, :], AF.Ln,
                                 bias=onescol[:64, :1]).then_inc(s_prs, 1)
                    s.wait_ge(s_pre, p["pre"] + 1)
                    s.activation(dcy_fm[:, :], cumA_fm[:, :], AF.Exp, scale=-1.0,
                                 bias=cumA_fm[:, 255:256])
                    s.activation(gcol[:, :], cumA_fm[:, 255:256],
                                 AF.Exp).then_inc(s_prs, 1)
                    s.wait_ge(s_prt, p["prt"] + 1)
                    s.copy(grow[:1, :], ps_t[:1, 448:512]).then_inc(s_prs, 1)
                    s.wait_ge(s_pre, p["pre"] + 3)
                    if main:
                        for s2 in range(2):
                            ec = s.activation(ecum_tm[s2][:, :], ncum_tm[s2][:, :],
                                              AF.Exp, scale=-1.0)
                        ec.then_inc(s_prs, 1)
                    else:
                        s.nop().then_inc(s_prs, 1)
                    if main:
                        for h in range(64):
                            par = h % 2
                            s.wait_ge(s_E, p["E"] + 2 * h + 2)
                            W(s, s_sf, p["sf"] + 2 * (h - 2) + 2)
                            for s2 in range(2):
                                s.activation(sex[par][s2][:, :], Et[par][s2][:, :],
                                             AF.Exp).then_inc(s_ex, 1)
                            if h >= 1:
                                hm = h - 1
                                s.wait_ge(s_ym, p["ym"] + hm + 1)
                                W(s, s_ev, p["ev"] + 2 * (hm - 2) + 2)
                                for lt in range(2):
                                    s.activation(ytmp[hm % 2][lt][:, :],
                                                 hp_yo(hm % 2, lt), AF.Copy,
                                                 scale=ecum_tm[lt][:, hm:hm + 1]).then_inc(s_yo, 1)
                        hm = 63
                        s.wait_ge(s_ym, p["ym"] + 64)
                        W(s, s_ev, p["ev"] + 2 * (hm - 2) + 2)
                        for lt in range(2):
                            s.activation(ytmp[hm % 2][lt][:, :], hp_yo(hm % 2, lt),
                                         AF.Copy,
                                         scale=ecum_tm[lt][:, hm:hm + 1]).then_inc(s_yo, 1)
                        s.wait_ge(s_ldg, p["ldg"] + 32)
                        for s2 in range(2):
                            sg = s.activation(gate_tm[s2][:, :], gate_tm[s2][:, :],
                                              AF.Silu)
                        sg.then_inc(s_sg, 1)
                        s.wait_ge(s_gd, p["gd"] + 1)
                        for lt in range(2):
                            b.drain(s, "s", s.activation(rc3[:, 4 + lt:5 + lt],
                                                         rc3[:, 2 + lt:3 + lt],
                                                         AF.Ln, bias=epsln[:, :1]))
                            ex = s.activation(rc3[:, 4 + lt:5 + lt],
                                              rc3[:, 4 + lt:5 + lt],
                                              AF.Exp, scale=-0.5)
                        ex.then_inc(s_r, 1)

            @blk.tensor
            def _(t):
                t.wait_ge(s_p0, 112 + 2)
                t.wait_ge(s_p0g, 16)
                for p in PL:
                    c, main = p["c"], p["main"]
                    prev = PL[c - 1] if c >= 1 else None
                    if prev is not None:
                        t.wait_ge(s_su, prev["su1"])
                    t.wait_ge(s_pre, p["pre"] + 2)
                    t.wait_ge(s_prs, p["prs"] + 2)
                    for s2 in range(2):
                        t.transpose(ps_t[:, s2 * 64:(s2 + 1) * 64],
                                    dtp_fm[:, s2 * 128:(s2 + 1) * 128], identf_sb[:64, :64])
                        t.transpose(ps_t[:, 128 + s2 * 64:128 + (s2 + 1) * 64],
                                    cumA_fm[:, s2 * 128:(s2 + 1) * 128], identf_sb[:64, :64])
                        t.transpose(ps_t[:, 256 + s2 * 64:256 + (s2 + 1) * 64],
                                    dpd_fm[:, s2 * 128:(s2 + 1) * 128], identf_sb[:64, :64])
                    t.transpose(ps_t[:1, 448:512], gcol[:, :1],
                                identf_sb[:64, :64]).then_inc(s_prt, 1)
                    t.wait_ge(s_prs, p["prs"] + 3)
                    t.matmul(ps_t[:, 384:448], onesr_sb[:1, :], grow[:1, :],
                             start=True, stop=True).then_inc(s_prt, 1)
                    if main:
                        t.wait_ge(s_pre, p["pre"] + 3)
                        for s2 in range(2):
                            mm = t.matmul(ps_sc[:, s2 * 256:(s2 + 1) * 256],
                                          Bfm[:, c * 256 + s2 * 128:c * 256 + (s2 + 1) * 128],
                                          Cfm[:, c * 256:(c + 1) * 256],
                                          start=True, stop=True)
                        mm.then_inc(s_prt, 1)
                    t.wait_ge(s_pre, p["pre1"])
                    t.wait_ge(s_ldb, 32 * (c + 1))

                    def sel_mm(h):
                        par = h % 2
                        W(t, s_E, p["E"] + 2 * (h - 2) + 2)
                        if h == 32:
                            t.wait_ge(s_sel2, p["sel2"] + 16)
                        if h == 0:
                            W(t, s_sel2, p["sel2"])
                        hsel = h % 32
                        t.matmul(ps_bc[par][:, :], sel_sb[:, hsel * 128:(hsel + 1) * 128],
                                 cumA_fm[:, :], start=True, stop=True).then_inc(s_selm, 1)

                    if main:
                        sel_mm(0)
                        sel_mm(1)
                    for h in range(64):
                        par = h % 2
                        hs = slice(h * 64, (h + 1) * 64)
                        if main and h + 2 < 64:
                            sel_mm(h + 2)
                        W(t, s_su, p["su"] + (h - 2) + 1)
                        t.matmul(hp_st(par), B_tm[0][:, :], xdd_tm[0][:, hs],
                                 start=True, stop=False)
                        mm = t.matmul(hp_st(par), B_tm[1][:, :], xdd_tm[1][:, hs],
                                      start=False, stop=True)
                        if main:
                            t.wait_ge(s_sf, p["sf"] + 2 * h + 2)
                            W(t, s_ev, p["ev"] + 2 * (h - 2) + 2)
                            t.matmul(hp_yd(par, 0), scf[par][0][:, 0:128],
                                     xdt_tm[0][:, hs], start=True, stop=False)
                            t.matmul(hp_yd(par, 0), identb_sb[:, :],
                                     xsD_tm[0][:, hs], start=False, stop=True)
                            t.matmul(hp_yd(par, 1), scf[par][0][:, 128:256],
                                     xdt_tm[0][:, hs], start=True, stop=False)
                            t.matmul(hp_yd(par, 1), scf[par][1][:, 128:256],
                                     xdt_tm[1][:, hs], start=False, stop=False)
                            t.matmul(hp_yd(par, 1), identb_sb[:, :],
                                     xsD_tm[1][:, hs], start=False, stop=True)
                            W(t, s_yo, p["yo"] + 2 * (h - 2) + 2)
                            t.matmul(hp_yo(par, 0), Cfm[:, c * 256:c * 256 + 128],
                                     Sbf[:, hs], start=True, stop=True)
                            mm = t.matmul(hp_yo(par, 1),
                                          Cfm[:, c * 256 + 128:c * 256 + 256],
                                          Sbf[:, hs], start=True, stop=True)
                        mm.then_inc(s_ym, 1)

            @blk.vector
            def _(v):
                v.wait_ge(s_p0, 112 + 2)
                v.wait_ge(s_p0g, 16)
                v.tensor_scalar_mul(negA_sb[:64, :1], negA_sb[:64, :1],
                                    -1.0).then_inc(s_p0, 1)
                v.wait_ge(s_p0, 112 + 3)
                for p in PL:
                    c, main = p["c"], p["main"]
                    prev = PL[c - 1] if c >= 1 else None
                    if prev is not None:
                        v.wait_ge(s_ym, prev["ym1"])
                        v.wait_ge(s_yo, prev["yo1"])
                    v.wait_ge(s_prs, p["prs"] + 1)
                    v.tensor_add(dtp_fm[:, :], dtp_fm[:, :], tmp_fm[:, :])
                    v.tensor_scalar_mul(adt_fm[:, :], dtp_fm[:, :], negA_sb[:64, :1])
                    v.tensor_tensor_scan(cumA_fm[:, :], adt_fm[:, :], adt_fm[:, :],
                                         0.0, ALU.add, ALU.bypass).then_inc(s_pre, 1)
                    v.wait_ge(s_prs, p["prs"] + 2)
                    v.tensor_mul(dpd_fm[:, :], dtp_fm[:, :],
                                 dcy_fm[:, :]).then_inc(s_pre, 1)
                    v.wait_ge(s_prt, p["prt"] + 2)
                    for s2 in range(2):
                        v.tensor_copy(dtp_tm[s2][:, :], ps_t[:, s2 * 64:(s2 + 1) * 64])
                        v.tensor_scalar_mul(ncum_tm[s2][:, :],
                                            ps_t[:, 128 + s2 * 64:128 + (s2 + 1) * 64], -1.0)
                        v.tensor_copy(dpd_tm[s2][:, :],
                                      ps_t[:, 256 + s2 * 64:256 + (s2 + 1) * 64])
                    v.tensor_copy(gbc[:, :], ps_t[:, 384:448]).then_inc(s_pre, 1)
                    v.wait_ge(s_pre, p["pre"] + 3)
                    v.wait_ge(s_ldx, 32 * (c + 1))
                    for s2 in range(2):
                        if main:
                            v.tensor_mul(xsD_tm[s2][:, :], xs_tm[s2][:, :], drep_sb[:, :])
                        for h in range(64):
                            hs = slice(h * 64, (h + 1) * 64)
                            if main:
                                v.tensor_scalar_mul(xdt_tm[s2][:, hs], xs_tm[s2][:, hs],
                                                    dtp_tm[s2][:, h:h + 1])
                            v.tensor_scalar_mul(xdd_tm[s2][:, hs], xs_tm[s2][:, hs],
                                                dpd_tm[s2][:, h:h + 1])
                    if main:
                        v.wait_ge(s_prt, p["prt1"])
                        for s2 in range(2):
                            v.tensor_mul(scm[s2][:, :], ps_sc[:, s2 * 256:(s2 + 1) * 256],
                                         mask01_sb[:, s2 * 256:(s2 + 1) * 256])
                    if c == 0:
                        v.wait_ge(s_p0, 112 + 3)
                    v.tensor_copy(Sbf[:, :], Ssb[:, :]).then_inc(s_pre, 1)
                    if main and p["yw"] > 0:
                        v.wait_ge(s_yw, p["yw"])
                    for h in range(64):
                        par = h % 2
                        if main:
                            v.wait_ge(s_selm, p["selm"] + h + 1)
                            W(v, s_ex, p["ex"] + 2 * (h - 2) + 2)
                            for s2 in range(2):
                                v.tensor_scalar(Et[par][s2][:, :], ps_bc[par][:, :],
                                                ncum_tm[s2][:, h:h + 1], 0.0,
                                                ALU.add, ALU.min).then_inc(s_E, 1)
                            v.wait_ge(s_ex, p["ex"] + 2 * h + 2)
                            W(v, s_ym, p["ym"] + (h - 2) + 1)
                            for s2 in range(2):
                                v.tensor_mul(scf[par][s2][:, :], scm[s2][:, :],
                                             sex[par][s2][:, :]).then_inc(s_sf, 1)
                        if h >= 1:
                            _v_tail(v, p, h - 1, main)
                    _v_tail(v, p, 63, main)
                    if c == 7:
                        v.tensor_scalar_mul(Ssb[:, :], Ssb[:, :], maskst_sb[:, :1])
                    if main:
                        v.wait_ge(s_sg, p["sg"] + 1)
                        for lt in range(2):
                            v.tensor_mul(Y_sb[lt][:, :], Y_sb[lt][:, :], gate_tm[lt][:, :])
                        v.tensor_mul(ysq[:, :], Y_sb[0][:, :], Y_sb[0][:, :])
                        v.reduce_sum(rc3[:, 0:1], ysq[:, :], axis=AXX)
                        v.tensor_mul(ysq[:, :], Y_sb[1][:, :], Y_sb[1][:, :])
                        b.drain(v, "v", v.reduce_sum(rc3[:, 1:2], ysq[:, :], axis=AXX))
                        v.tensor_scalar_mul(rc3[:, 2:3], rc3[:, 0:1], 1.0 / INTER)
                        v.tensor_scalar_mul(rc3[:, 3:4], rc3[:, 1:2],
                                            1.0 / INTER).then_inc(s_gd, 1)
                        v.wait_ge(s_r, p["r"] + 1)
                        for lt in range(2):
                            ym = v.tensor_scalar_mul(Y_sb[lt][:, :], Y_sb[lt][:, :],
                                                     rc3[:, 4 + lt:5 + lt])
                        ym.then_inc(s_gd, 1)



def _emit_p456(nc, b, d):
    st = ExitStack()
    with st:
        arena = b.sb("arena", [128, 4 * FFI], BF16, st)
        stream = [b.sb(f"stm{i}", [128, H], F32, st) for i in range(4)]
        w4b = [b.sb(f"w4b{i}", [128, 4 * 512], BF16, st) for i in range(2)]
        xq_fm = b.sb("xq_fm", [128, 16 * 512], BF16, st)
        qt = b.sb("qt", [128, 4096], F32, st)
        qi = b.sb("qi", [128, 4096], I32, st)
        qb = b.sb("qb", [128, 4096], BF16, st)
        hlnw = b.sb("hlnw", [128, FFI], BF16, st)
        lnw_sb = b.sb("lnw_sb", [128, FFI], BF16, st)
        sgt = [b.sb(f"sgt{i}", [128, 512], F32, st) for i in range(2)]
        ost = [b.sb(f"ost{i}", [128, 512], F32, st) for i in range(2)]
        qc = b.sb("qc", [128, 16], F32, st)
        invsx = b.sb("invsx", [128, 4], F32, st)
        invsh = b.sb("invsh", [128, 4], F32, st)
        comb = b.sb("comb", [128, 12], F32, st)
        pga = [b.ps(f"pg{i}", [128, 512], st) for i in range(4)]
        pua = [b.ps(f"pu{i}", [128, 512], st) for i in range(4)]

        xwin, y_sc, xq_sc, hq_sc, y_out = (d[k] for k in
                                           ("xwin", "y_sc", "xq_sc", "hq_sc", "y_out"))
        wout_sc, wq_g, wq_u, wq_d = (d[k] for k in ("wout_sc", "wq_g", "wq_u", "wq_d"))
        invsw, epsln, epsrms, epsq = (d[k] for k in ("invsw", "epsln", "epsrms", "epsq"))

        s_l0 = b.sem("p456_l0")
        s_yl = b.sem("p4_yl")
        s_wl2 = [b.sem(f"p4_wl{i}") for i in range(2)]
        s_tc = b.sem("p4_tc")
        s_pe = b.sem("p4_pe")
        s_xl = b.sem("p4_xl")
        s_sr = b.sem("p4_sr")
        s_q4 = b.sem("p4_q4")
        s_qs = b.sem("p4_qs")
        s_xw = b.sem("p4_xw")
        s_x5 = b.sem("p5_xl")
        s_w52 = [b.sem(f"p5_wl{i}") for i in range(2)]
        s_t5 = b.sem("p5_tc")
        s_sg5 = b.sem("p5_sg")
        s_h5 = b.sem("p5_h")
        s_hq = b.sem("p5_hq")
        s_hqs = b.sem("p5_hqs")
        s_hw = b.sem("p5_hw")
        s_h6 = b.sem("p6_hl")
        s_w62 = [b.sem(f"p6_wl{i}") for i in range(2)]
        s_t6 = b.sem("p6_tc")
        s_e6 = b.sem("p6_ev")
        s_ow2 = [b.sem(f"p6_ow{i}") for i in range(2)]

        with nc.Block("p456_init") as blk:
            @blk.gpsimd
            def _(g):
                g.dma_start(lnw_sb[:, :],
                            bass.AP(d["lnww"], 0, [[0, 128], [1, FFI]])).then_inc(s_l0, 16)

            @blk.scalar
            def _(s):
                s.wait_ge(s_l0, 16)
        b.fence("p456i")

        for sp in range(NSUP):
            t0 = sp * 512

            with nc.Block(f"p4a_{sp}") as blk:
                @blk.sync
                def _(sync, sp=sp, t0=t0):
                    for yc in range(32):
                        sync.dma_start_transpose(
                            arena[:, yc * 512:(yc + 1) * 512],
                            y_sc[t0:t0 + 512, yc * 128:(yc + 1) * 128]).then_inc(s_yl, 16)
                    for tb in range(4):
                        if tb >= 1:
                            sync.wait_ge(s_sr, sp * 4 + tb)
                        sync.dma_start(
                            qt[:, :H],
                            xwin[MAIN + t0 + tb * 128:MAIN + t0 + (tb + 1) * 128, :],
                        ).then_inc(s_xl, 16)

                @blk.gpsimd
                def _(g, sp=sp):
                    for j in range(32):
                        if j >= 2:
                            g.wait_ge(s_tc, sp * 128 + 4 * (j - 1))
                        ht, ycq = divmod(j, 8)
                        g.dma_start(
                            w4b[j % 2][:, :].rearrange("p (k n) -> p k n", n=512),
                            wout_sc[ht, ycq * 4:(ycq + 1) * 4].rearrange(
                                "k p n -> p k n")).then_inc(s_wl2[j % 2], 16)

                @blk.tensor
                def _(t, sp=sp):
                    t.wait_ge(s_yl, 16 * 32 * (sp + 1))
                    for ht in range(4):
                        if ht >= 1:
                            t.wait_ge(s_pe, sp * 16 + 4 * ht)
                        for ycq in range(8):
                            j = ht * 8 + ycq
                            t.wait_ge(s_wl2[j % 2], 16 * (sp * 16 + j // 2 + 1))
                            for tb in range(4):
                                for y4 in range(4):
                                    yc = ycq * 4 + y4
                                    mm = t.matmul(
                                        pga[tb][:, :],
                                        arena[:, yc * 512 + tb * 128:
                                              yc * 512 + (tb + 1) * 128],
                                        w4b[j % 2][:, y4 * 512:(y4 + 1) * 512],
                                        start=(yc == 0), stop=(yc == 31))
                                mm.then_inc(s_tc, 1)

                @blk.vector
                def _(v, sp=sp):
                    for ht in range(4):
                        v.wait_ge(s_tc, sp * 128 + 32 * (ht + 1))
                        for tb in range(4):
                            v.tensor_copy(stream[tb][:, ht * 512:(ht + 1) * 512],
                                          pga[tb][:, :]).then_inc(s_pe, 1)
                    for tb in range(4):
                        v.wait_ge(s_xl, 16 * (sp * 4 + tb + 1))
                        v.tensor_add(stream[tb][:, :], stream[tb][:, :],
                                     qt[:, :H]).then_inc(s_sr, 1)
            b.fence(f"p4a_{sp}")

            with nc.Block(f"p4b_{sp}") as blk:
                @blk.vector
                def _(v, sp=sp):
                    for tb in range(4):
                        j = sp * 4 + tb
                        b.drain(v, "v", v.reduce_sum(qc[:, 0:1], stream[tb][:, :],
                                                     axis=AXX))
                        v.wait_ge(s_qs, 3 * j + 1)
                        v.tensor_scalar_mul(qc[:, 2:3], qc[:, 0:1], 1.0 / H)
                        b.drain(v, "v", v.tensor_scalar_mul(qc[:, 3:4], qc[:, 1:2],
                                                            1.0 / H))
                        b.drain(v, "v", v.tensor_mul(qc[:, 4:5], qc[:, 2:3],
                                                     qc[:, 2:3]))
                        v.tensor_sub(qc[:, 4:5], qc[:, 3:4], qc[:, 4:5]).then_inc(s_q4, 1)
                        v.wait_ge(s_qs, 3 * j + 2)
                        b.drain(v, "v", v.reduce_max(qc[:, 7:8], qt[:, :H], axis=AXX,
                                                     apply_absolute_value=True))
                        b.drain(v, "v", v.tensor_scalar_max(qc[:, 7:8], qc[:, 7:8],
                                                            EPS_Q))
                        b.drain(v, "v", v.reciprocal(qc[:, 8:9], qc[:, 7:8]))
                        b.drain(v, "v", v.tensor_scalar_mul(qc[:, 9:10], qc[:, 8:9],
                                                            127.0))
                        v.tensor_scalar_mul(invsx[:, tb:tb + 1], qc[:, 7:8], 1.0 / 127.0)
                        v.tensor_scalar(qt[:, :H], qt[:, :H], qc[:, 9:10], MAGIC,
                                        ALU.mult, ALU.add)
                        v.tensor_scalar(qt[:, :H], qt[:, :H], -MAGIC, None, ALU.add)
                        v.tensor_copy(qi[:, :H], qt[:, :H]).then_inc(s_q4, 1)
                    for tb in range(4):
                        v.tensor_mul(comb[:, tb:tb + 1], invsx[:, tb:tb + 1], invsw[:, 0:1])
                        v.tensor_mul(comb[:, 4 + tb:5 + tb], invsx[:, tb:tb + 1],
                                     invsw[:, 1:2])

                @blk.scalar
                def _(s, sp=sp):
                    for tb in range(4):
                        j = sp * 4 + tb
                        s.activation(hlnw[:, :H], stream[tb][:, :], AF.Square,
                                     accum_out=qc[:, 1:2]).then_inc(s_qs, 1)
                        s.wait_ge(s_q4, 2 * j + 1)
                        b.drain(s, "s", s.activation(qc[:, 5:6], qc[:, 4:5],
                                                     AF.Ln, bias=epsln[:, :1]))
                        b.drain(s, "s", s.activation(qc[:, 5:6], qc[:, 5:6],
                                                     AF.Exp, scale=-0.5))
                        b.drain(s, "s", s.activation(qc[:, 6:7], qc[:, 2:3],
                                                     AF.Copy, scale=qc[:, 5:6]))
                        b.drain(s, "s", s.mul(qc[:, 6:7], qc[:, 6:7], -1.0))
                        s.activation(qt[:, :H], stream[tb][:, :], AF.Identity,
                                     scale=qc[:, 5:6], bias=qc[:, 6:7]).then_inc(s_qs, 1)
                        s.wait_ge(s_q4, 2 * j + 2)
                        if tb >= 1:
                            s.wait_ge(s_xw, 16 * (j - 1) + 16)
                        s.copy(qb[:, :H], qi[:, :H]).then_inc(s_qs, 1)

                @blk.sync
                def _(sync, sp=sp, t0=t0):
                    for tb in range(4):
                        j = sp * 4 + tb
                        sync.wait_ge(s_qs, 3 * j + 3)
                        sync.dma_start(xq_sc[t0 + tb * 128:t0 + (tb + 1) * 128, :],
                                       qb[:, :H]).then_inc(s_xw, 16)
                    sync.wait_ge(s_xw, 16 * 4 * (sp + 1))
            b.fence(f"p4b_{sp}")

            with nc.Block(f"p5a_{sp}") as blk:
                @blk.sync
                def _(sync, sp=sp, t0=t0):
                    for hc in range(16):
                        sync.dma_start_transpose(
                            xq_fm[:, hc * 512:(hc + 1) * 512],
                            xq_sc[t0:t0 + 512, hc * 128:(hc + 1) * 128]).then_inc(s_x5, 16)

                @blk.gpsimd
                def _(g, sp=sp):
                    for j in range(128):
                        if j >= 2:
                            g.wait_ge(s_t5, sp * 512 + 4 * (j - 1))
                        it, r = divmod(j, 8)
                        gu, q = divmod(r, 4)
                        g.dma_start(
                            w4b[j % 2][:, :].rearrange("p (k n) -> p k n", n=512),
                            (wq_g if gu == 0 else wq_u)[it, q * 4:(q + 1) * 4]
                            .rearrange("k p n -> p k n"),
                        ).then_inc(s_w52[j % 2], 16)

                @blk.tensor
                def _(t, sp=sp):
                    t.wait_ge(s_x5, 16 * 16 * (sp + 1))
                    for it in range(16):
                        for gu in range(2):
                            pp = pga if gu == 0 else pua
                            if it >= 1:
                                if gu == 0:
                                    t.wait_ge(s_sg5, sp * 64 + 4 * it)
                                else:
                                    t.wait_ge(s_h5, sp * 64 + 4 * it)
                            for q in range(4):
                                j = it * 8 + gu * 4 + q
                                t.wait_ge(s_w52[j % 2], 16 * (sp * 64 + j // 2 + 1))
                                for tb in range(4):
                                    for h4 in range(4):
                                        hc = q * 4 + h4
                                        mm = t.matmul(
                                            pp[tb][:, :],
                                            xq_fm[:, hc * 512 + tb * 128:
                                                  hc * 512 + (tb + 1) * 128],
                                            w4b[j % 2][:, h4 * 512:(h4 + 1) * 512],
                                            start=(hc == 0), stop=(hc == 15))
                                    mm.then_inc(s_t5, 1)

                @blk.scalar
                def _(s, sp=sp):
                    for it in range(16):
                        s.wait_ge(s_t5, sp * 512 + it * 32 + 16)
                        if it >= 1:
                            s.wait_ge(s_h5, sp * 64 + 4 * it)
                        for tb in range(4):
                            if tb >= 2:
                                s.wait_ge(s_h5, sp * 64 + 4 * it + tb - 1)
                            s.activation(sgt[tb % 2][:, :], pga[tb][:, :], AF.Silu,
                                         scale=comb[:, tb:tb + 1]).then_inc(s_sg5, 1)

                @blk.vector
                def _(v, sp=sp):
                    for it in range(16):
                        v.wait_ge(s_t5, sp * 512 + it * 32 + 32)
                        for tb in range(4):
                            v.wait_ge(s_sg5, sp * 64 + 4 * it + tb + 1)
                            v.scalar_tensor_tensor(
                                arena[:, tb * FFI + it * 512:tb * FFI + (it + 1) * 512],
                                pua[tb][:, :], comb[:, 4 + tb:5 + tb], sgt[tb % 2][:, :],
                                ALU.mult, ALU.mult).then_inc(s_h5, 1)
            b.fence(f"p5a_{sp}")

            with nc.Block(f"p5b_{sp}") as blk:
                @blk.scalar
                def _(s, sp=sp):
                    for tb in range(4):
                        j = sp * 4 + tb
                        b.drain(s, "s", s.activation(hlnw[:, :],
                                                     arena[:, tb * FFI:(tb + 1) * FFI],
                                                     AF.Square, accum_out=qc[:, 1:2]))
                        b.drain(s, "s", s.activation(qc[:, 13:14], qc[:, 1:2], AF.Ln,
                                                     scale=1.0 / FFI,
                                                     bias=epsrms[:, :1]))
                        s.activation(qc[:, 13:14], qc[:, 13:14], AF.Exp,
                                     scale=-0.5).then_inc(s_hqs, 1)
                        for hf in range(2):
                            s.wait_ge(s_hq, 2 * j + hf + 1)
                            if not (tb == 0 and hf == 0):
                                s.wait_ge(s_hw, 16 * (2 * j + hf - 1) + 16)
                            s.copy(qb[:, :4096], qi[:, :]).then_inc(s_hqs, 1)

                @blk.vector
                def _(v, sp=sp):
                    for tb in range(4):
                        j = sp * 4 + tb
                        v.wait_ge(s_hqs, 3 * j + 1)
                        v.tensor_mul(hlnw[:, :], arena[:, tb * FFI:(tb + 1) * FFI],
                                     lnw_sb[:, :])
                        b.drain(v, "v", v.reduce_max(qc[:, 10:11], hlnw[:, :],
                                                     axis=AXX,
                                                     apply_absolute_value=True))
                        b.drain(v, "v", v.tensor_mul(qc[:, 11:12], qc[:, 10:11],
                                                     qc[:, 13:14]))
                        b.drain(v, "v", v.tensor_scalar_max(qc[:, 11:12],
                                                            qc[:, 11:12], EPS_Q))
                        b.drain(v, "v", v.reciprocal(qc[:, 12:13], qc[:, 11:12]))
                        b.drain(v, "v", v.tensor_scalar_mul(qc[:, 12:13],
                                                            qc[:, 12:13], 127.0))
                        b.drain(v, "v", v.tensor_mul(qc[:, 12:13], qc[:, 12:13],
                                                     qc[:, 13:14]))
                        v.tensor_scalar_mul(invsh[:, tb:tb + 1], qc[:, 11:12], 1.0 / 127.0)
                        for hf in range(2):
                            if not (tb == 0 and hf == 0):
                                v.wait_ge(s_hqs, 3 * j + 1 + hf)
                            v.tensor_scalar(qt[:, :], hlnw[:, hf * 4096:(hf + 1) * 4096],
                                            qc[:, 12:13], MAGIC, ALU.mult, ALU.add)
                            v.tensor_scalar(qt[:, :], qt[:, :], -MAGIC, None, ALU.add)
                            v.tensor_copy(qi[:, :], qt[:, :]).then_inc(s_hq, 1)
                    for tb in range(4):
                        v.tensor_mul(comb[:, 8 + tb:9 + tb], invsh[:, tb:tb + 1],
                                     invsw[:, 2:3])

                @blk.sync
                def _(sync, sp=sp, t0=t0):
                    for tb in range(4):
                        j = sp * 4 + tb
                        for hf in range(2):
                            sync.wait_ge(s_hqs, 3 * j + 2 + hf)
                            sync.dma_start(
                                bass.AP(hq_sc, (t0 + tb * 128) * FFI + hf * 4096,
                                        [[FFI, 128], [1, 4096]]),
                                qb[:, :4096]).then_inc(s_hw, 16)
                    sync.wait_ge(s_hw, 16 * 8 * (sp + 1))
            b.fence(f"p5b_{sp}")

            with nc.Block(f"p6_{sp}") as blk:
                @blk.sync
                def _(sync, sp=sp, t0=t0):
                    for ic in range(64):
                        sync.dma_start_transpose(
                            arena[:, ic * 512:(ic + 1) * 512],
                            hq_sc[t0:t0 + 512, ic * 128:(ic + 1) * 128]).then_inc(s_h6, 16)
                    for k in range(16):
                        ht, tb = divmod(k, 4)
                        o = sp * 16 + k
                        sync.wait_ge(s_e6, sp * 16 + k + 1)
                        sync.dma_start(
                            bass.AP(y_out, (t0 + tb * 128) * H + ht * 512,
                                    [[H, 128], [1, 512]]),
                            ost[o % 2][:, :]).then_inc(s_ow2[o % 2], 16)

                @blk.gpsimd
                def _(g, sp=sp, t0=t0):
                    for j in range(64):
                        if j >= 2:
                            g.wait_ge(s_t6, sp * 256 + 4 * (j - 1))
                        ht, icq = divmod(j, 16)
                        g.dma_start(
                            w4b[j % 2][:, :].rearrange("p (k n) -> p k n", n=512),
                            wq_d[ht, icq * 4:(icq + 1) * 4].rearrange(
                                "k p n -> p k n")).then_inc(s_w62[j % 2], 16)

                @blk.tensor
                def _(t, sp=sp):
                    t.wait_ge(s_h6, 16 * 64 * (sp + 1))
                    for ht in range(4):
                        if ht >= 1:
                            t.wait_ge(s_e6, sp * 16 + 4 * ht)
                        for icq in range(16):
                            j = ht * 16 + icq
                            t.wait_ge(s_w62[j % 2], 16 * (sp * 32 + j // 2 + 1))
                            for tb in range(4):
                                for i4 in range(4):
                                    ic = icq * 4 + i4
                                    mm = t.matmul(
                                        pga[tb][:, :],
                                        arena[:, ic * 512 + tb * 128:
                                              ic * 512 + (tb + 1) * 128],
                                        w4b[j % 2][:, i4 * 512:(i4 + 1) * 512],
                                        start=(ic == 0), stop=(ic == 63))
                                mm.then_inc(s_t6, 1)

                @blk.vector
                def _(v, sp=sp):
                    for ht in range(4):
                        v.wait_ge(s_t6, sp * 256 + 64 * (ht + 1))
                        for tb in range(4):
                            k = ht * 4 + tb
                            o = sp * 16 + k
                            if o >= 2:
                                v.wait_ge(s_ow2[o % 2], 16 * (o // 2))
                            v.scalar_tensor_tensor(
                                ost[o % 2][:, :], pga[tb][:, :], comb[:, 8 + tb:9 + tb],
                                stream[tb][:, ht * 512:(ht + 1) * 512],
                                ALU.mult, ALU.add).then_inc(s_e6, 1)
            b.fence(f"p6_{sp}")

        with nc.Block("p456_end") as blk:
            @blk.sync
            def _(sync):
                sync.wait_ge(s_ow2[0], 16 * 8 * NSUP)
                sync.wait_ge(s_ow2[1], 16 * 8 * NSUP)


_NC_CACHE = {}


def _get_nc():
    if "nc" not in _NC_CACHE:
        _NC_CACHE["nc"] = build_nc()
    return _NC_CACHE["nc"]


def kernel(L0_attn_x, L0_attn_in_proj_w, L0_attn_conv_w, L0_attn_conv_b,
           L0_attn_dt_bias, L0_attn_A_log, L0_attn_D, L0_attn_norm_w,
           L0_attn_out_proj_w, L0_ffn_w_gate, L0_ffn_w_up, L0_ffn_w_down,
           L0_ffn_ln_weight):
    f32 = np.float32
    bf16 = ml_dtypes.bfloat16
    x = np.asarray(L0_attn_x, dtype=f32)
    W_in = np.asarray(L0_attn_in_proj_w, dtype=f32)

    Wc = W_in[:, INTER:INTER + 4352]
    w_conv = np.ascontiguousarray(
        Wc.reshape(16, 128, 34, 128).transpose(2, 0, 1, 3))
    w_dt = np.ascontiguousarray(W_in[:, 8448:8512].reshape(16, 128, 64))
    w_gate = np.ascontiguousarray(
        W_in[:, :INTER].reshape(16, 128, 8, 512).transpose(2, 0, 1, 3))
    w_out = np.ascontiguousarray(
        np.asarray(L0_attn_out_proj_w, dtype=f32).reshape(32, 128, 4, 512)
        .transpose(2, 0, 1, 3))
    w_gT = np.ascontiguousarray(
        np.asarray(L0_ffn_w_gate, dtype=f32).T.reshape(16, 128, 16, 512)
        .transpose(2, 0, 1, 3))
    w_uT = np.ascontiguousarray(
        np.asarray(L0_ffn_w_up, dtype=f32).T.reshape(16, 128, 16, 512)
        .transpose(2, 0, 1, 3))
    w_dT = np.ascontiguousarray(
        np.asarray(L0_ffn_w_down, dtype=f32).T.reshape(64, 128, 4, 512)
        .transpose(2, 0, 1, 3))
    convw = np.ascontiguousarray(np.asarray(L0_attn_conv_w, dtype=f32)
                                 .reshape(34, 128, KC))
    convb = np.ascontiguousarray(np.asarray(L0_attn_conv_b, dtype=f32)
                                 .reshape(34, 128).T)
    dtbias = np.ascontiguousarray(np.asarray(L0_attn_dt_bias, dtype=f32)
                                  .reshape(64, 1))
    alog = np.ascontiguousarray(np.asarray(L0_attn_A_log, dtype=f32).reshape(64, 1))
    d_rep = np.ascontiguousarray(
        np.repeat(np.asarray(L0_attn_D, dtype=f32), 64).reshape(1, INTER))
    normw = np.ascontiguousarray(np.asarray(L0_attn_norm_w, dtype=f32)
                                 .reshape(32, 128).T)
    lnww = np.ascontiguousarray(np.asarray(L0_ffn_ln_weight, dtype=f32)
                                .reshape(1, FFI))
    selb = np.zeros((64, 64 * 128), f32)
    for h in range(64):
        selb[h, h * 128:(h + 1) * 128] = 1.0
    identf_a = np.eye(128, dtype=f32)
    identb_a = np.eye(128, dtype=f32).astype(bf16)
    onesr_a = np.ones((1, 128), f32)
    m01 = np.zeros((2, 128, CHUNK), f32)
    for s2 in range(2):
        for p in range(128):
            m01[s2, p, s2 * 128 + p:] = 1.0
    m01 = m01.astype(bf16)

    shared = dict(w_conv=w_conv, w_dt=w_dt, w_gate=w_gate, w_out=w_out,
                  w_gT=w_gT, w_uT=w_uT, w_dT=w_dT, convw=convw, convb=convb,
                  dtbias=dtbias, alog=alog, d_rep=d_rep, normw=normw, lnww=lnww,
                  selbank=selb, identf=identf_a, identb=identb_a, onesr=onesr_a,
                  mask01=m01)

    in_maps = []
    for core in range(8):
        bb, hh = divmod(core, 2)
        xw = np.zeros((WIN, H), f32)
        xw[MAIN:] = x[bb, hh * MAIN:(hh + 1) * MAIN]
        if hh == 1:
            xw[:MAIN] = x[bb, :MAIN]
        m = dict(shared)
        m["xwin"] = xw
        m["maskst"] = np.array([[1.0 if hh == 1 else 0.0]], f32)
        in_maps.append(m)

    nc = _get_nc()
    res = run_bass_kernel_spmd(nc, in_maps, core_ids=list(range(8)))
    out = np.empty((4, 4096, H), np.float32)
    for core in range(8):
        bb, hh = divmod(core, 2)
        out[bb, hh * MAIN:(hh + 1) * MAIN] = res.results[core]["y_out"]
    return out


# version 06
# speedup vs baseline: 1.4511x; 1.0773x over previous
import numpy as np
import ml_dtypes
from contextlib import ExitStack

import concourse.bass as bass
import concourse.mybir as mybir
from concourse.bass_utils import run_bass_kernel_spmd

F32 = mybir.dt.float32
BF16 = mybir.dt.bfloat16
I32 = mybir.dt.int32
AF = mybir.ActivationFunctionType
ALU = mybir.AluOpType
AXX = mybir.AxisListType.X

MAGIC = 12582912.0
H = 2048
WIN = 4096
MAIN = 2048
INTER = 4096
KC = 4
CHUNK = 256
NCH = 16
FFI = 8192
EPS_LN = 1e-5
EPS_MIX = 1e-5
EPS_RMS = 1e-6
EPS_Q = 1e-5
NSUP = 4


class Bld:
    def __init__(self, nc, ctx):
        self.nc, self.ctx = nc, ctx
        self.fsem = None
        self.fn = 0

    def drain(self, eng, key, instr):
        eng.drain()

    def sem(self, name):
        return self.ctx.enter_context(self.nc.semaphore(name))

    def sb(self, name, shape, dt, st=None):
        return (st or self.ctx).enter_context(self.nc.sbuf_tensor(name, shape, dt))

    def ps(self, name, shape, st=None):
        return (st or self.ctx).enter_context(self.nc.psum_tensor(name, shape, F32))

    def fence(self, tag):
        nc = self.nc
        if self.fsem is None:
            self.fsem = self.sem("fence")
        self.fn += 5
        s, n = self.fsem, self.fn
        with nc.Block("f_" + tag) as blk:
            for setter in (blk.sync, blk.tensor):
                def f(e, s=s, n=n):
                    e.nop().then_inc(s, 1)
                    e.wait_ge(s, n)
                setter(f)

            for setter in (blk.vector, blk.scalar, blk.gpsimd):
                def f(e, s=s, n=n):
                    e.drain().then_inc(s, 1)
                    e.wait_ge(s, n)
                setter(f)


def build_nc(limit=99, sub=9, dump=()):
    nc = bass.Bass("TRN2", target_bir_lowering=False, detect_race_conditions=False)
    _dk = lambda n: dict(kind="ExternalOutput") if n in dump else {}

    xwin = nc.dram_tensor("xwin", [WIN, H], F32, kind="ExternalInput")
    w_conv = nc.dram_tensor("w_conv", [34, 16, 128, 128], F32, kind="ExternalInput")
    w_dt = nc.dram_tensor("w_dt", [16, 128, 64], F32, kind="ExternalInput")
    w_gate = nc.dram_tensor("w_gate", [8, 16, 128, 512], F32, kind="ExternalInput")
    w_out = nc.dram_tensor("w_out", [4, 32, 128, 512], F32, kind="ExternalInput")
    w_gT = nc.dram_tensor("w_gT", [16, 16, 128, 512], F32, kind="ExternalInput")
    w_uT = nc.dram_tensor("w_uT", [16, 16, 128, 512], F32, kind="ExternalInput")
    w_dT = nc.dram_tensor("w_dT", [4, 64, 128, 512], F32, kind="ExternalInput")
    convw = nc.dram_tensor("convw", [34, 128, KC], F32, kind="ExternalInput")
    convb = nc.dram_tensor("convb", [128, 34], F32, kind="ExternalInput")
    dtbias = nc.dram_tensor("dtbias", [64, 1], F32, kind="ExternalInput")
    alog = nc.dram_tensor("alog", [64, 1], F32, kind="ExternalInput")
    d_rep = nc.dram_tensor("d_rep", [1, INTER], F32, kind="ExternalInput")
    normw = nc.dram_tensor("normw", [128, 32], F32, kind="ExternalInput")
    lnww = nc.dram_tensor("lnww", [1, FFI], F32, kind="ExternalInput")
    selbank = nc.dram_tensor("selbank", [64, 64 * 128], F32, kind="ExternalInput")
    identf = nc.dram_tensor("identf", [128, 128], F32, kind="ExternalInput")
    identb = nc.dram_tensor("identb", [128, 128], BF16, kind="ExternalInput")
    onesr = nc.dram_tensor("onesr", [1, 128], F32, kind="ExternalInput")
    mask01 = nc.dram_tensor("mask01", [2, 128, CHUNK], BF16, kind="ExternalInput")
    maskst = nc.dram_tensor("maskst", [1, 1], F32, kind="ExternalInput")

    y_out = nc.dram_tensor("y_out", [MAIN, H], F32, kind="ExternalOutput")

    u1_sc = nc.dram_tensor("u1_sc", [WIN, H], BF16, **_dk("u1_sc"))
    conv_sc = nc.dram_tensor("conv_sc", [34, 128, WIN + 5], BF16, **_dk("conv_sc"))
    gate_sc = nc.dram_tensor("gate_sc", [MAIN, INTER], BF16, **_dk("gate_sc"))
    xs_sc = nc.dram_tensor("xs_sc", [32, 128, WIN], BF16, **_dk("xs_sc"))
    y_sc = nc.dram_tensor("y_sc", [MAIN, INTER], BF16, **_dk("y_sc"))
    xq_sc = nc.dram_tensor("xq_sc", [MAIN, H], BF16, **_dk("xq_sc"))
    hq_sc = nc.dram_tensor("hq_sc", [MAIN, FFI], BF16, **_dk("hq_sc"))
    wq_g = nc.dram_tensor("wq_g", [16, 16, 128, 512], BF16, **_dk("wq_g"))
    wq_u = nc.dram_tensor("wq_u", [16, 16, 128, 512], BF16, **_dk("wq_u"))
    wq_d = nc.dram_tensor("wq_d", [4, 64, 128, 512], BF16, **_dk("wq_d"))
    wout_sc = nc.dram_tensor("wout_sc", [4, 32, 128, 512], BF16, **_dk("wout_sc"))

    with ExitStack() as ctx:
        b = Bld(nc, ctx)

        onesr_sb = b.sb("onesr_sb", [1, 128], F32)
        onescol = b.sb("onescol", [128, 1], F32)
        epsln = b.sb("epsln", [128, 1], F32)
        epsrms = b.sb("epsrms", [128, 1], F32)
        epsq = b.sb("epsq", [128, 1], F32)
        invsw = b.sb("invsw", [128, 3], F32)

        p0 = b.sem("p0")
        with nc.Block("p0") as blk:
            @blk.sync
            def _(sync):
                sync.dma_start(onesr_sb[:, :], onesr[:, :]).then_inc(p0, 16)
                sync.wait_ge(p0, 17)

            @blk.gpsimd
            def _(g):
                g.memset(onescol[:, :], 1.0)
                g.memset(epsln[:, :], EPS_LN)
                g.memset(epsrms[:, :], EPS_RMS)
                g.memset(epsq[:, :], EPS_Q).then_inc(p0, 1)
        b.fence("p0")

        if limit < 2:
            return nc
        with ExitStack() as st:
            xt = [b.sb(f"l1x{i}", [128, H], F32, st) for i in range(2)]
            sq = b.sb("l1sq", [128, H], F32, st)
            u1t = [b.sb(f"l1u{i}", [128, H], BF16, st) for i in range(2)]
            cols = b.sb("l1c", [128, 8], F32, st)
            ld2 = [b.sem(f"l1_ld{i}") for i in range(2)]
            wr2 = [b.sem(f"l1_wr{i}") for i in range(2)]
            vd, sd, ud = (b.sem(f"l1_{n}") for n in "bcd")
            NT = WIN // 128
            with nc.Block("ln1") as blk:
                @blk.sync
                def _(sync):
                    for i in range(NT):
                        if i >= 2:
                            sync.wait_ge(ud, i - 1)
                        sync.dma_start(xt[i % 2][:, :],
                                       xwin[i * 128:(i + 1) * 128, :]).then_inc(ld2[i % 2], 16)
                        if i >= 1:
                            sync.wait_ge(ud, i)
                            sync.dma_start(u1_sc[(i - 1) * 128:i * 128, :],
                                           u1t[(i - 1) % 2][:, :]).then_inc(wr2[(i - 1) % 2], 16)
                    sync.wait_ge(ud, NT)
                    sync.dma_start(u1_sc[(NT - 1) * 128:NT * 128, :],
                                   u1t[(NT - 1) % 2][:, :]).then_inc(wr2[(NT - 1) % 2], 16)
                    sync.wait_ge(wr2[0], 16 * (NT // 2))
                    sync.wait_ge(wr2[1], 16 * (NT // 2))

                @blk.vector
                def _(v):
                    for i in range(NT):
                        v.wait_ge(ld2[i % 2], 16 * (i // 2 + 1))
                        b.drain(v, "v", v.reduce_sum(cols[:, 0:1], xt[i % 2][:, :],
                                                     axis=AXX))
                        v.wait_ge(sd, i + 1)
                        v.tensor_scalar_mul(cols[:, 2:3], cols[:, 0:1], 1.0 / H)
                        b.drain(v, "v", v.tensor_scalar_mul(cols[:, 3:4],
                                                            cols[:, 1:2], 1.0 / H))
                        b.drain(v, "v", v.tensor_mul(cols[:, 4:5], cols[:, 2:3],
                                                     cols[:, 2:3]))
                        v.tensor_sub(cols[:, 4:5], cols[:, 3:4], cols[:, 4:5]).then_inc(vd, 1)

                @blk.scalar
                def _(s):
                    for i in range(NT):
                        s.wait_ge(ld2[i % 2], 16 * (i // 2 + 1))
                        s.activation(sq[:, :], xt[i % 2][:, :], AF.Square,
                                     accum_out=cols[:, 1:2]).then_inc(sd, 1)
                        s.wait_ge(vd, i + 1)
                        if i >= 2:
                            s.wait_ge(wr2[i % 2], 16 * (i // 2))
                        b.drain(s, "s", s.activation(cols[:, 5:6], cols[:, 4:5],
                                                     AF.Ln, bias=epsln[:, :1]))
                        b.drain(s, "s", s.activation(cols[:, 5:6], cols[:, 5:6],
                                                     AF.Exp, scale=-0.5))
                        b.drain(s, "s", s.activation(cols[:, 6:7], cols[:, 2:3],
                                                     AF.Copy, scale=cols[:, 5:6]))
                        b.drain(s, "s", s.mul(cols[:, 6:7], cols[:, 6:7], -1.0))
                        s.activation(u1t[i % 2][:, :], xt[i % 2][:, :], AF.Identity,
                                     scale=cols[:, 5:6], bias=cols[:, 6:7]).then_inc(ud, 1)
        b.fence("ln1")

        if limit < 3:
            return nc
        with ExitStack() as st:
            wt = [b.sb(f"pww{i}", [128, 2048], F32, st) for i in range(2)]
            wb = [b.sb(f"pwb{i}", [128, 2048], BF16, st) for i in range(2)]
            ta = b.sb("pwta", [128, 2048], BF16, st)
            tb2 = b.sb("pwtb", [128, 2048], BF16, st)
            acc = b.sb("pwacc", [128, 64], F32, st)
            scal = b.sb("pwscal", [128, 4], F32, st)
            normw_sb = b.sb("normw_sb", [128, 32], F32, st)
            rowp = b.ps("pwrow", [1, 8], st)
            colp = b.ps("pwcol", [128, 1], st)

            ldn = b.sem("pw_ldn")
            ld2 = [b.sem(f"pw_ld{i}") for i in range(2)]
            wr2 = [b.sem(f"pw_wr{i}") for i in range(2)]
            qd = b.sem("pw_qd")
            NWT = 128
            with nc.Block("pw_out") as blk:
                @blk.sync
                def _(sync):
                    sync.dma_start(normw_sb[:, :], normw[:, :]).then_inc(ldn, 16)
                    for i in range(NWT):
                        ht, yc = divmod(i, 32)
                        if i >= 2:
                            sync.wait_ge(qd, i - 1)
                        sync.dma_start(wt[i % 2][:, :512],
                                       w_out[ht, yc]).then_inc(ld2[i % 2], 16)

                @blk.vector
                def _(v):
                    v.wait_ge(ldn, 16)
                    for i in range(NWT):
                        ht, yc = divmod(i, 32)
                        v.wait_ge(ld2[i % 2], 16 * (i // 2 + 1))
                        if i >= 2:
                            v.wait_ge(wr2[i % 2], 16 * (i // 2))
                        v.tensor_scalar_mul(wb[i % 2][:, :512], wt[i % 2][:, :512],
                                            normw_sb[:, yc:yc + 1]).then_inc(qd, 1)

                @blk.gpsimd
                def _(g):
                    for i in range(NWT):
                        ht, yc = divmod(i, 32)
                        g.wait_ge(qd, i + 1)
                        g.dma_start(wout_sc[ht, yc],
                                    wb[i % 2][:, :512]).then_inc(wr2[i % 2], 16)
                    g.wait_ge(wr2[0], 16 * (NWT // 2))
                    g.wait_ge(wr2[1], 16 * (NWT // 2))
            b.fence("pw_out")

            if limit < 4:
                return nc
            lda2 = [b.sem(f"pwa_l{i}") for i in range(2)]
            aba = b.sem("pwa_a")
            ldb2 = [b.sem(f"pwb_l{i}") for i in range(2)]
            wrb2 = [b.sem(f"pwb_w{i}") for i in range(2)]
            qdb = b.sem("pwb_q")
            for wi, (wsrc, wdst, nblk) in enumerate(
                    ((w_gT, wq_g, 16), (w_uT, wq_u, 16), (w_dT, wq_d, 64))):
                NA = 64
                BL, BA = wi * 4096, wi * 69

                def idx2(i, nblk=nblk):
                    return divmod(i, nblk)
                with nc.Block(f"pw_a{wi}") as blk:
                    @blk.gpsimd
                    def _(g, wsrc=wsrc, idx2=idx2, BL=BL, BA=BA, wi=wi):
                        for i in range(NA):
                            if i >= 2:
                                g.wait_ge(aba, BA + i - 1)
                            gg = wi * 64 + i
                            for j in range(4):
                                a0, c0 = idx2(4 * i + j)
                                g.dma_start(wb[i % 2][:, 512 * j:512 * (j + 1)],
                                            wsrc[a0, c0]).then_inc(lda2[gg % 2], 16)

                    @blk.scalar
                    def _(s, BL=BL, BA=BA, wi=wi):
                        for i in range(NA):
                            gg = wi * 64 + i
                            s.wait_ge(lda2[gg % 2], 64 * (gg // 2 + 1))
                            s.activation(ta[:, :], wb[i % 2][:, :], AF.Abs,
                                         accum_out=acc[:, i:i + 1]).then_inc(aba, 1)

                    @blk.vector
                    def _(v, BA=BA):
                        v.wait_ge(aba, BA + NA)
                        v.reduce_sum(scal[:, 0:1], acc[:, :NA], axis=AXX).then_inc(aba, 1)

                    @blk.tensor
                    def _(t, BA=BA):
                        t.wait_ge(aba, BA + NA + 1)
                        t.matmul(rowp[:1, 0:1], scal[:, 0:1], onescol[:, :1],
                                 start=True, stop=True).then_inc(aba, 1)

                    @blk.scalar
                    def _(s, BA=BA):
                        s.wait_ge(aba, BA + NA + 2)
                        b.drain(s, "s", s.copy(scal[:1, 1:2], rowp[:1, 0:1]))
                        s.mul(scal[:1, 1:2], scal[:1, 1:2],
                              1.0 / (256 * 128 * 512)).then_inc(aba, 1)

                    @blk.tensor
                    def _(t, BA=BA):
                        t.wait_ge(aba, BA + NA + 3)
                        t.matmul(colp[:, 0:1], onesr_sb[:1, :], scal[:1, 1:2],
                                 start=True, stop=True).then_inc(aba, 1)

                    @blk.vector
                    def _(v, wi=wi, BA=BA):
                        v.wait_ge(aba, BA + NA + 4)
                        b.drain(v, "v", v.tensor_copy(invsw[:, wi:wi + 1],
                                                      colp[:, 0:1]))
                        v.tensor_scalar_mul(scal[:, 2:3], invsw[:, wi:wi + 1],
                                            0.5).then_inc(aba, 1)
                b.fence(f"pw_a{wi}")

                with nc.Block(f"pw_b{wi}") as blk:
                    @blk.sync
                    def _(sync, wsrc=wsrc, idx2=idx2, BL=BL, wi=wi):
                        for i in range(NA):
                            if i >= 2:
                                sync.wait_ge(qdb, BL // 64 + i - 1)
                            gg = wi * 64 + i
                            for j in range(4):
                                a0, c0 = idx2(4 * i + j)
                                sync.dma_start(wt[i % 2][:, 512 * j:512 * (j + 1)],
                                               wsrc[a0, c0]).then_inc(ldb2[gg % 2], 16)

                    @blk.vector
                    def _(v, BL=BL, wi=wi, BA=BA):
                        v.wait_ge(aba, BA + NA + 5)
                        for i in range(NA):
                            gg = wi * 64 + i
                            v.wait_ge(ldb2[gg % 2], 64 * (gg // 2 + 1))
                            if i >= 2:
                                v.wait_ge(wrb2[gg % 2], 64 * (gg // 2))
                            v.tensor_scalar(ta[:, :], wt[i % 2][:, :], scal[:, 2:3],
                                            None, ALU.is_gt)
                            v.tensor_scalar(tb2[:, :], wt[i % 2][:, :], -1.0,
                                            scal[:, 2:3], ALU.mult, ALU.is_gt)
                            v.tensor_sub(wb[i % 2][:, :], ta[:, :], tb2[:, :]).then_inc(qdb, 1)

                    @blk.gpsimd
                    def _(g, wdst=wdst, idx2=idx2, BL=BL, wi=wi):
                        for i in range(NA):
                            g.wait_ge(qdb, BL // 64 + i + 1)
                            gg = wi * 64 + i
                            for j in range(4):
                                a0, c0 = idx2(4 * i + j)
                                g.dma_start(wdst[a0, c0],
                                            wb[i % 2][:, 512 * j:512 * (j + 1)]).then_inc(wrb2[gg % 2], 16)
                        g.wait_ge(wrb2[0], 64 * 32 * (wi + 1))
                        g.wait_ge(wrb2[1], 64 * 32 * (wi + 1))
                b.fence(f"pw_b{wi}")

        if limit < 7:
            return nc
        with ExitStack() as mid:
            dt_fm = b.sb("dt_fm", [64, WIN], F32, mid)
            Bfm = b.sb("Bfm", [128, WIN], BF16, mid)
            Cfm = b.sb("Cfm", [128, WIN], BF16, mid)

            with ExitStack() as st:
                u1f = b.sb("u1f", [128, 16 * MAIN], BF16, st)
                wbuf = [b.sb(f"p1w{i}", [128, 2048], BF16, st) for i in range(2)]
                stg = [b.sb(f"p1s{i}", [128, 512], BF16, st) for i in range(2)]
                gwb = b.sb("p1gw", [128, 16 * 512], BF16, st)
                pss = [b.ps(f"p1p{i}", [128, 512], st) for i in range(2)]
                uld = b.sem("p1u")
                wl2 = [b.sem(f"p1wl{i}") for i in range(2)]
                wr2 = [b.sem(f"p1wr{i}") for i in range(2)]
                md, ev = b.sem("p1md"), b.sem("p1ev")
                for half in range(2):
                    BU, BW, BM = half * 256, half * 560, half * 140
                    with nc.Block(f"p1u_{half}") as blk:
                        @blk.sync
                        def _(sync, half=half, BU=BU):
                            for hc in range(16):
                                sync.dma_start_transpose(
                                    u1f[:, hc * MAIN:(hc + 1) * MAIN],
                                    u1_sc[half * MAIN:(half + 1) * MAIN,
                                          hc * 128:(hc + 1) * 128]).then_inc(uld, 16)
                            sync.wait_ge(uld, BU + 256)
                    b.fence(f"p1u_{half}")
                    if limit == 7 and sub < 1 + 2 * half:
                        return nc

                    NCC = 35
                    with nc.Block(f"p1a_{half}") as blk:
                        @blk.gpsimd
                        def _(g, BM=BM, half=half):
                            for cc in range(NCC):
                                gcc = half * NCC + cc
                                if cc >= 2:
                                    g.wait_ge(md, BM + 4 * (cc - 1))
                                if cc < 34:
                                    g.dma_start(
                                        wbuf[cc % 2][:, :].rearrange("p (k m) -> p k m", m=128),
                                        w_conv[cc].rearrange("k p m -> p k m"),
                                    ).then_inc(wl2[gcc % 2], 16)
                                else:
                                    g.dma_start(
                                        wbuf[cc % 2][:, :16 * 64].rearrange("p (k m) -> p k m", m=64),
                                        w_dt.rearrange("k p m -> p k m"),
                                    ).then_inc(wl2[gcc % 2], 16)

                        @blk.tensor
                        def _(t, BW=BW, BM=BM, half=half):
                            for cc in range(NCC):
                                gcc = half * NCC + cc
                                t.wait_ge(wl2[gcc % 2], 16 * (gcc // 2 + 1))
                                w = 128 if cc < 34 else 64
                                for tb in range(4):
                                    i = cc * 4 + tb
                                    if i >= 2:
                                        t.wait_ge(ev, BM + i - 1)
                                    for hc in range(16):
                                        mm = t.matmul(
                                            pss[i % 2][:w, :],
                                            wbuf[cc % 2][:, hc * w:(hc + 1) * w],
                                            u1f[:, hc * MAIN + tb * 512:hc * MAIN + (tb + 1) * 512],
                                            start=(hc == 0), stop=(hc == 15))
                                    mm.then_inc(md, 1)

                        @blk.vector
                        def _(v, half=half, BM=BM):
                            for cc in range(NCC):
                                for tb in range(4):
                                    i = cc * 4 + tb
                                    v.wait_ge(md, BM + i + 1)
                                    if cc < 34:
                                        j = half * 136 + i
                                        if j >= 2:
                                            v.wait_ge(wr2[j % 2], 16 * (j // 2))
                                        v.tensor_copy(stg[j % 2][:, :],
                                                      pss[i % 2][:, :]).then_inc(ev, 1)
                                    else:
                                        v.tensor_copy(
                                            dt_fm[:, half * MAIN + tb * 512:half * MAIN + (tb + 1) * 512],
                                            pss[i % 2][:64, :]).then_inc(ev, 1)

                        @blk.sync
                        def _(sync, half=half, BM=BM):
                            for cc in range(NCC):
                                for tb in range(4):
                                    i = cc * 4 + tb
                                    sync.wait_ge(ev, BM + i + 1)
                                    if cc < 34:
                                        j = half * 136 + i
                                        sync.dma_start(
                                            conv_sc[cc, :, 4 + half * MAIN + tb * 512:
                                                    4 + half * MAIN + (tb + 1) * 512],
                                            stg[j % 2][:, :]).then_inc(wr2[j % 2], 16)
                            sync.wait_ge(wr2[0], 16 * ((half * 136 + 136 + 1) // 2))
                            sync.wait_ge(wr2[1], 16 * ((half * 136 + 136) // 2))
                    b.fence(f"p1a_{half}")
                    if limit == 7 and sub < 2 + 2 * half:
                        return nc

                    if half == 1:
                        gl, gm, ge = (b.sem(f"p1g{n}") for n in "lme")
                        gws = [b.sem(f"p1gw{i}") for i in range(2)]
                        with nc.Block("p1b") as blk:
                            @blk.gpsimd
                            def _(g):
                                for cb in range(8):
                                    if cb >= 1:
                                        g.wait_ge(gm, 16 * cb)
                                    for hc in range(16):
                                        g.dma_start(gwb[:, hc * 512:(hc + 1) * 512],
                                                    w_gate[cb, hc]).then_inc(gl, 16)

                            @blk.tensor
                            def _(t):
                                for cb in range(8):
                                    t.wait_ge(gl, 256 * (cb + 1))
                                    for tb in range(16):
                                        i = cb * 16 + tb
                                        if i >= 2:
                                            t.wait_ge(ge, i - 1)
                                        for hc in range(16):
                                            mm = t.matmul(
                                                pss[i % 2][:, :],
                                                u1f[:, hc * MAIN + tb * 128:hc * MAIN + (tb + 1) * 128],
                                                gwb[:, hc * 512:(hc + 1) * 512],
                                                start=(hc == 0), stop=(hc == 15))
                                        mm.then_inc(gm, 1)

                            @blk.vector
                            def _(v):
                                for i in range(128):
                                    v.wait_ge(gm, i + 1)
                                    if i >= 2:
                                        v.wait_ge(gws[i % 2], 16 * (i // 2))
                                    v.tensor_copy(stg[i % 2][:, :],
                                                  pss[i % 2][:, :]).then_inc(ge, 1)

                            @blk.sync
                            def _(sync):
                                for cb in range(8):
                                    for tb in range(16):
                                        i = cb * 16 + tb
                                        sync.wait_ge(ge, i + 1)
                                        sync.dma_start(
                                            bass.AP(gate_sc, tb * 128 * INTER + cb * 512,
                                                    [[INTER, 128], [1, 512]]),
                                            stg[i % 2][:, :]).then_inc(gws[i % 2], 16)
                                sync.wait_ge(gws[0], 16 * 64)
                                sync.wait_ge(gws[1], 16 * 64)
                        b.fence("p1b")

            if limit < 8:
                return nc
            with ExitStack() as st:
                convw_sb = b.sb("convw_sb", [128, 34 * KC], F32, st)
                convb_sb = b.sb("convb_sb", [128, 34], F32, st)
                cin = [b.sb(f"p2i{i}", [128, 34 * 516], BF16, st) for i in range(2)]
                cacc = [b.sb(f"p2a{i}", [128, 512], BF16, st) for i in range(2)]
                cout = [b.sb(f"p2o{i}", [128, 32 * 512], BF16, st) for i in range(2)]
                zl = b.sb("p2z", [128, 4 * 34], BF16, st)
                ld2 = [b.sem(f"p2ld{i}") for i in range(2)]
                wr2 = [b.sem(f"p2wr{i}") for i in range(2)]
                cv, sl, zz, zzg = (b.sem(f"p2_{n}") for n in "bcef")
                NS = 8
                with nc.Block("p2") as blk:
                    @blk.gpsimd
                    def _(g):
                        b.drain(g, "g", g.memset(zl[:, :], 0.0))
                        g.dma_start(
                            conv_sc.rearrange("c p t -> p c t")[:, :, 0:4],
                            zl[:, :].rearrange("p (c k) -> p c k", k=4),
                        ).then_inc(zzg, 16)
                        g.dma_start(
                            conv_sc.rearrange("c p t -> p c t")[:, :, WIN + 4:WIN + 5],
                            zl[:, :34].rearrange("p (c k) -> p c k", k=1),
                            allow_slow_non_contiguous=True,
                        ).then_inc(zzg, 16)
                        for s_ in range(NS):
                            g.wait_ge(sl, s_ * 34 + 32)
                            g.dma_start(
                                xs_sc.rearrange("c p t -> p c t")[:, :, s_ * 512:(s_ + 1) * 512],
                                cout[s_ % 2][:, :].rearrange("p (c t) -> p c t", t=512),
                            ).then_inc(wr2[s_ % 2], 16)
                        g.wait_ge(wr2[0], 16 * (NS // 2))
                        g.wait_ge(wr2[1], 16 * (NS // 2))
                        g.wait_ge(sl, NS * 34)

                    @blk.sync
                    def _(sync):
                        sync.dma_start(convw_sb[:, :].rearrange("p (c k) -> p c k", k=KC),
                                       convw.rearrange("c p k -> p c k")).then_inc(zz, 16)
                        sync.dma_start(convb_sb[:, :], convb[:, :]).then_inc(zz, 16)
                        sync.wait_ge(zz, 32)
                        sync.wait_ge(zzg, 32)
                        for s_ in range(NS):
                            if s_ >= 2:
                                sync.wait_ge(cv, 34 * (s_ - 1))
                            sync.dma_start(
                                cin[s_ % 2][:, :].rearrange("p (c t) -> p c t", t=516),
                                conv_sc.rearrange("c p t -> p c t")[:, :, s_ * 512 + 1:s_ * 512 + 517],
                            ).then_inc(ld2[s_ % 2], 16)

                    @blk.vector
                    def _(v):
                        for s_ in range(NS):
                            v.wait_ge(ld2[s_ % 2], 16 * (s_ // 2 + 1))
                            for cc in range(34):
                                i = s_ * 34 + cc
                                if i >= 2:
                                    v.wait_ge(sl, i - 1)
                                base = cc * 516
                                src = cin[s_ % 2]
                                v.tensor_scalar_mul(cacc[i % 2][:, :], src[:, base:base + 512],
                                                    convw_sb[:, cc * KC:cc * KC + 1])
                                for k in range(1, KC):
                                    st3 = v.scalar_tensor_tensor(
                                        cacc[i % 2][:, :], src[:, base + k:base + k + 512],
                                        convw_sb[:, cc * KC + k:cc * KC + k + 1],
                                        cacc[i % 2][:, :], ALU.mult, ALU.add)
                                st3.then_inc(cv, 1)

                    @blk.scalar
                    def _(s):
                        for s_ in range(NS):
                            for cc in range(34):
                                i = s_ * 34 + cc
                                s.wait_ge(cv, i + 1)
                                if cc < 32:
                                    if s_ >= 2:
                                        s.wait_ge(wr2[s_ % 2], 16 * (s_ // 2))
                                    dst = cout[s_ % 2][:, cc * 512:(cc + 1) * 512]
                                elif cc == 32:
                                    dst = Bfm[:, s_ * 512:(s_ + 1) * 512]
                                else:
                                    dst = Cfm[:, s_ * 512:(s_ + 1) * 512]
                                s.activation(dst, cacc[i % 2][:, :], AF.Silu,
                                             bias=convb_sb[:, cc:cc + 1]).then_inc(sl, 1)
            b.fence("p2")

            if limit < 9:
                return nc
            _emit_p3(nc, b, dict(
                dt_fm=dt_fm, Bfm=Bfm, Cfm=Cfm, selbank=selbank, identf=identf,
                identb=identb, mask01=mask01, d_rep=d_rep, dtbias=dtbias, alog=alog,
                maskst=maskst, xs_sc=xs_sc, gate_sc=gate_sc, y_sc=y_sc,
                onesr_sb=onesr_sb, onescol=onescol, epsln=epsln))
            b.fence("p3")

        if limit < 10:
            return nc
        _emit_p456(nc, b, dict(
            xwin=xwin, y_sc=y_sc, xq_sc=xq_sc, hq_sc=hq_sc, y_out=y_out,
            wout_sc=wout_sc, wq_g=wq_g, wq_u=wq_u, wq_d=wq_d, lnww=lnww,
            invsw=invsw, epsln=epsln, epsrms=epsrms, epsq=epsq))
        b.fence("end")

    return nc


def _emit_p3(nc, b, d):
    st = ExitStack()
    with st:
        sel_sb = b.sb("sel_sb", [64, 32 * 128], F32, st)
        identf_sb = b.sb("identf_sb", [128, 128], F32, st)
        identb_sb = b.sb("identb_sb", [128, 128], BF16, st)
        mask01_sb = b.sb("mask01_sb", [128, 2 * CHUNK], BF16, st)
        drep_sb = b.sb("drep_sb", [128, INTER], BF16, st)
        dtb_sb = b.sb("dtb_sb", [64, 1], F32, st)
        negA_sb = b.sb("negA_sb", [64, 1], F32, st)
        maskst_sb = b.sb("maskst_sb", [128, 1], F32, st)
        Ssb = b.sb("Ssb", [128, INTER], F32, st)
        Sbf = b.sb("Sbf", [128, INTER], BF16, st)
        dtp_fm = b.sb("dtp_fm", [64, CHUNK], F32, st)
        tmp_fm = b.sb("tmp_fm", [64, CHUNK], F32, st)
        adt_fm = b.sb("adt_fm", [64, CHUNK], F32, st)
        cumA_fm = b.sb("cumA_fm", [64, CHUNK], F32, st)
        dcy_fm = b.sb("dcy_fm", [64, CHUNK], F32, st)
        dpd_fm = b.sb("dpd_fm", [64, CHUNK], F32, st)
        gcol = b.sb("gcol", [64, 1], F32, st)
        grow = b.sb("grow", [1, 64], F32, st)
        gbc = b.sb("gbc", [128, 64], F32, st)
        dtp_tm = [b.sb(f"dtpt{i}", [128, 64], F32, st) for i in range(2)]
        ncum_tm = [b.sb(f"ncumt{i}", [128, 64], F32, st) for i in range(2)]
        ecum_tm = [b.sb(f"ecumt{i}", [128, 64], F32, st) for i in range(2)]
        dpd_tm = [b.sb(f"dpdt{i}", [128, 64], F32, st) for i in range(2)]
        B_tm = [b.sb(f"Btm{i}", [128, 128], BF16, st) for i in range(2)]
        xs_tm = [b.sb(f"xstm{i}", [128, INTER], BF16, st) for i in range(2)]
        xdt_tm = [b.sb(f"xdtm{i}", [128, INTER], BF16, st) for i in range(2)]
        xdd_tm = [b.sb(f"xddm{i}", [128, INTER], BF16, st) for i in range(2)]
        xsD_tm = [b.sb(f"xsDm{i}", [128, INTER], BF16, st) for i in range(2)]
        scm = [b.sb(f"scm{i}", [128, CHUNK], BF16, st) for i in range(2)]
        Et = [[b.sb(f"Et{p}_{s2}", [128, CHUNK], F32, st) for s2 in range(2)]
              for p in range(2)]
        sex = [[b.sb(f"sex{p}_{s2}", [128, CHUNK], BF16, st) for s2 in range(2)]
               for p in range(2)]
        scf = [[b.sb(f"scf{p}_{s2}", [128, CHUNK], BF16, st) for s2 in range(2)]
               for p in range(2)]
        ytmp = [[b.sb(f"ytp{p}_{lt}", [128, 64], F32, st) for lt in range(2)]
                for p in range(2)]
        Y_sb = [b.sb(f"Ysb{lt}", [128, INTER], BF16, st) for lt in range(2)]
        gate_tm = [b.sb(f"gtm{i}", [128, INTER], BF16, st) for i in range(2)]
        ysq = b.sb("ysq", [128, INTER], BF16, st)
        rc3 = b.sb("rc3", [128, 8], F32, st)
        ps_hp = [b.ps(f"pshp{i}", [128, 320], st) for i in range(2)]
        ps_bc = [b.ps(f"psbc{i}", [128, 256], st) for i in range(2)]
        ps_t = b.ps("ps_t", [128, 512], st)
        ps_sc = b.ps("ps_sc", [128, 512], st)

        def hp_st(par):
            return ps_hp[par][:, 0:64]

        def hp_yd(par, lt):
            return ps_hp[par][:, 64 + lt * 64:128 + lt * 64]

        def hp_yo(par, lt):
            return ps_hp[par][:, 192 + lt * 64:256 + lt * 64]

        s_p0 = b.sem("p3_p0")
        s_p0g = b.sem("p3_p0g")
        s_ldx = b.sem("p3_ldx")
        s_ldb = b.sem("p3_ldb")
        s_ldg = b.sem("p3_ldg")
        s_pre = b.sem("p3_pre")
        s_prs = b.sem("p3_prs")
        s_prt = b.sem("p3_prt")
        s_selm = b.sem("p3_selm")
        s_E = b.sem("p3_E")
        s_ex = b.sem("p3_ex")
        s_sf = b.sem("p3_sf")
        s_ym = b.sem("p3_ym")
        s_yo = b.sem("p3_yo")
        s_ev = b.sem("p3_ev")
        s_su = b.sem("p3_su")
        s_sg = b.sem("p3_sg")
        s_r = b.sem("p3_r")
        s_gd = b.sem("p3_gd")
        s_yw = b.sem("p3_yw")
        s_sel2 = b.sem("p3_sel2")

        dt_fm, Bfm, Cfm = d["dt_fm"], d["Bfm"], d["Cfm"]
        xs_sc, gate_sc, y_sc = d["xs_sc"], d["gate_sc"], d["y_sc"]
        onesr_sb, onescol, epsln = d["onesr_sb"], d["onescol"], d["epsln"]

        PL = []
        tt = dict(ldx=0, ldg=0, pre=0, prs=0, prt=0, selm=0, E=0, ex=0, sf=0,
                  ym=0, yo=0, ev=0, su=0, sg=0, r=0, gd=0, yw=0, sel2=0)
        for c in range(NCH):
            main = c >= 8
            p = dict(tt)
            p.update(c=c, main=main)
            tt["ldx"] += 32
            if main:
                tt["ldg"] += 32
            tt["pre"] += 4
            tt["prs"] += 4
            tt["prt"] += 2 + (1 if main else 0)
            if main:
                tt["selm"] += 64
                tt["E"] += 128
                tt["ex"] += 128
                tt["sf"] += 128
                tt["yo"] += 128
                tt["ev"] += 128
                tt["sg"] += 1
                tt["r"] += 1
                tt["gd"] += 2
                tt["yw"] += 32
                tt["sel2"] += 32
            tt["ym"] += 64
            tt["su"] += 64
            for k, v_ in tt.items():
                p[k + "1"] = v_
            PL.append(p)

        W = lambda eng, sem, val: eng.wait_ge(sem, val) if val > 0 else None

        def _v_tail(v, p, hm, main):
            pm = hm % 2
            hsm = slice(hm * 64, (hm + 1) * 64)
            v.wait_ge(s_ym, p["ym"] + hm + 1)
            if main:
                v.wait_ge(s_yo, p["yo"] + 2 * hm + 2)
                for lt in range(2):
                    v.tensor_add(Y_sb[lt][:, hsm], hp_yd(pm, lt),
                                 ytmp[pm][lt][:, :]).then_inc(s_ev, 1)
            v.scalar_tensor_tensor(Ssb[:, hsm], Ssb[:, hsm], gbc[:, hm:hm + 1],
                                   hp_st(pm), ALU.mult, ALU.add).then_inc(s_su, 1)

        with nc.Block("p3") as blk:
            @blk.gpsimd
            def _(g):
                g.memset(Ssb[:, :], 0.0)
                g.memset(Sbf[:, :], 0.0).then_inc(s_p0, 1)
                g.dma_start(drep_sb[:, :],
                            bass.AP(d["d_rep"], 0, [[0, 128], [1, INTER]])).then_inc(s_p0g, 16)
                for p in PL:
                    if not p["main"]:
                        continue
                    mc = p["c"] - 8
                    g.wait_ge(s_gd, p["gd"] + 2)
                    for lt in range(2):
                        g.dma_start(
                            y_sc[mc * 256 + lt * 128:mc * 256 + (lt + 1) * 128, :],
                            Y_sb[lt][:, :]).then_inc(s_yw, 16)
                g.wait_ge(s_yw, PL[-1]["yw1"])

            @blk.sync
            def _(sync):
                for dst, src in (
                    (sel_sb[:, :], d["selbank"][:, 0:32 * 128]),
                    (identf_sb[:, :], d["identf"][:, :]),
                    (identb_sb[:, :], d["identb"][:, :]),
                    (mask01_sb[:, :], d["mask01"].rearrange("s p c -> p s c")),
                    (dtb_sb[:, :], d["dtbias"][:, :]),
                    (negA_sb[:, :], d["alog"][:, :]),
                    (maskst_sb[:, :], bass.AP(d["maskst"], 0, [[0, 128], [1, 1]])),
                ):
                    sync.dma_start(dst, src).then_inc(s_p0, 16)
                for p in PL:
                    c = p["c"]
                    if c >= 1:
                        sync.wait_ge(s_pre, PL[c - 1]["pre1"])
                    for s2 in range(2):
                        sync.dma_start_transpose(
                            xs_tm[s2][:, :],
                            xs_sc.rearrange("c p t -> (c p) t")[
                                :, c * 256 + s2 * 128:c * 256 + (s2 + 1) * 128],
                        ).then_inc(s_ldx, 16)
                    if c >= 1:
                        sync.wait_ge(s_ym, PL[c - 1]["ym1"])
                    for s2 in range(2):
                        sync.dma_start_transpose(
                            B_tm[s2][:, :],
                            Bfm[:, c * 256 + s2 * 128:c * 256 + (s2 + 1) * 128],
                        ).then_inc(s_ldb, 16)
                    if p["main"]:
                        mc = c - 8
                        if c >= 9 and PL[c - 1]["main"]:
                            sync.wait_ge(s_gd, PL[c - 1]["gd"] + 1)
                        for s2 in range(2):
                            sync.dma_start(
                                gate_tm[s2][:, :],
                                gate_sc[mc * 256 + s2 * 128:mc * 256 + (s2 + 1) * 128, :],
                            ).then_inc(s_ldg, 16)
                        sync.wait_ge(s_selm, p["selm"] + 32)
                        sync.dma_start(sel_sb[:, :],
                                       d["selbank"][:, 32 * 128:]).then_inc(s_sel2, 16)
                        sync.wait_ge(s_selm, p["selm"] + 64)
                        sync.dma_start(sel_sb[:, :],
                                       d["selbank"][:, 0:32 * 128]).then_inc(s_sel2, 16)
                sync.wait_ge(s_yw, PL[-1]["yw1"])
                sync.wait_ge(s_sel2, PL[-1]["sel21"])

            @blk.scalar
            def _(s):
                s.wait_ge(s_p0, 112 + 1)
                s.activation(negA_sb[:, :1], negA_sb[:, :1], AF.Exp).then_inc(s_p0, 1)
                for p in PL:
                    c, main = p["c"], p["main"]
                    prev = PL[c - 1] if c >= 1 else None
                    dsl = dt_fm[:, c * 256:(c + 1) * 256]
                    if prev is not None:
                        s.wait_ge(s_pre, prev["pre"] + 2)
                        s.wait_ge(s_prt, prev["prt"] + 1)
                    s.activation(dtp_fm[:, :], dsl, AF.Relu, bias=dtb_sb[:, :1])
                    s.activation(tmp_fm[:, :], dsl, AF.Abs, bias=dtb_sb[:, :1])
                    s.activation(tmp_fm[:, :], tmp_fm[:, :], AF.Exp, scale=-1.0)
                    s.activation(tmp_fm[:, :], tmp_fm[:, :], AF.Ln,
                                 bias=onescol[:64, :1]).then_inc(s_prs, 1)
                    s.wait_ge(s_pre, p["pre"] + 1)
                    s.activation(dcy_fm[:, :], cumA_fm[:, :], AF.Exp, scale=-1.0,
                                 bias=cumA_fm[:, 255:256])
                    s.activation(gcol[:, :], cumA_fm[:, 255:256],
                                 AF.Exp).then_inc(s_prs, 1)
                    s.wait_ge(s_prt, p["prt"] + 1)
                    s.copy(grow[:1, :], ps_t[:1, 448:512]).then_inc(s_prs, 1)
                    s.wait_ge(s_pre, p["pre"] + 3)
                    if main:
                        for s2 in range(2):
                            ec = s.activation(ecum_tm[s2][:, :], ncum_tm[s2][:, :],
                                              AF.Exp)
                        ec.then_inc(s_prs, 1)
                    else:
                        s.nop().then_inc(s_prs, 1)
                    if main:
                        for h in range(64):
                            par = h % 2
                            s.wait_ge(s_selm, p["selm"] + h + 1)
                            W(s, s_sf, p["sf"] + 2 * (h - 2) + 2)
                            for s2 in range(2):
                                s.activation(Et[par][s2][:, :], ps_bc[par][:, :],
                                             AF.Relu, scale=-1.0,
                                             bias=ncum_tm[s2][:, h:h + 1]).then_inc(s_E, 1)
                            for s2 in range(2):
                                s.activation(sex[par][s2][:, :], Et[par][s2][:, :],
                                             AF.Exp, scale=-1.0).then_inc(s_ex, 1)
                            if h >= 1:
                                hm = h - 1
                                s.wait_ge(s_ym, p["ym"] + hm + 1)
                                W(s, s_ev, p["ev"] + 2 * (hm - 2) + 2)
                                for lt in range(2):
                                    s.activation(ytmp[hm % 2][lt][:, :],
                                                 hp_yo(hm % 2, lt), AF.Copy,
                                                 scale=ecum_tm[lt][:, hm:hm + 1]).then_inc(s_yo, 1)
                        hm = 63
                        s.wait_ge(s_ym, p["ym"] + 64)
                        W(s, s_ev, p["ev"] + 2 * (hm - 2) + 2)
                        for lt in range(2):
                            s.activation(ytmp[hm % 2][lt][:, :], hp_yo(hm % 2, lt),
                                         AF.Copy,
                                         scale=ecum_tm[lt][:, hm:hm + 1]).then_inc(s_yo, 1)
                        s.wait_ge(s_ldg, p["ldg"] + 32)
                        for s2 in range(2):
                            sg = s.activation(gate_tm[s2][:, :], gate_tm[s2][:, :],
                                              AF.Silu)
                        sg.then_inc(s_sg, 1)
                        s.wait_ge(s_gd, p["gd"] + 1)
                        for lt in range(2):
                            b.drain(s, "s", s.activation(rc3[:, 4 + lt:5 + lt],
                                                         rc3[:, 2 + lt:3 + lt],
                                                         AF.Ln, bias=epsln[:, :1]))
                            ex = s.activation(rc3[:, 4 + lt:5 + lt],
                                              rc3[:, 4 + lt:5 + lt],
                                              AF.Exp, scale=-0.5)
                        ex.then_inc(s_r, 1)

            @blk.tensor
            def _(t):
                t.wait_ge(s_p0, 112 + 2)
                t.wait_ge(s_p0g, 16)
                for p in PL:
                    c, main = p["c"], p["main"]
                    prev = PL[c - 1] if c >= 1 else None
                    if prev is not None:
                        t.wait_ge(s_su, prev["su1"])
                    t.wait_ge(s_pre, p["pre"] + 2)
                    t.wait_ge(s_prs, p["prs"] + 2)
                    for s2 in range(2):
                        t.transpose(ps_t[:, s2 * 64:(s2 + 1) * 64],
                                    dtp_fm[:, s2 * 128:(s2 + 1) * 128], identf_sb[:64, :64])
                        t.transpose(ps_t[:, 128 + s2 * 64:128 + (s2 + 1) * 64],
                                    cumA_fm[:, s2 * 128:(s2 + 1) * 128], identf_sb[:64, :64])
                        t.transpose(ps_t[:, 256 + s2 * 64:256 + (s2 + 1) * 64],
                                    dpd_fm[:, s2 * 128:(s2 + 1) * 128], identf_sb[:64, :64])
                    t.transpose(ps_t[:1, 448:512], gcol[:, :1],
                                identf_sb[:64, :64]).then_inc(s_prt, 1)
                    t.wait_ge(s_prs, p["prs"] + 3)
                    t.matmul(ps_t[:, 384:448], onesr_sb[:1, :], grow[:1, :],
                             start=True, stop=True).then_inc(s_prt, 1)
                    if main:
                        t.wait_ge(s_pre, p["pre"] + 3)
                        for s2 in range(2):
                            mm = t.matmul(ps_sc[:, s2 * 256:(s2 + 1) * 256],
                                          Bfm[:, c * 256 + s2 * 128:c * 256 + (s2 + 1) * 128],
                                          Cfm[:, c * 256:(c + 1) * 256],
                                          start=True, stop=True)
                        mm.then_inc(s_prt, 1)
                    t.wait_ge(s_pre, p["pre1"])
                    t.wait_ge(s_ldb, 32 * (c + 1))

                    def sel_mm(h):
                        par = h % 2
                        W(t, s_E, p["E"] + 2 * (h - 2) + 2)
                        if h == 32:
                            t.wait_ge(s_sel2, p["sel2"] + 16)
                        if h == 0:
                            W(t, s_sel2, p["sel2"])
                        hsel = h % 32
                        t.matmul(ps_bc[par][:, :], sel_sb[:, hsel * 128:(hsel + 1) * 128],
                                 cumA_fm[:, :], start=True, stop=True).then_inc(s_selm, 1)

                    if main:
                        sel_mm(0)
                        sel_mm(1)
                    for h in range(64):
                        par = h % 2
                        hs = slice(h * 64, (h + 1) * 64)
                        if main and h + 2 < 64:
                            sel_mm(h + 2)
                        W(t, s_su, p["su"] + (h - 2) + 1)
                        t.matmul(hp_st(par), B_tm[0][:, :], xdd_tm[0][:, hs],
                                 start=True, stop=False)
                        mm = t.matmul(hp_st(par), B_tm[1][:, :], xdd_tm[1][:, hs],
                                      start=False, stop=True)
                        if main:
                            t.wait_ge(s_sf, p["sf"] + 2 * h + 2)
                            W(t, s_ev, p["ev"] + 2 * (h - 2) + 2)
                            t.matmul(hp_yd(par, 0), scf[par][0][:, 0:128],
                                     xdt_tm[0][:, hs], start=True, stop=False)
                            t.matmul(hp_yd(par, 0), identb_sb[:, :],
                                     xsD_tm[0][:, hs], start=False, stop=True)
                            t.matmul(hp_yd(par, 1), scf[par][0][:, 128:256],
                                     xdt_tm[0][:, hs], start=True, stop=False)
                            t.matmul(hp_yd(par, 1), scf[par][1][:, 128:256],
                                     xdt_tm[1][:, hs], start=False, stop=False)
                            t.matmul(hp_yd(par, 1), identb_sb[:, :],
                                     xsD_tm[1][:, hs], start=False, stop=True)
                            W(t, s_yo, p["yo"] + 2 * (h - 2) + 2)
                            t.matmul(hp_yo(par, 0), Cfm[:, c * 256:c * 256 + 128],
                                     Sbf[:, hs], start=True, stop=True)
                            mm = t.matmul(hp_yo(par, 1),
                                          Cfm[:, c * 256 + 128:c * 256 + 256],
                                          Sbf[:, hs], start=True, stop=True)
                        mm.then_inc(s_ym, 1)

            @blk.vector
            def _(v):
                v.wait_ge(s_p0, 112 + 2)
                v.wait_ge(s_p0g, 16)
                v.tensor_scalar_mul(negA_sb[:64, :1], negA_sb[:64, :1],
                                    -1.0).then_inc(s_p0, 1)
                v.wait_ge(s_p0, 112 + 3)
                for p in PL:
                    c, main = p["c"], p["main"]
                    prev = PL[c - 1] if c >= 1 else None
                    if prev is not None:
                        v.wait_ge(s_ym, prev["ym1"])
                        v.wait_ge(s_yo, prev["yo1"])
                    v.wait_ge(s_prs, p["prs"] + 1)
                    v.tensor_add(dtp_fm[:, :], dtp_fm[:, :], tmp_fm[:, :])
                    v.tensor_scalar_mul(adt_fm[:, :], dtp_fm[:, :], negA_sb[:64, :1])
                    v.tensor_tensor_scan(cumA_fm[:, :], adt_fm[:, :], adt_fm[:, :],
                                         0.0, ALU.add, ALU.bypass).then_inc(s_pre, 1)
                    v.wait_ge(s_prs, p["prs"] + 2)
                    v.tensor_mul(dpd_fm[:, :], dtp_fm[:, :],
                                 dcy_fm[:, :]).then_inc(s_pre, 1)
                    v.wait_ge(s_prt, p["prt"] + 2)
                    for s2 in range(2):
                        v.tensor_copy(dtp_tm[s2][:, :], ps_t[:, s2 * 64:(s2 + 1) * 64])
                        v.tensor_copy(ncum_tm[s2][:, :],
                                      ps_t[:, 128 + s2 * 64:128 + (s2 + 1) * 64])
                        v.tensor_copy(dpd_tm[s2][:, :],
                                      ps_t[:, 256 + s2 * 64:256 + (s2 + 1) * 64])
                    v.tensor_copy(gbc[:, :], ps_t[:, 384:448]).then_inc(s_pre, 1)
                    v.wait_ge(s_pre, p["pre"] + 3)
                    v.wait_ge(s_ldx, 32 * (c + 1))
                    for s2 in range(2):
                        if main:
                            v.tensor_mul(xsD_tm[s2][:, :], xs_tm[s2][:, :], drep_sb[:, :])
                        for h in range(64):
                            hs = slice(h * 64, (h + 1) * 64)
                            if main:
                                v.tensor_scalar_mul(xdt_tm[s2][:, hs], xs_tm[s2][:, hs],
                                                    dtp_tm[s2][:, h:h + 1])
                            v.tensor_scalar_mul(xdd_tm[s2][:, hs], xs_tm[s2][:, hs],
                                                dpd_tm[s2][:, h:h + 1])
                    if main:
                        v.wait_ge(s_prt, p["prt1"])
                        for s2 in range(2):
                            v.tensor_mul(scm[s2][:, :], ps_sc[:, s2 * 256:(s2 + 1) * 256],
                                         mask01_sb[:, s2 * 256:(s2 + 1) * 256])
                    if c == 0:
                        v.wait_ge(s_p0, 112 + 3)
                    v.tensor_copy(Sbf[:, :], Ssb[:, :]).then_inc(s_pre, 1)
                    if main and p["yw"] > 0:
                        v.wait_ge(s_yw, p["yw"])
                    for h in range(64):
                        par = h % 2
                        if main:
                            v.wait_ge(s_ex, p["ex"] + 2 * h + 2)
                            W(v, s_ym, p["ym"] + (h - 2) + 1)
                            for s2 in range(2):
                                v.tensor_mul(scf[par][s2][:, :], scm[s2][:, :],
                                             sex[par][s2][:, :]).then_inc(s_sf, 1)
                        if h >= 1:
                            _v_tail(v, p, h - 1, main)
                    _v_tail(v, p, 63, main)
                    if c == 7:
                        v.tensor_scalar_mul(Ssb[:, :], Ssb[:, :], maskst_sb[:, :1])
                    if main:
                        v.wait_ge(s_sg, p["sg"] + 1)
                        for lt in range(2):
                            v.tensor_mul(Y_sb[lt][:, :], Y_sb[lt][:, :], gate_tm[lt][:, :])
                        v.tensor_mul(ysq[:, :], Y_sb[0][:, :], Y_sb[0][:, :])
                        v.reduce_sum(rc3[:, 0:1], ysq[:, :], axis=AXX)
                        v.tensor_mul(ysq[:, :], Y_sb[1][:, :], Y_sb[1][:, :])
                        b.drain(v, "v", v.reduce_sum(rc3[:, 1:2], ysq[:, :], axis=AXX))
                        v.tensor_scalar_mul(rc3[:, 2:3], rc3[:, 0:1], 1.0 / INTER)
                        v.tensor_scalar_mul(rc3[:, 3:4], rc3[:, 1:2],
                                            1.0 / INTER).then_inc(s_gd, 1)
                        v.wait_ge(s_r, p["r"] + 1)
                        for lt in range(2):
                            ym = v.tensor_scalar_mul(Y_sb[lt][:, :], Y_sb[lt][:, :],
                                                     rc3[:, 4 + lt:5 + lt])
                        ym.then_inc(s_gd, 1)



def _emit_p456(nc, b, d):
    st = ExitStack()
    with st:
        arena = b.sb("arena", [128, 4 * FFI], BF16, st)
        stream = [b.sb(f"stm{i}", [128, H], F32, st) for i in range(4)]
        w4b = [b.sb(f"w4b{i}", [128, 4 * 512], BF16, st) for i in range(2)]
        xq_fm = b.sb("xq_fm", [128, 16 * 512], BF16, st)
        qt = b.sb("qt", [128, 4096], F32, st)
        qi = b.sb("qi", [128, 4096], I32, st)
        qb = b.sb("qb", [128, 4096], BF16, st)
        hlnw = b.sb("hlnw", [128, FFI], BF16, st)
        lnw_sb = b.sb("lnw_sb", [128, FFI], BF16, st)
        sgt = [b.sb(f"sgt{i}", [128, 512], F32, st) for i in range(2)]
        ost = [b.sb(f"ost{i}", [128, 512], F32, st) for i in range(2)]
        qc = b.sb("qc", [128, 16], F32, st)
        invsx = b.sb("invsx", [128, 4], F32, st)
        invsh = b.sb("invsh", [128, 4], F32, st)
        comb = b.sb("comb", [128, 12], F32, st)
        pga = [b.ps(f"pg{i}", [128, 512], st) for i in range(4)]
        pua = [b.ps(f"pu{i}", [128, 512], st) for i in range(4)]

        xwin, y_sc, xq_sc, hq_sc, y_out = (d[k] for k in
                                           ("xwin", "y_sc", "xq_sc", "hq_sc", "y_out"))
        wout_sc, wq_g, wq_u, wq_d = (d[k] for k in ("wout_sc", "wq_g", "wq_u", "wq_d"))
        invsw, epsln, epsrms, epsq = (d[k] for k in ("invsw", "epsln", "epsrms", "epsq"))

        s_l0 = b.sem("p456_l0")
        s_yl = b.sem("p4_yl")
        s_wl2 = [b.sem(f"p4_wl{i}") for i in range(2)]
        s_tc = b.sem("p4_tc")
        s_pe = b.sem("p4_pe")
        s_xl = b.sem("p4_xl")
        s_sr = b.sem("p4_sr")
        s_q4 = b.sem("p4_q4")
        s_qs = b.sem("p4_qs")
        s_xw = b.sem("p4_xw")
        s_x5 = b.sem("p5_xl")
        s_w52 = [b.sem(f"p5_wl{i}") for i in range(2)]
        s_t5 = b.sem("p5_tc")
        s_sg5 = b.sem("p5_sg")
        s_h5 = b.sem("p5_h")
        s_hq = b.sem("p5_hq")
        s_hqs = b.sem("p5_hqs")
        s_hw = b.sem("p5_hw")
        s_h6 = b.sem("p6_hl")
        s_w62 = [b.sem(f"p6_wl{i}") for i in range(2)]
        s_t6 = b.sem("p6_tc")
        s_e6 = b.sem("p6_ev")
        s_ow2 = [b.sem(f"p6_ow{i}") for i in range(2)]

        with nc.Block("p456_init") as blk:
            @blk.gpsimd
            def _(g):
                g.dma_start(lnw_sb[:, :],
                            bass.AP(d["lnww"], 0, [[0, 128], [1, FFI]])).then_inc(s_l0, 16)

            @blk.scalar
            def _(s):
                s.wait_ge(s_l0, 16)
        b.fence("p456i")

        for sp in range(NSUP):
            t0 = sp * 512

            with nc.Block(f"p4a_{sp}") as blk:
                @blk.sync
                def _(sync, sp=sp, t0=t0):
                    for yc in range(32):
                        sync.dma_start_transpose(
                            arena[:, yc * 512:(yc + 1) * 512],
                            y_sc[t0:t0 + 512, yc * 128:(yc + 1) * 128]).then_inc(s_yl, 16)
                    for tb in range(4):
                        if tb >= 1:
                            sync.wait_ge(s_sr, sp * 4 + tb)
                        sync.dma_start(
                            qt[:, :H],
                            xwin[MAIN + t0 + tb * 128:MAIN + t0 + (tb + 1) * 128, :],
                        ).then_inc(s_xl, 16)

                @blk.gpsimd
                def _(g, sp=sp):
                    for j in range(32):
                        if j >= 2:
                            g.wait_ge(s_tc, sp * 128 + 4 * (j - 1))
                        ht, ycq = divmod(j, 8)
                        g.dma_start(
                            w4b[j % 2][:, :].rearrange("p (k n) -> p k n", n=512),
                            wout_sc[ht, ycq * 4:(ycq + 1) * 4].rearrange(
                                "k p n -> p k n")).then_inc(s_wl2[j % 2], 16)

                @blk.tensor
                def _(t, sp=sp):
                    for ht in range(4):
                        if ht >= 1:
                            t.wait_ge(s_pe, sp * 16 + 4 * ht)
                        for ycq in range(8):
                            j = ht * 8 + ycq
                            if ht == 0:
                                t.wait_ge(s_yl, 16 * (sp * 32 + (ycq + 1) * 4))
                            t.wait_ge(s_wl2[j % 2], 16 * (sp * 16 + j // 2 + 1))
                            for tb in range(4):
                                for y4 in range(4):
                                    yc = ycq * 4 + y4
                                    mm = t.matmul(
                                        pga[tb][:, :],
                                        arena[:, yc * 512 + tb * 128:
                                              yc * 512 + (tb + 1) * 128],
                                        w4b[j % 2][:, y4 * 512:(y4 + 1) * 512],
                                        start=(yc == 0), stop=(yc == 31))
                                mm.then_inc(s_tc, 1)

                @blk.vector
                def _(v, sp=sp):
                    for ht in range(4):
                        v.wait_ge(s_tc, sp * 128 + 32 * (ht + 1))
                        for tb in range(4):
                            v.tensor_copy(stream[tb][:, ht * 512:(ht + 1) * 512],
                                          pga[tb][:, :]).then_inc(s_pe, 1)
                    for tb in range(4):
                        v.wait_ge(s_xl, 16 * (sp * 4 + tb + 1))
                        v.tensor_add(stream[tb][:, :], stream[tb][:, :],
                                     qt[:, :H]).then_inc(s_sr, 1)
            b.fence(f"p4a_{sp}")

            with nc.Block(f"p4b_{sp}") as blk:
                @blk.vector
                def _(v, sp=sp):
                    for tb in range(4):
                        j = sp * 4 + tb
                        b.drain(v, "v", v.reduce_sum(qc[:, 0:1], stream[tb][:, :],
                                                     axis=AXX))
                        v.wait_ge(s_qs, 3 * j + 1)
                        v.tensor_scalar_mul(qc[:, 2:3], qc[:, 0:1], 1.0 / H)
                        b.drain(v, "v", v.tensor_scalar_mul(qc[:, 3:4], qc[:, 1:2],
                                                            1.0 / H))
                        b.drain(v, "v", v.tensor_mul(qc[:, 4:5], qc[:, 2:3],
                                                     qc[:, 2:3]))
                        v.tensor_sub(qc[:, 4:5], qc[:, 3:4], qc[:, 4:5]).then_inc(s_q4, 1)
                        v.wait_ge(s_qs, 3 * j + 2)
                        b.drain(v, "v", v.reduce_max(qc[:, 7:8], qt[:, :H], axis=AXX,
                                                     apply_absolute_value=True))
                        b.drain(v, "v", v.tensor_scalar_max(qc[:, 7:8], qc[:, 7:8],
                                                            EPS_Q))
                        b.drain(v, "v", v.reciprocal(qc[:, 8:9], qc[:, 7:8]))
                        b.drain(v, "v", v.tensor_scalar_mul(qc[:, 9:10], qc[:, 8:9],
                                                            127.0))
                        v.tensor_scalar_mul(invsx[:, tb:tb + 1], qc[:, 7:8], 1.0 / 127.0)
                        v.tensor_scalar(qt[:, :H], qt[:, :H], qc[:, 9:10], MAGIC,
                                        ALU.mult, ALU.add)
                        v.tensor_scalar(qt[:, :H], qt[:, :H], -MAGIC, None, ALU.add)
                        v.tensor_copy(qi[:, :H], qt[:, :H]).then_inc(s_q4, 1)
                    for tb in range(4):
                        v.tensor_mul(comb[:, tb:tb + 1], invsx[:, tb:tb + 1], invsw[:, 0:1])
                        v.tensor_mul(comb[:, 4 + tb:5 + tb], invsx[:, tb:tb + 1],
                                     invsw[:, 1:2])

                @blk.scalar
                def _(s, sp=sp):
                    for tb in range(4):
                        j = sp * 4 + tb
                        s.activation(hlnw[:, :H], stream[tb][:, :], AF.Square,
                                     accum_out=qc[:, 1:2]).then_inc(s_qs, 1)
                        s.wait_ge(s_q4, 2 * j + 1)
                        b.drain(s, "s", s.activation(qc[:, 5:6], qc[:, 4:5],
                                                     AF.Ln, bias=epsln[:, :1]))
                        b.drain(s, "s", s.activation(qc[:, 5:6], qc[:, 5:6],
                                                     AF.Exp, scale=-0.5))
                        b.drain(s, "s", s.activation(qc[:, 6:7], qc[:, 2:3],
                                                     AF.Copy, scale=qc[:, 5:6]))
                        b.drain(s, "s", s.mul(qc[:, 6:7], qc[:, 6:7], -1.0))
                        s.activation(qt[:, :H], stream[tb][:, :], AF.Identity,
                                     scale=qc[:, 5:6], bias=qc[:, 6:7]).then_inc(s_qs, 1)
                        s.wait_ge(s_q4, 2 * j + 2)
                        if tb >= 1:
                            s.wait_ge(s_xw, 16 * (j - 1) + 16)
                        s.copy(qb[:, :H], qi[:, :H]).then_inc(s_qs, 1)

                @blk.sync
                def _(sync, sp=sp, t0=t0):
                    for tb in range(4):
                        j = sp * 4 + tb
                        sync.wait_ge(s_qs, 3 * j + 3)
                        sync.dma_start(xq_sc[t0 + tb * 128:t0 + (tb + 1) * 128, :],
                                       qb[:, :H]).then_inc(s_xw, 16)
                    sync.wait_ge(s_xw, 16 * 4 * (sp + 1))
            b.fence(f"p4b_{sp}")

            with nc.Block(f"p5a_{sp}") as blk:
                @blk.sync
                def _(sync, sp=sp, t0=t0):
                    for hc in range(16):
                        sync.dma_start_transpose(
                            xq_fm[:, hc * 512:(hc + 1) * 512],
                            xq_sc[t0:t0 + 512, hc * 128:(hc + 1) * 128]).then_inc(s_x5, 16)

                @blk.gpsimd
                def _(g, sp=sp):
                    for j in range(128):
                        if j >= 2:
                            g.wait_ge(s_t5, sp * 512 + 4 * (j - 1))
                        it, r = divmod(j, 8)
                        gu, q = divmod(r, 4)
                        g.dma_start(
                            w4b[j % 2][:, :].rearrange("p (k n) -> p k n", n=512),
                            (wq_g if gu == 0 else wq_u)[it, q * 4:(q + 1) * 4]
                            .rearrange("k p n -> p k n"),
                        ).then_inc(s_w52[j % 2], 16)

                @blk.tensor
                def _(t, sp=sp):
                    for it in range(16):
                        for gu in range(2):
                            pp = pga if gu == 0 else pua
                            if it >= 1:
                                if gu == 0:
                                    t.wait_ge(s_sg5, sp * 64 + 4 * it)
                                else:
                                    t.wait_ge(s_h5, sp * 64 + 4 * it)
                            for q in range(4):
                                j = it * 8 + gu * 4 + q
                                if it == 0 and gu == 0:
                                    t.wait_ge(s_x5, 16 * (sp * 16 + (q + 1) * 4))
                                t.wait_ge(s_w52[j % 2], 16 * (sp * 64 + j // 2 + 1))
                                for tb in range(4):
                                    for h4 in range(4):
                                        hc = q * 4 + h4
                                        mm = t.matmul(
                                            pp[tb][:, :],
                                            xq_fm[:, hc * 512 + tb * 128:
                                                  hc * 512 + (tb + 1) * 128],
                                            w4b[j % 2][:, h4 * 512:(h4 + 1) * 512],
                                            start=(hc == 0), stop=(hc == 15))
                                    mm.then_inc(s_t5, 1)

                @blk.scalar
                def _(s, sp=sp):
                    for it in range(16):
                        s.wait_ge(s_t5, sp * 512 + it * 32 + 16)
                        if it >= 1:
                            s.wait_ge(s_h5, sp * 64 + 4 * it)
                        for tb in range(4):
                            if tb >= 2:
                                s.wait_ge(s_h5, sp * 64 + 4 * it + tb - 1)
                            s.activation(sgt[tb % 2][:, :], pga[tb][:, :], AF.Silu,
                                         scale=comb[:, tb:tb + 1]).then_inc(s_sg5, 1)

                @blk.vector
                def _(v, sp=sp):
                    for it in range(16):
                        v.wait_ge(s_t5, sp * 512 + it * 32 + 32)
                        for tb in range(4):
                            v.wait_ge(s_sg5, sp * 64 + 4 * it + tb + 1)
                            v.scalar_tensor_tensor(
                                arena[:, tb * FFI + it * 512:tb * FFI + (it + 1) * 512],
                                pua[tb][:, :], comb[:, 4 + tb:5 + tb], sgt[tb % 2][:, :],
                                ALU.mult, ALU.mult).then_inc(s_h5, 1)
            b.fence(f"p5a_{sp}")

            with nc.Block(f"p5b_{sp}") as blk:
                @blk.scalar
                def _(s, sp=sp):
                    for tb in range(4):
                        j = sp * 4 + tb
                        b.drain(s, "s", s.activation(hlnw[:, :],
                                                     arena[:, tb * FFI:(tb + 1) * FFI],
                                                     AF.Square, accum_out=qc[:, 1:2]))
                        b.drain(s, "s", s.activation(qc[:, 13:14], qc[:, 1:2], AF.Ln,
                                                     scale=1.0 / FFI,
                                                     bias=epsrms[:, :1]))
                        s.activation(qc[:, 13:14], qc[:, 13:14], AF.Exp,
                                     scale=-0.5).then_inc(s_hqs, 1)
                        for hf in range(2):
                            s.wait_ge(s_hq, 2 * j + hf + 1)
                            if not (tb == 0 and hf == 0):
                                s.wait_ge(s_hw, 16 * (2 * j + hf - 1) + 16)
                            s.copy(qb[:, :4096], qi[:, :]).then_inc(s_hqs, 1)

                @blk.vector
                def _(v, sp=sp):
                    for tb in range(4):
                        j = sp * 4 + tb
                        v.wait_ge(s_hqs, 3 * j + 1)
                        v.tensor_mul(hlnw[:, :], arena[:, tb * FFI:(tb + 1) * FFI],
                                     lnw_sb[:, :])
                        b.drain(v, "v", v.reduce_max(qc[:, 10:11], hlnw[:, :],
                                                     axis=AXX,
                                                     apply_absolute_value=True))
                        b.drain(v, "v", v.tensor_mul(qc[:, 11:12], qc[:, 10:11],
                                                     qc[:, 13:14]))
                        b.drain(v, "v", v.tensor_scalar_max(qc[:, 11:12],
                                                            qc[:, 11:12], EPS_Q))
                        b.drain(v, "v", v.reciprocal(qc[:, 12:13], qc[:, 11:12]))
                        b.drain(v, "v", v.tensor_scalar_mul(qc[:, 12:13],
                                                            qc[:, 12:13], 127.0))
                        b.drain(v, "v", v.tensor_mul(qc[:, 12:13], qc[:, 12:13],
                                                     qc[:, 13:14]))
                        v.tensor_scalar_mul(invsh[:, tb:tb + 1], qc[:, 11:12], 1.0 / 127.0)
                        for hf in range(2):
                            if not (tb == 0 and hf == 0):
                                v.wait_ge(s_hqs, 3 * j + 1 + hf)
                            v.tensor_scalar(qt[:, :], hlnw[:, hf * 4096:(hf + 1) * 4096],
                                            qc[:, 12:13], MAGIC, ALU.mult, ALU.add)
                            v.tensor_scalar(qt[:, :], qt[:, :], -MAGIC, None, ALU.add)
                            v.tensor_copy(qi[:, :], qt[:, :]).then_inc(s_hq, 1)
                    for tb in range(4):
                        v.tensor_mul(comb[:, 8 + tb:9 + tb], invsh[:, tb:tb + 1],
                                     invsw[:, 2:3])

                @blk.sync
                def _(sync, sp=sp, t0=t0):
                    for tb in range(4):
                        j = sp * 4 + tb
                        for hf in range(2):
                            sync.wait_ge(s_hqs, 3 * j + 2 + hf)
                            sync.dma_start(
                                bass.AP(hq_sc, (t0 + tb * 128) * FFI + hf * 4096,
                                        [[FFI, 128], [1, 4096]]),
                                qb[:, :4096]).then_inc(s_hw, 16)
                    sync.wait_ge(s_hw, 16 * 8 * (sp + 1))
            b.fence(f"p5b_{sp}")

            with nc.Block(f"p6_{sp}") as blk:
                @blk.sync
                def _(sync, sp=sp, t0=t0):
                    for ic in range(64):
                        sync.dma_start_transpose(
                            arena[:, ic * 512:(ic + 1) * 512],
                            hq_sc[t0:t0 + 512, ic * 128:(ic + 1) * 128]).then_inc(s_h6, 16)
                    for k in range(16):
                        ht, tb = divmod(k, 4)
                        o = sp * 16 + k
                        sync.wait_ge(s_e6, sp * 16 + k + 1)
                        sync.dma_start(
                            bass.AP(y_out, (t0 + tb * 128) * H + ht * 512,
                                    [[H, 128], [1, 512]]),
                            ost[o % 2][:, :]).then_inc(s_ow2[o % 2], 16)

                @blk.gpsimd
                def _(g, sp=sp, t0=t0):
                    for j in range(64):
                        if j >= 2:
                            g.wait_ge(s_t6, sp * 256 + 4 * (j - 1))
                        ht, icq = divmod(j, 16)
                        g.dma_start(
                            w4b[j % 2][:, :].rearrange("p (k n) -> p k n", n=512),
                            wq_d[ht, icq * 4:(icq + 1) * 4].rearrange(
                                "k p n -> p k n")).then_inc(s_w62[j % 2], 16)

                @blk.tensor
                def _(t, sp=sp):
                    for ht in range(4):
                        if ht >= 1:
                            t.wait_ge(s_e6, sp * 16 + 4 * ht)
                        for icq in range(16):
                            j = ht * 16 + icq
                            if ht == 0:
                                t.wait_ge(s_h6, 16 * (sp * 64 + (icq + 1) * 4))
                            t.wait_ge(s_w62[j % 2], 16 * (sp * 32 + j // 2 + 1))
                            for tb in range(4):
                                for i4 in range(4):
                                    ic = icq * 4 + i4
                                    mm = t.matmul(
                                        pga[tb][:, :],
                                        arena[:, ic * 512 + tb * 128:
                                              ic * 512 + (tb + 1) * 128],
                                        w4b[j % 2][:, i4 * 512:(i4 + 1) * 512],
                                        start=(ic == 0), stop=(ic == 63))
                                mm.then_inc(s_t6, 1)

                @blk.vector
                def _(v, sp=sp):
                    for ht in range(4):
                        v.wait_ge(s_t6, sp * 256 + 64 * (ht + 1))
                        for tb in range(4):
                            k = ht * 4 + tb
                            o = sp * 16 + k
                            if o >= 2:
                                v.wait_ge(s_ow2[o % 2], 16 * (o // 2))
                            v.scalar_tensor_tensor(
                                ost[o % 2][:, :], pga[tb][:, :], comb[:, 8 + tb:9 + tb],
                                stream[tb][:, ht * 512:(ht + 1) * 512],
                                ALU.mult, ALU.add).then_inc(s_e6, 1)
            b.fence(f"p6_{sp}")

        with nc.Block("p456_end") as blk:
            @blk.sync
            def _(sync):
                sync.wait_ge(s_ow2[0], 16 * 8 * NSUP)
                sync.wait_ge(s_ow2[1], 16 * 8 * NSUP)


_NC_CACHE = {}


def _get_nc():
    if "nc" not in _NC_CACHE:
        _NC_CACHE["nc"] = build_nc()
    return _NC_CACHE["nc"]


def kernel(L0_attn_x, L0_attn_in_proj_w, L0_attn_conv_w, L0_attn_conv_b,
           L0_attn_dt_bias, L0_attn_A_log, L0_attn_D, L0_attn_norm_w,
           L0_attn_out_proj_w, L0_ffn_w_gate, L0_ffn_w_up, L0_ffn_w_down,
           L0_ffn_ln_weight):
    f32 = np.float32
    bf16 = ml_dtypes.bfloat16
    x = np.asarray(L0_attn_x, dtype=f32)
    W_in = np.asarray(L0_attn_in_proj_w, dtype=f32)

    Wc = W_in[:, INTER:INTER + 4352]
    w_conv = np.ascontiguousarray(
        Wc.reshape(16, 128, 34, 128).transpose(2, 0, 1, 3))
    w_dt = np.ascontiguousarray(W_in[:, 8448:8512].reshape(16, 128, 64))
    w_gate = np.ascontiguousarray(
        W_in[:, :INTER].reshape(16, 128, 8, 512).transpose(2, 0, 1, 3))
    w_out = np.ascontiguousarray(
        np.asarray(L0_attn_out_proj_w, dtype=f32).reshape(32, 128, 4, 512)
        .transpose(2, 0, 1, 3))
    w_gT = np.ascontiguousarray(
        np.asarray(L0_ffn_w_gate, dtype=f32).T.reshape(16, 128, 16, 512)
        .transpose(2, 0, 1, 3))
    w_uT = np.ascontiguousarray(
        np.asarray(L0_ffn_w_up, dtype=f32).T.reshape(16, 128, 16, 512)
        .transpose(2, 0, 1, 3))
    w_dT = np.ascontiguousarray(
        np.asarray(L0_ffn_w_down, dtype=f32).T.reshape(64, 128, 4, 512)
        .transpose(2, 0, 1, 3))
    convw = np.ascontiguousarray(np.asarray(L0_attn_conv_w, dtype=f32)
                                 .reshape(34, 128, KC))
    convb = np.ascontiguousarray(np.asarray(L0_attn_conv_b, dtype=f32)
                                 .reshape(34, 128).T)
    dtbias = np.ascontiguousarray(np.asarray(L0_attn_dt_bias, dtype=f32)
                                  .reshape(64, 1))
    alog = np.ascontiguousarray(np.asarray(L0_attn_A_log, dtype=f32).reshape(64, 1))
    d_rep = np.ascontiguousarray(
        np.repeat(np.asarray(L0_attn_D, dtype=f32), 64).reshape(1, INTER))
    normw = np.ascontiguousarray(np.asarray(L0_attn_norm_w, dtype=f32)
                                 .reshape(32, 128).T)
    lnww = np.ascontiguousarray(np.asarray(L0_ffn_ln_weight, dtype=f32)
                                .reshape(1, FFI))
    selb = np.zeros((64, 64 * 128), f32)
    for h in range(64):
        selb[h, h * 128:(h + 1) * 128] = 1.0
    identf_a = np.eye(128, dtype=f32)
    identb_a = np.eye(128, dtype=f32).astype(bf16)
    onesr_a = np.ones((1, 128), f32)
    m01 = np.zeros((2, 128, CHUNK), f32)
    for s2 in range(2):
        for p in range(128):
            m01[s2, p, s2 * 128 + p:] = 1.0
    m01 = m01.astype(bf16)

    shared = dict(w_conv=w_conv, w_dt=w_dt, w_gate=w_gate, w_out=w_out,
                  w_gT=w_gT, w_uT=w_uT, w_dT=w_dT, convw=convw, convb=convb,
                  dtbias=dtbias, alog=alog, d_rep=d_rep, normw=normw, lnww=lnww,
                  selbank=selb, identf=identf_a, identb=identb_a, onesr=onesr_a,
                  mask01=m01)

    in_maps = []
    for core in range(8):
        bb, hh = divmod(core, 2)
        xw = np.zeros((WIN, H), f32)
        xw[MAIN:] = x[bb, hh * MAIN:(hh + 1) * MAIN]
        if hh == 1:
            xw[:MAIN] = x[bb, :MAIN]
        m = dict(shared)
        m["xwin"] = xw
        m["maskst"] = np.array([[1.0 if hh == 1 else 0.0]], f32)
        in_maps.append(m)

    nc = _get_nc()
    res = run_bass_kernel_spmd(nc, in_maps, core_ids=list(range(8)))
    out = np.empty((4, 4096, H), np.float32)
    for core in range(8):
        bb, hh = divmod(core, 2)
        out[bb, hh * MAIN:(hh + 1) * MAIN] = res.results[core]["y_out"]
    return out
